# version 02
# baseline (speedup 1.0000x reference)
import numpy as np
import ml_dtypes
from contextlib import ExitStack

DBG_WLINT = False

import concourse.bass as bass
import concourse.mybir as mybir
from concourse.bass_utils import run_bass_kernel_spmd

F32 = mybir.dt.float32
BF16 = mybir.dt.bfloat16
I32 = mybir.dt.int32
AF = mybir.ActivationFunctionType
ALU = mybir.AluOpType
AX = mybir.AxisListType
BF = ml_dtypes.bfloat16

NCORE = 8
NT = 4096
D = 1024
H = 16
HD = 64
DECAY = -0.6065306597126334
GN_EPS = HD * 1e-05
LN_EPS = 1e-5

ENGINES = ("sync", "tensor", "vector", "scalar", "gpsimd")


class Rec:

    def __init__(self, nc):
        self.nc = nc
        self.ops = {e: [] for e in ENGINES}
        self.stack = ExitStack()
        self.scopes = [self.stack]
        self.sems = {}
        self.cnt = {}
        self.prefix = ""

    def sem(self, name):
        name = self.prefix + name
        if name not in self.sems:
            self.sems[name] = self.stack.enter_context(self.nc.semaphore(name))
            self.cnt[name] = 0
        return self.sems[name]

    def push_scope(self):
        s = ExitStack()
        self.scopes.append(s)
        return s

    def pop_scope(self):
        self.scopes.pop().close()

    def sb(self, name, shape, dtype):
        return self.scopes[-1].enter_context(
            self.nc.sbuf_tensor(self.prefix + name, list(shape), dtype))

    def ps(self, name, shape, dtype=F32):
        return self.scopes[-1].enter_context(
            self.nc.psum_tensor(self.prefix + name, list(shape), dtype))

    def barrier(self):
        waits = [(n, v) for n, v in self.cnt.items() if v > 0]
        for e in ENGINES:
            for n, v in waits:
                h = self.sems[n]
                self.ops[e].append(lambda eng, h=h, v=v: eng.wait_ge(h, v))
        save_prefix = self.prefix
        self.prefix = ""
        self.sem("barm")
        for e in ENGINES:
            self.ops[e].append(lambda eng: eng.drain())
            self.cnt["barm"] += 1
            h = self.sems["barm"]
            self.ops[e].append(lambda eng, h=h: eng.nop().then_inc(h, 1))
        v = self.cnt["barm"]
        h = self.sems["barm"]
        for e in ENGINES:
            self.ops[e].append(lambda eng, h=h, v=v: eng.wait_ge(h, v))
        self.prefix = save_prefix

    def op(self, eng_name, fn, inc=None, inc_amt=1, nodrain=False):
        if inc is not None:
            sem_h = self.sem(inc)
            self.cnt[self.prefix + inc] += inc_amt

            def play(eng, fn=fn, sem_h=sem_h, amt=inc_amt):
                fn(eng).then_inc(sem_h, amt)
            self.ops[eng_name].append(play)
        else:
            self.ops[eng_name].append(fn)
        if eng_name in ("vector", "scalar") and not nodrain:
            self.ops[eng_name].append(lambda eng: eng.drain())

    def wait(self, eng_name, sem_name, val=None):
        sem_h = self.sem(sem_name)
        v = self.cnt[self.prefix + sem_name] if val is None else val
        if v <= 0:
            return
        self.ops[eng_name].append(lambda eng, s=sem_h, v=v: eng.wait_ge(s, v))

    def dma(self, q, out, in_, inc=None, **kw):
        if inc is None:
            inc = "st_" + q
        self.op(q, lambda eng, o=out, i=in_, kw=kw: eng.dma_start(out=o, in_=i, **kw),
                inc=inc, inc_amt=16)

    def val(self, sem_name):
        self.sem(sem_name)
        return self.cnt[self.prefix + sem_name]

    def close(self):
        self.stack.close()

    def build_block(self):
        with self.nc.Block() as blk:
            for e in ENGINES:
                if not self.ops[e]:
                    continue

                def make(ename):
                    def body(eng):
                        for f in self.ops[ename]:
                            f(eng)
                    return body
                getattr(blk, e)(make(e))



def host_consts():
    c = {}
    c["ident"] = np.eye(128, dtype=BF)
    c["ut_incl"] = np.triu(np.ones((128, 128), np.float32)).astype(BF)
    c["ut_strict"] = np.triu(np.ones((128, 128), np.float32), 1).astype(BF)
    c["m_sl"] = np.tril(np.ones((128, 128), np.float32), -1)
    c["m_su"] = np.triu(np.ones((128, 128), np.float32), 1)
    c["m_tl"] = np.tril(np.ones((128, 128), np.float32))
    c["m_tu"] = np.triu(np.ones((128, 128), np.float32))
    bo = np.zeros((128, 2), np.float32)
    bo[0:64, 0] = 1.0
    bo[64:128, 1] = 1.0
    c["blockones"] = bo.astype(BF)
    c["blockpat2"] = bo.T.copy().astype(BF)
    return c




class Fctx:

    def __init__(self):
        self.nc = bass.Bass("TRN2", target_bir_lowering=False)
        self.r = Rec(self.nc)
        self.pA = self.r.ps("arena_pA", [128, D], F32).ap()
        self.pB = self.r.ps("arena_pB", [128, D], F32).ap()
        self.pC = self.r.ps("arena_pC", [128, D], F32).ap()
        self.pD = self.r.ps("arena_pD", [128, D], F32).ap()
        self.pC_bf = self.pC.bitcast(BF16)
        self.ext = {}
        self.scratch = {}

    def ext_in(self, name, shape, dt_):
        if name not in self.ext:
            self.ext[name] = self.nc.dram_tensor(name, list(shape), dt_,
                                                 kind="ExternalInput")
        return self.ext[name]

    def scr(self, name, shape, dt_):
        if name not in self.scratch:
            self.scratch[name] = self.nc.dram_tensor(name, list(shape), dt_,
                                                     kind="Internal")
        return self.scratch[name]

    def begin_phase(self, prefix):
        self.r.barrier()
        self.r.ops["gpsimd"].append(lambda eng: eng.drain())
        self.r.prefix = prefix
        self.r.push_scope()

    def end_phase(self):
        self.r.pop_scope()
        self.r.prefix = ""



def build_l0pre(F=None):
    if F is None:
        nc = bass.Bass("TRN2", target_bir_lowering=False)
        r = Rec(nc)
        din = lambda n, s, d: nc.dram_tensor(n, list(s), d, kind="ExternalInput")
        dout = lambda n, s, d: nc.dram_tensor(n, list(s), d, kind="ExternalOutput")
    else:
        nc, r = F.nc, F.r
        F.begin_phase("A_")
        din = F.ext_in
        dout = F.scr
    GS = 512
    NG = NT // GS

    xh = din("xh", [NT + 1, D], F32)
    dW = {n: din(n, [D, D], F32) for n in ("Wr", "Wk", "Wv")}
    dwA = din("wA", [D, 64], F32)
    dwB = din("wB", [64, D], F32)
    daA = din("aA", [D, 64], F32)
    daB = din("aB", [64, D], F32)
    dgA = din("gA", [D, 160], F32)
    dgB = din("gB", [160, D], F32)
    dwbias = din("w_bias", [1, D], F32)
    dabias = din("a_bias", [1, D], F32)
    dkk = din("k_k", [1, D], F32)
    dka = din("k_a", [1, D], F32)
    dmix = din("mixes", [6, D], F32)
    dident = din("identD", [128, 128], BF16)
    dblko = din("blockonesD", [128, 2], BF16)
    dblkp = din("blockpat2D", [2, 128], BF16)

    o_fm = {n: dout(n, [D, NT], BF16)
            for n in ("xn_fm", "r_fm", "kk_fm", "k2_fm", "a_fm")}
    o_tm = {n: dout(n, [NT, D], BF16)
            for n in ("r_tm", "kk_tm", "k2_tm", "v_tm", "a_tm", "wsig_tm", "g_tm")}

    sW = {n: r.sb("s" + n, [128, 8 * D], BF16) for n in ("Wr", "Wk", "Wv")}
    swA = r.sb("swA", [128, 8 * 64], BF16)
    saA = r.sb("saA", [128, 8 * 64], BF16)
    sgA = r.sb("sgA", [128, 8 * 160], BF16)
    swB = r.sb("swB", [64, D], BF16)
    saB = r.sb("saB", [64, D], BF16)
    sgB1 = r.sb("sgB1", [128, D], BF16)
    sgB2 = r.sb("sgB2", [32, D], BF16)
    swbias_r = r.sb("swbias_r", [1, D], BF16)
    sabias_r = r.sb("sabias_r", [1, D], BF16)
    sabias_fm = r.sb("sabias_fm", [128, 8], F32)
    skkv = r.sb("skkv", [128, 8], F32)
    skav = r.sb("skav", [128, 8], F32)
    sc0 = r.sb("sc0", [128, 8], F32)
    smix = r.sb("smix", [128, 48], F32)
    sident = r.sb("sident", [128, 128], BF16)
    sblko = r.sb("sblko", [128, 2], BF16)
    sblkp = r.sb("sblkp", [2, 128], BF16)
    sonesrow = r.sb("sonesrow", [1, 128], BF16)

    sx = [r.sb(f"sx{i}", [128, D], F32) for i in range(4)]
    sxhalo = r.sb("sxhalo", [1, D], F32)
    sxn_tm = [r.sb(f"sxn_tm{i}", [128, D], BF16) for i in range(4)]
    sxnh = r.sb("sxnh", [1, D], BF16)
    stat = r.sb("stat", [128, 8], F32)
    stat1 = r.sb("stat1", [1, 8], F32)
    ssq = r.sb("ssq", [128, D], BF16)
    sxn_fm = r.sb("sxn_fm", [128, 8 * (GS + 1)], BF16)
    sdelta = r.sb("sdelta", [128, 8 * GS], BF16)
    sxj = r.sb("sxj", [128, 8 * GS], BF16)
    sk32 = r.sb("sk32", [128, 8 * GS], F32)
    smid = r.sb("smid", [64, GS], BF16)
    smidg1 = r.sb("smidg1", [128, GS], BF16)
    smidg2 = r.sb("smidg2", [32, GS], BF16)
    sfm1 = r.sb("sfm1", [128, 8 * GS], BF16)
    sfm_a = r.sb("sfm_a", [128, 8 * GS], BF16)
    sfm_kkraw = r.sb("sfm_kkraw", [128, 8 * GS], BF16)
    sfm_kk = r.sb("sfm_kk", [128, 8 * GS], BF16)
    sfm_k2 = r.sb("sfm_k2", [128, 8 * GS], BF16)
    stm = r.sb("stm", [128, 4 * D], BF16)
    snorm = r.sb("snorm", [2, GS], F32)
    snorm2 = r.sb("snorm2", [2, GS], F32)
    snormbf = r.sb("snormbf", [2, GS], BF16)
    st4 = r.sb("st4", [128, GS], BF16)

    if F is None:
        p_proj = [r.ps(f"p_proj{i}", [128, GS], F32) for i in range(2)]
        p_tm = r.ps("p_tm", [128, D], F32)
        p_tr = r.ps("p_tr", [128, 128], BF16)
        p_small = r.ps("p_small", [64, GS], F32)
    else:
        p_proj = [F.pB[:, 0:512], F.pB[:, 512:1024]]
        p_tm = F.pA
        p_tr = F.pC_bf[:, 0:128]
        p_small = F.pD[0:64, 0:512]

    def wload(dst, src_t, rows, cols, row0=0, col0=0):
        src_ap = bass.AP(src_t, row0 * src_t.shape[1] + col0,
                         [[src_t.shape[1], rows], [1, cols]])
        r.dma("gpsimd", dst, src_ap, inc="wld")

    for d in range(8):
        for n in ("Wr", "Wk", "Wv"):
            wload(sW[n][:, d * D:(d + 1) * D], dW[n], 128, D, row0=d * 128)
        wload(swA[:, d * 64:(d + 1) * 64], dwA, 128, 64, row0=d * 128)
        wload(saA[:, d * 64:(d + 1) * 64], daA, 128, 64, row0=d * 128)
        wload(sgA[:, d * 160:(d + 1) * 160], dgA, 128, 160, row0=d * 128)
    wload(swB[:], dwB, 64, D)
    wload(saB[:], daB, 64, D)
    wload(sgB1[:], dgB, 128, D)
    wload(sgB2[:], dgB, 32, D, row0=128)
    r.dma("gpsimd", swbias_r[:], dwbias.ap(), inc="wld")
    r.dma("gpsimd", sabias_r[:], dabias.ap(), inc="wld")
    for dst, src in ((sabias_fm, dabias), (skkv, dkk), (skav, dka)):
        r.dma("gpsimd", dst[:], bass.AP(src, 0, [[1, 128], [128, 8]]), inc="wld",
              allow_slow_non_contiguous=True)
    for j in range(6):
        r.dma("gpsimd", smix[:, j * 8:(j + 1) * 8],
              bass.AP(dmix, j * D, [[1, 128], [128, 8]]), inc="wld",
              allow_slow_non_contiguous=True)
    r.dma("sync", sident[:], dident.ap(), inc="wld")
    r.dma("sync", sblko[:], dblko.ap(), inc="wld")
    r.dma("sync", sblkp[:], dblkp.ap(), inc="wld")
    r.op("gpsimd", lambda eng: eng.memset(sonesrow[:], 1.0), inc="wldg")
    WLD = r.val("wld")
    r.wait("vector", "wld", WLD)
    r.wait("vector", "wldg", r.val("wldg"))
    r.op("vector", lambda eng: eng.tensor_scalar(
        out=sc0[:], in0=skav[:], scalar1=-1.0, scalar2=1.0,
        op0=ALU.mult, op1=ALU.add), inc="ve")
    r.wait("tensor", "wld", WLD)
    r.wait("scalar", "wld", WLD)

    def ln_rows(x_ap, xn_ap, nrow, st, cols=D):
        r.op("scalar", lambda eng: eng.activation(
            out=ssq[0:nrow, 0:cols], in_=x_ap, func=AF.Square,
            accum_out=st[0:nrow, 1:2]), inc="ac")
        r.op("scalar", lambda eng: eng.activation(
            out=ssq[0:nrow, 0:cols], in_=x_ap, func=AF.Identity,
            accum_out=st[0:nrow, 0:1]), inc="ac")
        r.wait("vector", "ac")
        r.op("vector", lambda eng: eng.tensor_scalar(
            out=st[0:nrow, 2:3], in0=st[0:nrow, 0:1], scalar1=1.0 / cols,
            scalar2=None, op0=ALU.mult))
        r.op("vector", lambda eng: eng.tensor_scalar(
            out=st[0:nrow, 3:4], in0=st[0:nrow, 1:2], scalar1=1.0 / cols,
            scalar2=LN_EPS, op0=ALU.mult, op1=ALU.add))
        r.op("vector", lambda eng: eng.tensor_scalar(
            out=st[0:nrow, 5:6], in0=st[0:nrow, 2:3],
            scalar1=st[0:nrow, 2:3], scalar2=None, op0=ALU.mult))
        r.op("vector", lambda eng: eng.tensor_tensor(
            out=st[0:nrow, 4:5], in0=st[0:nrow, 3:4], in1=st[0:nrow, 5:6],
            op=ALU.subtract), inc="ve")
        r.wait("scalar", "ve")
        r.op("scalar", lambda eng: eng.activation(
            out=st[0:nrow, 6:7], in_=st[0:nrow, 4:5], func=AF.Sqrt), inc="ac")
        r.wait("vector", "ac")
        r.op("vector", lambda eng: eng.reciprocal(
            out=st[0:nrow, 7:8], in_=st[0:nrow, 6:7]))
        r.op("vector", lambda eng: eng.tensor_scalar(
            out=xn_ap, in0=x_ap, scalar1=st[0:nrow, 2:3],
            scalar2=st[0:nrow, 7:8], op0=ALU.subtract, op1=ALU.mult), inc="ve")

    def transpose_pe(src_ap, dst_ap):
        p, f = src_ap.shape
        r.wait("tensor", "ve")
        r.op("tensor", lambda eng: eng.transpose(
            p_tr[0:f, 0:p], src_ap, sident[0:p, 0:p]), inc="pe")
        r.wait("vector", "pe")
        r.op("vector", lambda eng: eng.tensor_copy(
            out=dst_ap, in_=p_tr[0:f, 0:p]), inc="ve")

    def mk_xj(j):
        r.wait("vector", "ve")
        r.wait("vector", "pe")
        for c in range(8):
            r.op("vector", lambda eng, c=c, j=j: eng.scalar_tensor_tensor(
                out=sxj[:, c * GS:(c + 1) * GS],
                in0=sdelta[:, c * GS:(c + 1) * GS],
                scalar=smix[:, j * 8 + c: j * 8 + c + 1],
                in1=sxn_fm[:, c * (GS + 1) + 1: c * (GS + 1) + 1 + GS],
                op0=ALU.mult, op1=ALU.add), inc="ve")
        r.wait("tensor", "ve")

    def proj_fm(w_sb, evac):
        for dp in range(8):
            pp = p_proj[dp % 2]
            r.wait("tensor", "pev")
            for d in range(8):
                r.op("tensor", lambda eng, d=d, dp=dp, pp=pp: eng.matmul(
                    pp[:], w_sb[:, d * D + dp * 128: d * D + dp * 128 + 128],
                    sxj[:, d * GS:(d + 1) * GS],
                    start=(d == 0), stop=(d == 7)), inc=("pe" if d == 7 else None))
            evac(dp, pp)

    def evac_to(buf, engine="vector", func=None, bias=None):
        def evac(dp, pp):
            if engine == "vector":
                r.wait("vector", "pe")
                r.op("vector", lambda eng, dp=dp, pp=pp: eng.tensor_copy(
                    out=buf[:, dp * GS:(dp + 1) * GS], in_=pp[:]), inc="pev")
            else:
                r.wait("scalar", "pe")
                kw = {}
                if bias is not None:
                    kw["bias"] = bias[:, dp:dp + 1]
                r.op("scalar", lambda eng, dp=dp, pp=pp, kw=kw: eng.activation(
                    out=buf[:, dp * GS:(dp + 1) * GS], in_=pp[:], func=func, **kw),
                    inc="pev")
        return evac

    def tm_out_and_store(lhs_tiles, rhs_list, dram_t, t0, act=None, bias_row=None,
                         extra_lhs=None, extra_rhs=None):
        r.wait("sync", "st_sync", r.val("st_sync"))
        r.wait("scalar", "st_sync", r.val("st_sync"))
        for i in range(4):
            for hf in range(2):
                sl = slice(hf * 512, (hf + 1) * 512)
                r.op("tensor", lambda eng, i=i, sl=sl, lt=lhs_tiles: eng.matmul(
                    p_tm[:, sl], lt(i), rhs_list[0][:, sl], start=True,
                    stop=(extra_lhs is None and bias_row is None)),
                    inc=("pe" if (extra_lhs is None and bias_row is None) else None))
                if extra_lhs is not None:
                    r.op("tensor", lambda eng, i=i, sl=sl: eng.matmul(
                        p_tm[:, sl], extra_lhs(i), extra_rhs[:, sl], start=False,
                        stop=(bias_row is None)),
                        inc=("pe" if bias_row is None else None))
                if bias_row is not None:
                    r.op("tensor", lambda eng, sl=sl: eng.matmul(
                        p_tm[:, sl], sonesrow[0:1, 0:128], bias_row[0:1, sl],
                        start=False, stop=True), inc="pe")
            r.wait("scalar", "pe")
            fn = act if act is not None else AF.Identity
            r.op("scalar", lambda eng, i=i, fn=fn: eng.activation(
                out=stm[:, i * D:(i + 1) * D], in_=p_tm[:], func=fn), inc="ac")
            r.wait("tensor", "ac")
        r.wait("sync", "ac")
        dst = bass.AP(dram_t, t0 * D, [[D, 128], [128 * D, 4], [1, D]])
        src = stm[:].rearrange("p (i j) -> p i j", i=4)
        r.dma("sync", dst, src, inc="st_sync")

    def fm_store(buf, dram_t, t0):
        dst = bass.AP(dram_t, t0, [[NT, 128], [128 * NT, 8], [1, GS]])
        src = buf[:].rearrange("p (c j) -> p c j", c=8)
        r.dma("sync", dst, src, inc="st_sync")

    def transpose_chunks_to_tm(fm_buf, dram_t, t0):
        r.wait("sync", "st_sync", r.val("st_sync"))
        r.wait("vector", "st_sync", r.val("st_sync"))
        for i in range(4):
            for c in range(8):
                transpose_pe(fm_buf[:, c * GS + i * 128: c * GS + (i + 1) * 128],
                             stm[:, i * D + c * 128: i * D + (c + 1) * 128])
        r.wait("sync", "ve")
        dst = bass.AP(dram_t, t0 * D, [[D, 128], [128 * D, 4], [1, D]])
        src = stm[:].rearrange("p (i j) -> p i j", i=4)
        r.dma("sync", dst, src, inc="st_sync")

    for g in range(NG):
        t0 = g * GS
        if g > 0:
            r.wait("sync", "ve")
            r.wait("sync", "ac")
            r.wait("sync", "pe")
            r.wait("sync", "st_sync", r.val("st_sync"))
        for i in range(4):
            r.dma("sync", sx[i][:], xh.ap()[1 + t0 + i * 128: 1 + t0 + (i + 1) * 128, :],
                  inc="xl")
        r.dma("sync", sxhalo[:], xh.ap()[t0:t0 + 1, :], inc="xl")
        XL = r.val("xl")
        r.wait("scalar", "xl", XL)
        r.wait("vector", "xl", XL)

        for i in range(4):
            ln_rows(sx[i][:], sxn_tm[i][:], 128, stat)
        ln_rows(sxhalo[:], sxnh[:], 1, stat1)

        for i in range(4):
            for c in range(8):
                transpose_pe(sxn_tm[i][:, c * 128:(c + 1) * 128],
                             sxn_fm[:, c * (GS + 1) + 1 + i * 128: c * (GS + 1) + 1 + (i + 1) * 128])
        for c in range(8):
            transpose_pe(sxnh[0:1, c * 128:(c + 1) * 128],
                         sxn_fm[:, c * (GS + 1): c * (GS + 1) + 1])
        r.wait("sync", "ve")
        dst = bass.AP(o_fm["xn_fm"], t0, [[NT, 128], [128 * NT, 8], [1, GS]])
        src = sxn_fm[:].rearrange("p (c j) -> p c j", c=8)[:, :, 1:GS + 1]
        r.dma("sync", dst, src, inc="st_sync")

        r.wait("vector", "ve")
        for c in range(8):
            r.op("vector", lambda eng, c=c: eng.tensor_tensor(
                out=sdelta[:, c * GS:(c + 1) * GS],
                in0=sxn_fm[:, c * (GS + 1): c * (GS + 1) + GS],
                in1=sxn_fm[:, c * (GS + 1) + 1: c * (GS + 1) + 1 + GS],
                op=ALU.subtract), inc="ve")

        mk_xj(1)
        for d in range(8):
            r.op("tensor", lambda eng, d=d: eng.matmul(
                p_small[:], swA[:, d * 64:(d + 1) * 64], sxj[:, d * GS:(d + 1) * GS],
                start=(d == 0), stop=(d == 7)), inc=("pe" if d == 7 else None))
        r.wait("scalar", "pe")
        r.op("scalar", lambda eng: eng.activation(
            out=smid[:], in_=p_small[:], func=AF.Tanh), inc="ac")
        r.wait("tensor", "ac")
        tm_out_and_store(lambda i: smid[:, i * 128:(i + 1) * 128], [swB],
                         o_tm["wsig_tm"], t0, act=AF.Sigmoid, bias_row=swbias_r)

        mk_xj(4)
        for d in range(8):
            r.op("tensor", lambda eng, d=d: eng.matmul(
                p_small[:], saA[:, d * 64:(d + 1) * 64], sxj[:, d * GS:(d + 1) * GS],
                start=(d == 0), stop=(d == 7)), inc=("pe" if d == 7 else None))
        r.wait("vector", "pe")
        r.op("vector", lambda eng: eng.tensor_copy(out=smid[:], in_=p_small[:]), inc="ve")
        r.wait("tensor", "ve")
        tm_out_and_store(lambda i: smid[:, i * 128:(i + 1) * 128], [saB],
                         o_tm["a_tm"], t0, act=AF.Sigmoid, bias_row=sabias_r)
        for dp in range(8):
            pp = p_proj[dp % 2]
            r.wait("tensor", "pev")
            r.op("tensor", lambda eng, dp=dp, pp=pp: eng.matmul(
                pp[:], saB[:, dp * 128:(dp + 1) * 128], smid[:],
                start=True, stop=True), inc="pe")
            r.wait("scalar", "pe")
            r.op("scalar", lambda eng, dp=dp, pp=pp: eng.activation(
                out=sfm_a[:, dp * GS:(dp + 1) * GS], in_=pp[:], func=AF.Sigmoid,
                bias=sabias_fm[:, dp:dp + 1]), inc="pev")
        r.wait("sync", "pev")
        fm_store(sfm_a, o_fm["a_fm"], t0)

        mk_xj(2)
        proj_fm(sW["Wk"], evac_to(sk32))
        r.wait("vector", "pev")
        r.wait("scalar", "pev")
        for c in range(8):
            r.op("vector", lambda eng, c=c: eng.tensor_scalar(
                out=sfm_kkraw[:, c * GS:(c + 1) * GS], in0=sk32[:, c * GS:(c + 1) * GS],
                scalar1=skkv[:, c:c + 1], scalar2=None, op0=ALU.mult), inc="ve")
            r.wait("scalar", "ve")
            r.wait("scalar", "pe")
            r.op("scalar", lambda eng, c=c: eng.activation(
                out=ssq[:, 0:GS], in_=sfm_kkraw[:, c * GS:(c + 1) * GS],
                func=AF.Square), inc="ac")
            r.wait("tensor", "ac")
            r.op("tensor", lambda eng: eng.matmul(
                p_small[0:2, :], sblko[:], ssq[:, 0:GS], start=True, stop=True), inc="pe")
            r.wait("scalar", "pe")
            r.op("scalar", lambda eng: eng.activation(
                out=snorm[0:2, :], in_=p_small[0:2, :], func=AF.Sqrt), inc="ac")
            r.wait("vector", "ac")
            r.op("vector", lambda eng: eng.tensor_scalar(
                out=snorm[0:2, :], in0=snorm[0:2, :], scalar1=1e-12, scalar2=None,
                op0=ALU.max))
            r.op("vector", lambda eng: eng.reciprocal(
                out=snorm2[0:2, :], in_=snorm[0:2, :]))
            r.op("vector", lambda eng: eng.tensor_copy(
                out=snormbf[0:2, :], in_=snorm2[0:2, :]), inc="ve")
            r.wait("tensor", "ve")
            r.wait("tensor", "pev")
            r.op("tensor", lambda eng: eng.matmul(
                p_proj[0][:], sblkp[:], snormbf[0:2, :], start=True, stop=True), inc="pe")
            r.wait("vector", "pe")
            r.op("vector", lambda eng, c=c: eng.tensor_tensor(
                out=sfm_kk[:, c * GS:(c + 1) * GS],
                in0=sfm_kkraw[:, c * GS:(c + 1) * GS],
                in1=p_proj[0][:], op=ALU.mult), inc="pev")
            r.op("vector", lambda eng, c=c: eng.tensor_scalar(
                out=st4[:], in0=sfm_a[:, c * GS:(c + 1) * GS],
                scalar1=skav[:, c:c + 1], scalar2=sc0[:, c:c + 1],
                op0=ALU.mult, op1=ALU.add))
            r.op("vector", lambda eng, c=c: eng.tensor_tensor(
                out=sfm_k2[:, c * GS:(c + 1) * GS], in0=sk32[:, c * GS:(c + 1) * GS],
                in1=st4[:], op=ALU.mult), inc="ve")
        r.wait("sync", "ve")
        r.wait("sync", "pev")
        fm_store(sfm_kk, o_fm["kk_fm"], t0)
        fm_store(sfm_k2, o_fm["k2_fm"], t0)
        transpose_chunks_to_tm(sfm_kk, o_tm["kk_tm"], t0)
        transpose_chunks_to_tm(sfm_k2, o_tm["k2_tm"], t0)

        mk_xj(3)
        proj_fm(sW["Wv"], evac_to(sfm1))
        r.wait("vector", "pev")
        transpose_chunks_to_tm(sfm1, o_tm["v_tm"], t0)

        mk_xj(0)
        proj_fm(sW["Wr"], evac_to(sfm1))
        r.wait("sync", "pev")
        fm_store(sfm1, o_fm["r_fm"], t0)
        r.wait("vector", "pev")
        transpose_chunks_to_tm(sfm1, o_tm["r_tm"], t0)

        mk_xj(5)
        r.wait("tensor", "pev")
        for d in range(8):
            r.op("tensor", lambda eng, d=d: eng.matmul(
                p_proj[0][:], sgA[:, d * 160: d * 160 + 128],
                sxj[:, d * GS:(d + 1) * GS],
                start=(d == 0), stop=(d == 7)), inc=("pe" if d == 7 else None))
        for d in range(8):
            r.op("tensor", lambda eng, d=d: eng.matmul(
                p_small[0:32, :], sgA[:, d * 160 + 128: d * 160 + 160],
                sxj[:, d * GS:(d + 1) * GS],
                start=(d == 0), stop=(d == 7)), inc=("pe" if d == 7 else None))
        r.wait("scalar", "pe")
        r.op("scalar", lambda eng: eng.activation(
            out=smidg1[:], in_=p_proj[0][:], func=AF.Sigmoid), inc="ac")
        r.op("scalar", lambda eng: eng.activation(
            out=smidg2[:], in_=p_small[0:32, :], func=AF.Sigmoid), inc="ac")
        r.wait("tensor", "ac")
        tm_out_and_store(lambda i: smidg1[:, i * 128:(i + 1) * 128], [sgB1],
                         o_tm["g_tm"], t0, act=None,
                         extra_lhs=lambda i: smidg2[:, i * 128:(i + 1) * 128],
                         extra_rhs=sgB2)

    r.wait("sync", "st_sync", r.val("st_sync"))
    if F is None:
        r.build_block()
        r.close()
        return nc
    F.end_phase()



def build_scan(stage=6, debug=False, T=8192):
    nc = bass.Bass("TRN2", target_bir_lowering=False)
    r = Rec(nc)
    C = 128
    NCH = T // C
    NS = 8
    PCH = min(4, NCH)

    d_fm = {n: nc.dram_tensor(n, [NS, 64, T], BF16, kind="ExternalInput")
            for n in ("rF", "kkF", "aF", "k2F")}
    d_tm = {n: nc.dram_tensor(n, [NS, T, 64], BF16, kind="ExternalInput")
            for n in ("wT", "kkT", "aT", "k2T", "vT")}
    dident = nc.dram_tensor("identD", [128, 128], BF16, kind="ExternalInput")
    dut_i = nc.dram_tensor("utinclD", [128, 128], BF16, kind="ExternalInput")
    dut_s = nc.dram_tensor("utstrictD", [128, 128], BF16, kind="ExternalInput")
    dmsl = nc.dram_tensor("mslD", [128, 128], F32, kind="ExternalInput")
    dmsu = nc.dram_tensor("msuD", [128, 128], F32, kind="ExternalInput")
    dmtu = nc.dram_tensor("mtuD", [128, 128], F32, kind="ExternalInput")
    d_o = nc.dram_tensor("o", [NS, T, 64], F32, kind="ExternalOutput")
    if debug:
        dbg_names = ["dM", "dMT", "dNT", "dTLRBt", "dTLRKt", "dWT",
                     "dAtf", "dBtf", "dKtf", "dRtf", "dLam", "dLamI"]
        d_dbg = {n: nc.dram_tensor(n, [128, 129 if n == "dLam" else 128], F32,
                                   kind="ExternalOutput") for n in dbg_names}
        d_dbg2 = {n: nc.dram_tensor(n, [128, 64], F32, kind="ExternalOutput")
                  for n in ["dNV", "dUp", "dWA"]}
        d_dbg3 = {n: nc.dram_tensor(n, [64, 128], F32, kind="ExternalOutput")
                  for n in ["dRhat"]}
        d_dbg4 = {n: nc.dram_tensor(n, [64, 64], F32, kind="ExternalOutput")
                  for n in ["dIpH", "dS"]}
        d_dbg5 = nc.dram_tensor("dwin", [128, 512], BF16, kind="ExternalOutput")
        d_dbg8 = nc.dram_tensor("dut", [128, 128], BF16, kind="ExternalOutput")
        d_dbg9 = nc.dram_tensor("duts", [128, 128], BF16, kind="ExternalOutput")
        d_dbg6 = nc.dram_tensor("dlogL", [128, 128], F32, kind="ExternalOutput")
        d_dbg7 = nc.dram_tensor("dvin", [128, 512], BF16, kind="ExternalOutput")

    sident = r.sb("sident", [128, 128], BF16)
    sut_i = r.sb("sut_i", [128, 128], BF16)
    sut_s = r.sb("sut_s", [128, 128], BF16)
    smsl = r.sb("smsl", [128, 128], F32)
    smsu = r.sb("smsu", [128, 128], F32)
    smtu = r.sb("smtu", [128, 128], F32)

    in_tm = {n: [r.sb(f"i_{n}{p}", [128, PCH * 512], BF16) for p in range(2)]
             for n in ("wT", "kkT", "aT", "k2T", "vT")}
    in_fm = {n: [r.sb(f"i_{n}{p}", [128, PCH * 512], BF16) for p in range(2)]
             for n in ("rF", "kkF", "aF", "k2F")}

    slogL = r.sb("slogL", [128, 512], F32)
    slogLp = r.sb("slogLp", [128, 512], F32)
    sLamI_tm = r.sb("sLamI_tm", [128, 512], BF16)
    sLamP_tm = r.sb("sLamP_tm", [128, 512], BF16)
    sAt_tm = r.sb("sAt_tm", [128, 512], BF16)
    sBt_tm = r.sb("sBt_tm", [128, 512], BF16)
    sKt_tm = r.sb("sKt_tm", [128, 512], BF16)
    slogL_fm = [r.sb(f"slogL_fm{j}", [128, 128], F32) for j in range(4)]
    sLam_fm = [r.sb(f"sLam_fm{j}", [128, 129], BF16) for j in range(4)]
    sLamI_fm = [r.sb(f"sLamI_fm{j}", [128, 128], BF16) for j in range(4)]
    sLamC = [r.sb(f"sLamC{j}", [128, 1], F32) for j in range(4)]
    sAt_fm = [r.sb(f"sAt_fm{j}", [128, 128], BF16) for j in range(4)]
    sBt_fm = [r.sb(f"sBt_fm{j}", [128, 128], BF16) for j in range(4)]
    sKt_fm = [r.sb(f"sKt_fm{j}", [128, 128], BF16) for j in range(4)]
    sRt_fm = [r.sb(f"sRt_fm{j}", [128, 128], BF16) for j in range(4)]

    sM = r.sb("sM", [128, 128], BF16)
    sMT = r.sb("sMT", [128, 128], BF16)
    sNT = r.sb("sNT", [128, 128], BF16)
    sTLRBt = r.sb("sTLRBt", [128, 128], BF16)
    sTLRKt = r.sb("sTLRKt", [128, 128], BF16)
    sMk = [r.sb(f"sMk{i}", [128, 128], BF16) for i in range(2)]
    sMkT = [r.sb(f"sMkT{i}", [128, 128], BF16) for i in range(2)]
    sWT = [r.sb(f"sWT{i}", [128, 128], BF16) for i in range(2)]
    sNV = r.sb("sNV", [128, 64], BF16)
    sUp = r.sb("sUp", [128, 64], BF16)
    sWAs = r.sb("sWAs", [128, 64], BF16)
    sRhat = r.sb("sRhat", [64, 128], BF16)
    sIpH = r.sb("sIpH", [64, 64], BF16)
    sS = [r.sb(f"sS{s}", [64, 64], BF16) for s in range(NS)]
    so = [r.sb(f"so{p}", [128, 512], F32) for p in range(2)]

    p_cum = r.ps("p_cum", [128, 512], F32).ap()
    p_a = r.ps("p_a", [128, 512], F32).ap()[:, 0:128]
    p_b = r.ps("p_b", [128, 512], F32).ap()[:, 0:128]
    p_sm = r.ps("p_sm", [128, 512], F32).ap()[:, 0:64]
    p_rh = r.ps("p_rh", [64, 512], F32).ap()[0:64, 0:128]
    p_h = r.ps("p_h", [64, 512], F32).ap()[0:64, 0:64]
    p_o = r.ps("p_o", [128, 512], F32).ap()[:, 0:64]
    p_st = r.ps("p_st", [64, 512], F32).ap()[0:64, 0:64]

    for dst, src in ((sident, dident), (sut_i, dut_i), (sut_s, dut_s),
                     (smsl, dmsl), (smsu, dmsu), (smtu, dmtu)):
        r.dma("sync", dst[:], src.ap(), inc="cld")
    for s in range(NS):
        r.op("gpsimd", lambda eng, s=s: eng.memset(sS[s][:], 0.0), inc="gp")
    for j in range(4):
        r.op("gpsimd", lambda eng, j=j: eng.memset(sLam_fm[j][:, 0:1], 1.0), inc="gp")
    CLD = r.val("cld")
    GP0 = r.val("gp")
    for e in ("tensor", "vector", "scalar"):
        r.wait(e, "cld", CLD)
        r.wait(e, "gp", GP0)

    def load_group(g, buf):
        c0 = g * PCH
        for cc in range(PCH):
            for n, dt in (("wT", d_tm["wT"]), ("kkT", d_tm["kkT"]), ("aT", d_tm["aT"]),
                          ("k2T", d_tm["k2T"]), ("vT", d_tm["vT"])):
                src = bass.AP(dt, (c0 + cc) * C * 64,
                              [[64, 128], [T * 64, NS], [1, 64]])
                dst = in_tm[n][buf][:, cc * 512:(cc + 1) * 512].rearrange(
                    "p (s v) -> p s v", s=NS)
                r.dma("sync", dst, src, inc="ld")
            for n, df in (("rF", d_fm["rF"]), ("kkF", d_fm["kkF"]), ("aF", d_fm["aF"]),
                          ("k2F", d_fm["k2F"])):
                src = bass.AP(df, (c0 + cc) * C,
                              [[T, 128], [2 * 64 * T, 4], [1, C]])
                dst = in_fm[n][buf][:, cc * 512:(cc + 1) * 512].rearrange(
                    "p (j t) -> p j t", j=4)
                r.dma("sync", dst, src, inc="ld")

    NG = NCH // PCH
    grp_done = {}

    load_group(0, 0)
    ld_per_group = 9 * 16 * PCH

    for g in range(NG):
        buf = g % 2
        if g + 1 < NG:
            if g >= 1:
                for sem in ("pe", "ve"):
                    r.wait("sync", sem, grp_done[g - 1][sem])
            load_group(g + 1, 1 - buf)
        for e in ("tensor", "vector", "scalar"):
            r.wait(e, "ld", (g + 1) * ld_per_group)

        for cc in range(PCH):
            ch = g * PCH + cc
            ob = ch % 2
            tmsl = lambda n, lo, w: in_tm[n][buf][:, cc * 512 + lo: cc * 512 + lo + w]
            fmsl = lambda n, j: in_fm[n][buf][:, cc * 512 + j * 128: cc * 512 + (j + 1) * 128]

            if stage < 3:
                r.wait("vector", "st", max(0, r.val("st") - 16))
                for s in range(NS):
                    r.op("vector", lambda eng, s=s, ob=ob: eng.memset(
                        so[ob][:, s * 64:(s + 1) * 64], 0.0), inc="oev")
            if stage < 2:
                r.wait("sync", "oev", r.val("oev"))
                dst = bass.AP(d_o, ch * C * 64, [[64, 128], [T * 64, NS], [1, 64]])
                srcap = so[ob][:].rearrange("p (s v) -> p s v", s=NS)
                r.dma("sync", dst, srcap, inc="st")
                continue
            wT_all = tmsl("wT", 0, 512)
            kkT_all = tmsl("kkT", 0, 512)
            aT_all = tmsl("aT", 0, 512)
            k2T_all = tmsl("k2T", 0, 512)
            r.wait("tensor", "ve")
            r.op("tensor", lambda eng, w=wT_all: eng.matmul(
                p_cum[:], sut_i[:], w, start=True, stop=True), inc="pe")
            r.wait("vector", "pe")
            r.wait("vector", "ac")
            r.op("vector", lambda eng: eng.tensor_scalar(
                out=slogL[:], in0=p_cum[:], scalar1=DECAY, scalar2=None,
                op0=ALU.mult), inc="ve", nodrain=True)
            r.wait("tensor", "ve")
            r.op("tensor", lambda eng, w=wT_all: eng.matmul(
                p_cum[:], sut_s[:], w, start=True, stop=True), inc="pe")
            r.wait("vector", "pe")
            r.op("vector", lambda eng: eng.tensor_scalar(
                out=slogLp[:], in0=p_cum[:], scalar1=DECAY, scalar2=None,
                op0=ALU.mult), inc="ve", nodrain=True)
            r.wait("scalar", "ve")
            r.op("scalar", lambda eng: eng.activation(
                out=sLamI_tm[:], in_=slogL[:], func=AF.Exp, scale=-1.0), inc="ac",
                nodrain=True)
            r.op("scalar", lambda eng: eng.activation(
                out=sLamP_tm[:], in_=slogLp[:], func=AF.Exp), inc="ac", nodrain=True)
            r.wait("vector", "ac")
            r.op("vector", lambda eng, kk=kkT_all: eng.scalar_tensor_tensor(
                out=sAt_tm[:], in0=kk, scalar=-1.0,
                in1=sLamP_tm[:], op0=ALU.mult, op1=ALU.mult), inc="ve", nodrain=True)
            r.op("vector", lambda eng, kk=kkT_all, aa=aT_all: eng.tensor_tensor(
                out=sBt_tm[:], in0=kk, in1=aa, op=ALU.mult), inc="ve")
            r.op("vector", lambda eng: eng.tensor_tensor(
                out=sBt_tm[:], in0=sBt_tm[:], in1=sLamI_tm[:], op=ALU.mult),
                inc="ve")
            r.op("vector", lambda eng, k2=k2T_all: eng.tensor_tensor(
                out=sKt_tm[:], in0=k2, in1=sLamI_tm[:],
                op=ALU.mult), inc="ve", nodrain=True)

            for j in range(4):
                wT_j = tmsl("wT", j * 128, 128)
                kkF_j = fmsl("kkF", j)
                aF_j = fmsl("aF", j)
                k2F_j = fmsl("k2F", j)
                rF_j = fmsl("rF", j)
                r.wait("tensor", "ve")
                r.op("tensor", lambda eng, w=wT_j: eng.matmul(
                    p_a[:], w, sut_i[:], start=True, stop=True), inc="pe")
                r.wait("vector", "pe")
                r.wait("vector", "ac")
                r.op("vector", lambda eng, j=j: eng.tensor_scalar(
                    out=slogL_fm[j][:], in0=p_a[:], scalar1=DECAY, scalar2=None,
                    op0=ALU.mult), inc="ve")
                r.wait("scalar", "ve")
                r.op("scalar", lambda eng, j=j: eng.activation(
                    out=sLam_fm[j][:, 1:129], in_=slogL_fm[j][:], func=AF.Exp),
                    inc="ac", nodrain=True)
                r.op("scalar", lambda eng, j=j: eng.activation(
                    out=sLamI_fm[j][:], in_=slogL_fm[j][:], func=AF.Exp,
                    scale=-1.0), inc="ac", nodrain=True)
                r.op("scalar", lambda eng, j=j: eng.activation(
                    out=sLamC[j][:], in_=slogL_fm[j][:, 127:128], func=AF.Exp),
                    inc="ac", nodrain=True)
                r.wait("vector", "ac")
                r.op("vector", lambda eng, j=j, kk=kkF_j: eng.scalar_tensor_tensor(
                    out=sAt_fm[j][:], in0=kk, scalar=-1.0,
                    in1=sLam_fm[j][:, 0:128], op0=ALU.mult, op1=ALU.mult),
                    inc="ve", nodrain=True)
                r.op("vector", lambda eng, j=j, kk=kkF_j, aa=aF_j: eng.tensor_tensor(
                    out=sBt_fm[j][:], in0=kk, in1=aa, op=ALU.mult), inc="ve")
                r.op("vector", lambda eng, j=j: eng.tensor_tensor(
                    out=sBt_fm[j][:], in0=sBt_fm[j][:], in1=sLamI_fm[j][:],
                    op=ALU.mult), inc="ve")
                r.op("vector", lambda eng, j=j, k2=k2F_j: eng.tensor_tensor(
                    out=sKt_fm[j][:], in0=k2, in1=sLamI_fm[j][:],
                    op=ALU.mult), inc="ve", nodrain=True)
                r.op("vector", lambda eng, j=j, rr=rF_j: eng.tensor_tensor(
                    out=sRt_fm[j][:], in0=rr, in1=sLam_fm[j][:, 1:129],
                    op=ALU.mult), inc="ve", nodrain=True)

            r.wait("vector", "st", max(0, r.val("st") - 16 * 1))

            for s in range(NS if stage >= 3 else 0):
                j = s // 2
                lo = (s % 2) * 64
                A_f = sAt_fm[j][lo:lo + 64, :]
                B_f = sBt_fm[j][lo:lo + 64, :]
                K_f = sKt_fm[j][lo:lo + 64, :]
                R_f = sRt_fm[j][lo:lo + 64, :]
                A_t = sAt_tm[:, s * 64:(s + 1) * 64]
                B_t = sBt_tm[:, s * 64:(s + 1) * 64]
                K_t = sKt_tm[:, s * 64:(s + 1) * 64]
                V_t = tmsl("vT", s * 64, 64)
                lamc = sLamC[j][lo:lo + 64, :]

                def mm(pp, lhsT, rhs, start=True, stop=True, inc="pe"):
                    r.op("tensor", lambda eng, pp=pp, l=lhsT, rr=rhs, st=start,
                         sp=stop: eng.matmul(pp, l, rr, start=st, stop=sp), inc=inc)

                def evac_v(dst, src_p, mask=None, add=None, inc="ve", nodrain=True):
                    r.wait("vector", "pe")
                    if mask is not None:
                        r.op("vector", lambda eng, d=dst, sp=src_p, m=mask:
                             eng.tensor_tensor(out=d, in0=sp, in1=m, op=ALU.mult),
                             inc=inc, nodrain=nodrain)
                    elif add is not None:
                        r.op("vector", lambda eng, d=dst, sp=src_p, a=add:
                             eng.tensor_tensor(out=d, in0=sp, in1=a, op=ALU.add),
                             inc=inc, nodrain=nodrain)
                    else:
                        r.op("vector", lambda eng, d=dst, sp=src_p:
                             eng.tensor_copy(out=d, in_=sp), inc=inc, nodrain=nodrain)

                def evac_s(dst, src_p, inc="ac"):
                    r.wait("scalar", "pe")
                    r.op("scalar", lambda eng, d=dst, sp=src_p: eng.activation(
                        out=d, in_=sp, func=AF.Identity), inc=inc, nodrain=True)

                r.wait("tensor", "ve")
                r.wait("tensor", "ac")
                if stage < 4:
                    r.op("vector", lambda eng, s=s, ob=ob: eng.memset(
                        so[ob][:, s * 64:(s + 1) * 64], 0.0), inc="oev")
                    continue
                mm(p_a[:], A_f, B_f)
                evac_v(sM[:], p_a[:], mask=smsl[:])
                mm(p_b[:], B_f, A_f)
                evac_v(sMT[:], p_b[:], mask=smsu[:])
                r.wait("tensor", "ve")
                mm(p_a[:], K_f, A_f)
                evac_v(sNT[:], p_a[:], mask=smsu[:])
                mm(p_b[:], B_f, R_f)
                evac_v(sTLRBt[:], p_b[:], mask=smtu[:])
                r.wait("tensor", "ve")
                mm(p_a[:], K_f, R_f)
                evac_v(sTLRKt[:], p_a[:], mask=smtu[:])
                r.wait("vector", "ve")
                r.op("vector", lambda eng: eng.tensor_tensor(
                    out=sWT[0][:], in0=sMT[:], in1=sident[:], op=ALU.add), inc="ve")
                r.op("vector", lambda eng: eng.tensor_copy(out=sMk[0][:], in_=sM[:]),
                     inc="ve", nodrain=True)
                r.op("vector", lambda eng: eng.tensor_copy(out=sMkT[0][:], in_=sMT[:]),
                     inc="ve", nodrain=True)
                if stage < 5:
                    r.op("vector", lambda eng, s=s, ob=ob: eng.memset(
                        so[ob][:, s * 64:(s + 1) * 64], 0.0), inc="oev")
                    continue
                wt_cur = 0
                mk_cur = 0
                for lvl in range(6):
                    r.wait("tensor", "ve")
                    r.wait("tensor", "ac")
                    mm(p_a[:], sMkT[mk_cur][:], sMk[mk_cur][:])
                    evac_s(sMk[1 - mk_cur][:], p_a[:])
                    if lvl < 5:
                        mm(p_b[:], sMk[mk_cur][:], sMkT[mk_cur][:])
                        evac_s(sMkT[1 - mk_cur][:], p_b[:])
                    mk_cur = 1 - mk_cur
                    r.wait("tensor", "ac")
                    mm(p_a[:], sMk[mk_cur][:], sWT[wt_cur][:])
                    evac_v(sWT[1 - wt_cur][:], p_a[:], add=sWT[wt_cur][:])
                    wt_cur = 1 - wt_cur
                WT = sWT[wt_cur]
                r.wait("tensor", "ve")
                mm(p_sm[:], sNT[:], V_t)
                evac_v(sNV[:], p_sm[:])
                r.wait("tensor", "ve")
                mm(p_sm[:], WT[:], sNV[:])
                evac_v(sUp[:], p_sm[:])
                mm(p_o[:], WT[:], A_t)
                evac_v(sWAs[:], p_o[:])
                r.wait("tensor", "ve")
                mm(p_rh[:], sWAs[:], sTLRBt[:])
                evac_v(sRhat[:], p_rh[:], add=R_f)
                mm(p_h[:], sWAs[:], B_t)
                evac_v(sIpH[:], p_h[:], add=sident[0:64, 0:64])
                if stage < 6:
                    r.op("vector", lambda eng, s=s, ob=ob: eng.memset(
                        so[ob][:, s * 64:(s + 1) * 64], 0.0), inc="oev")
                    continue
                r.wait("tensor", "ve")
                mm(p_o[:], sRhat[:], sS[s][:], start=True, stop=False, inc=None)
                mm(p_o[:], sTLRBt[:], sUp[:], start=False, stop=False, inc=None)
                mm(p_o[:], sTLRKt[:], V_t, start=False, stop=True)
                evac_v(so[ob][:, s * 64:(s + 1) * 64], p_o[:], inc="oev",
                       nodrain=True)
                mm(p_st[:], sIpH[:], sS[s][:], start=True, stop=False, inc=None)
                mm(p_st[:], B_t, sUp[:], start=False, stop=False, inc=None)
                mm(p_st[:], K_t, V_t, start=False, stop=True)
                r.wait("vector", "pe")
                r.op("vector", lambda eng, s=s, lamc=lamc: eng.tensor_scalar(
                    out=sS[s][:], in0=p_st[:], scalar1=lamc, scalar2=None,
                    op0=ALU.mult), inc="ve", nodrain=True)
                if debug and ch == 0 and s == 0:
                    r.wait("gpsimd", "ve")
                    r.wait("gpsimd", "ac")
                    for nm, tile in (("dM", sM), ("dMT", sMT), ("dNT", sNT),
                                     ("dTLRBt", sTLRBt), ("dTLRKt", sTLRKt),
                                     ("dWT", sWT[wt_cur]), ("dAtf", sAt_fm[0]),
                                     ("dBtf", sBt_fm[0]), ("dKtf", sKt_fm[0]),
                                     ("dRtf", sRt_fm[0]), ("dLam", sLam_fm[0]),
                                     ("dLamI", sLamI_fm[0])):
                        r.dma("gpsimd", d_dbg[nm].ap(), tile[:], inc="dbg")
                    for nm, tile in (("dNV", sNV), ("dUp", sUp), ("dWA", sWAs)):
                        r.dma("gpsimd", d_dbg2[nm].ap(), tile[:], inc="dbg")
                    r.dma("gpsimd", d_dbg3["dRhat"].ap(), sRhat[:], inc="dbg")
                    r.dma("gpsimd", d_dbg4["dIpH"].ap(), sIpH[:], inc="dbg")
                    r.dma("gpsimd", d_dbg4["dS"].ap(), sS[0][:], inc="dbg")
                    r.dma("gpsimd", d_dbg5.ap(), in_tm["wT"][buf][:, 0:512], inc="dbg")
                    r.dma("gpsimd", d_dbg6.ap(), slogL_fm[0][:], inc="dbg")
                    r.dma("gpsimd", d_dbg7.ap(), in_tm["vT"][buf][:, 0:512], inc="dbg")
                    r.dma("gpsimd", d_dbg8.ap(), sut_i[:], inc="dbg")
                    r.dma("gpsimd", d_dbg9.ap(), sut_s[:], inc="dbg")
                    for e in ("tensor", "vector", "scalar"):
                        r.wait(e, "dbg", r.val("dbg"))

            r.wait("sync", "oev", r.val("oev"))
            dst = bass.AP(d_o, ch * C * 64, [[64, 128], [T * 64, NS], [1, 64]])
            src = so[ob][:].rearrange("p (s v) -> p s v", s=NS)
            r.dma("sync", dst, src, inc="st")

        grp_done[g] = {"pe": r.val("pe"), "ve": r.val("ve")}

    r.wait("sync", "st", r.val("st"))
    r.build_block()
    r.close()
    return nc



def build_l0post(F=None):
    if F is None:
        nc = bass.Bass("TRN2", target_bir_lowering=False)
        r = Rec(nc)
        din = lambda n, s, d: nc.dram_tensor(n, list(s), d, kind="ExternalInput")
    else:
        nc, r = F.nc, F.r
        F.begin_phase("C_")
        din = F.ext_in
    NTile = NT // 128
    TTG = 512
    NTTG = NT // TTG
    DFF = 4096

    if F is None:
        dx = din("x", [NT, D], F32)
        do = din("o", [NT, D], F32)
        dr = din("r_tm", [NT, D], BF16)
        dk2 = din("k2_tm", [NT, D], BF16)
        dv = din("v_tm", [NT, D], BF16)
        dg = din("g_tm", [NT, D], BF16)
        dxn = din("xn_fm", [D, NT], BF16)
    else:
        dx = None
        do = F.scr("o", [NT, D], F32)
        dr = F.scr("r_tm", [NT, D], BF16)
        dk2 = F.scr("k2_tm", [NT, D], BF16)
        dv = F.scr("v_tm", [NT, D], BF16)
        dg = F.scr("g_tm", [NT, D], BF16)
        dxn = F.scr("xn_fm", [D, NT], BF16)
    drk = din("rk_tile", [128, D], F32)
    dgnw = din("gnw_tile", [128, D], F32)
    dgnb = din("gnb_tile", [128, D], F32)
    dWo = din("Wo", [D, D], F32)
    drW = din("router_W", [D, 8], F32)
    drb = din("router_b", [1, 8], F32)
    dW1 = din("W1", [8, D, DFF], F32)
    db1 = din("b1", [8, DFF], F32)
    dW2 = din("W2", [8, DFF, D], F32)
    db2 = din("b2", [8, D], F32)
    dident = din("identD", [128, 128], BF16)

    if F is None:
        dx1 = nc.dram_tensor("x1", [NT, D], F32, kind="ExternalOutput")
        dxn1 = nc.dram_tensor("xn1_tm", [NT, D], BF16, kind="ExternalOutput")
        xn1_row0 = 0
    else:
        dx1 = F.scr("x1", [NT, D], F32)
        dxn1 = F.scr("xn1h_scr", [NT + 30, D], BF16)
        xn1_row0 = 30

    mk_int = (lambda n, s, d: F.scr(n, s, d)) if F is not None else \
        (lambda n, s, d: nc.dram_tensor(n, list(s), d, kind="Internal"))
    s0_scr = mk_int("s0_scr", [NT, D], F32)
    y_scr = mk_int("y_scr", [NT, D], F32)
    w1bf = mk_int("w1bf", [8, D, DFF], BF16)
    w2bf = mk_int("w2bf", [8, DFF, D], BF16)
    b1bf = mk_int("b1bf", [8, DFF], BF16)
    b2bf = mk_int("b2bf", [8, D], BF16)
    gatesT_scr = mk_int("gatesT_scr", [8, NT], BF16)

    sident = r.sb("sident", [128, 128], BF16)
    srk = r.sb("srk", [128, D], BF16)
    sgnw = r.sb("sgnw", [128, D], F32)
    sgnb = r.sb("sgnb", [128, D], F32)
    sWo = r.sb("sWo", [128, 8 * D], BF16)
    srW = r.sb("srW", [128, 8 * 8], BF16)
    srb = r.sb("srb", [1, 8], BF16)
    sones = r.sb("sones", [1, 128], BF16)
    sgT_strip = r.sb("sgT_strip", [1, 8 * 128], BF16)
    sgTrow = r.sb("sgTrow", [1, 512], BF16)

    so_t = r.sb("so_t", [128, D], F32)
    sr_t = r.sb("sr_t", [128, D], BF16)
    sk2_t = r.sb("sk2_t", [128, D], BF16)
    sv_t = r.sb("sv_t", [128, D], BF16)
    sg_t = r.sb("sg_t", [128, D], BF16)
    sxn_t = r.sb("sxn_t", [128, 8 * 128], BF16)
    ssq32 = r.sb("ssq32", [128, D], F32)
    gstat = r.sb("gstat", [128, 7 * 16], F32)
    st_a = r.sb("st_a", [128, D], F32)
    st_b = ssq32
    s_in = r.sb("s_in", [128, D], BF16)
    sfm_s = r.sb("sfm_s", [128, 8 * 128], BF16)
    sstage = r.sb("sstage", [128, D], F32)
    sgates = r.sb("sgates", [128, 8], F32)
    sgates_bf = r.sb("sgates_bf", [128, 8], BF16)
    sprob = r.sb("sprob", [128, 12], F32)

    sW2e = r.sb("sW2e", [128, 16 * D], BF16)
    sh = r.sb("sh", [128, 32 * TTG], BF16)
    sxn_g = r.sb("sxn_g", [128, 8 * TTG], BF16)
    sxg = r.sb("sxg", [128, 8 * TTG], BF16)
    sgbc = r.sb("sgbc", [128, TTG], BF16)
    sw1f = [r.sb(f"sw1f{p}", [128, 8 * 128], BF16) for p in range(2)]
    sb1e = r.sb("sb1e", [1, DFF], BF16)
    sb2e = r.sb("sb2e", [1, D], BF16)
    szero = r.sb("szero", [128, D], F32)
    szrow = r.sb("szrow", [1, D], BF16)

    sx_t = sstage
    ss0_t = st_b
    sy_t = st_a
    sg0_t = so_t
    sxn1_t = s_in
    lstat = r.sb("lstat", [128, 8], F32)

    if F is None:
        p_tr = r.ps("p_tr", [128, 256], BF16).ap()[:, 0:128]
        p_big = r.ps("p_big", [128, D], F32).ap()
        p_sm = r.ps("p_sm", [128, 512], F32).ap()
        p_h = [r.ps(f"p_h{i}", [128, 512], F32).ap() for i in range(2)]
    else:
        p_tr = F.pC_bf[:, 0:128]
        p_big = F.pA
        p_sm = F.pB[:, 0:512]
        p_h = [F.pD[:, 0:512], F.pD[:, 512:1024]]

    r.dma("sync", sident[:], dident.ap(), inc="wld")
    r.dma("gpsimd", srk[:], drk.ap(), inc="cast")
    r.dma("sync", sgnw[:], dgnw.ap(), inc="wld")
    r.dma("sync", sgnb[:], dgnb.ap(), inc="wld")
    for d in range(8):
        src = bass.AP(dWo, d * 128 * D, [[D, 128], [1, D]])
        r.dma("gpsimd", sWo[:, d * D:(d + 1) * D], src, inc="cast")
        src = bass.AP(drW, d * 128 * 8, [[8, 128], [1, 8]])
        r.dma("gpsimd", srW[:, d * 8:(d + 1) * 8], src, inc="cast")
    r.dma("gpsimd", srb[:], drb.ap(), inc="cast")
    r.op("gpsimd", lambda eng: eng.memset(sones[:], 1.0), inc="gp")
    r.op("gpsimd", lambda eng: eng.memset(szero[:], 0.0), inc="gp")
    r.op("gpsimd", lambda eng: eng.memset(szrow[:], 0.0), inc="gp")
    CAST2 = r.val("cast")
    r.dma("gpsimd", w1bf.ap(), dW1.ap(), inc="cast")
    r.dma("gpsimd", w2bf.ap(), dW2.ap(), inc="cast")
    r.dma("gpsimd", b1bf.ap(), db1.ap(), inc="cast")
    r.dma("gpsimd", b2bf.ap(), db2.ap(), inc="cast")
    CAST = r.val("cast")
    WLD = r.val("wld")
    GP = r.val("gp")
    for e in ("tensor", "vector", "scalar"):
        r.wait(e, "wld", WLD)
        r.wait(e, "cast", CAST2)
        r.wait(e, "gp", GP)

    r.wait("sync", "gp", GP)
    for i in range(NTile):
        r.dma("sync", y_scr.ap()[i * 128:(i + 1) * 128, :], szero[:], inc="yz")
    YZ = r.val("yz")

    seg = lambda t: t.rearrange("p (h j) -> p h j", h=16)

    def bcast16(col_ap):
        return col_ap.unsqueeze(2).broadcast_to([128, 16, 64])

    for i in range(NTile):
        rows = slice(i * 128, (i + 1) * 128)
        r.wait("sync", "ve")
        r.wait("sync", "ac")
        r.wait("sync", "pe")
        r.wait("sync", "st_sync", r.val("st_sync"))
        r.dma("sync", so_t[:], do.ap()[rows, :], inc="tl")
        r.dma("sync", sr_t[:], dr.ap()[rows, :], inc="tl")
        r.dma("sync", sk2_t[:], dk2.ap()[rows, :], inc="tl")
        r.dma("sync", sv_t[:], dv.ap()[rows, :], inc="tl")
        r.dma("sync", sg_t[:], dg.ap()[rows, :], inc="tl")
        src = bass.AP(dxn, i * 128, [[NT, 128], [128 * NT, 8], [1, 128]])
        r.dma("sync", sxn_t[:].rearrange("p (c j) -> p c j", c=8), src, inc="tl")
        TL = r.val("tl")
        r.wait("scalar", "tl", TL)
        r.wait("scalar", "ve")
        r.wait("vector", "tl", TL)

        r.op("scalar", lambda eng: eng.activation(
            out=ssq32[:], in_=so_t[:], func=AF.Square), inc="ac")
        r.wait("vector", "ac")
        r.op("vector", lambda eng: eng.tensor_reduce(
            out=gstat[:, 0:16], in_=seg(so_t[:]), axis=AX.X, op=ALU.add), inc="ve")
        r.op("vector", lambda eng: eng.tensor_reduce(
            out=gstat[:, 16:32], in_=seg(ssq32[:]), axis=AX.X, op=ALU.add), inc="ve")
        r.op("vector", lambda eng: eng.tensor_scalar(
            out=gstat[:, 32:48], in0=gstat[:, 0:16], scalar1=1.0 / 64,
            scalar2=None, op0=ALU.mult))
        r.op("vector", lambda eng: eng.tensor_scalar(
            out=gstat[:, 48:64], in0=gstat[:, 16:32], scalar1=1.0 / 64,
            scalar2=GN_EPS, op0=ALU.mult, op1=ALU.add))
        r.op("vector", lambda eng: eng.tensor_tensor(
            out=gstat[:, 64:80], in0=gstat[:, 32:48], in1=gstat[:, 32:48],
            op=ALU.mult))
        r.op("vector", lambda eng: eng.tensor_tensor(
            out=gstat[:, 80:96], in0=gstat[:, 48:64], in1=gstat[:, 64:80],
            op=ALU.subtract), inc="ve")
        r.wait("scalar", "ve")
        r.op("scalar", lambda eng: eng.activation(
            out=gstat[:, 80:96], in_=gstat[:, 80:96], func=AF.Sqrt), inc="ac")
        r.wait("vector", "ac")
        r.op("vector", lambda eng: eng.reciprocal(
            out=gstat[:, 96:112], in_=gstat[:, 80:96]))
        r.op("vector", lambda eng: eng.tensor_tensor(
            out=seg(st_a[:]), in0=seg(so_t[:]), in1=bcast16(gstat[:, 32:48]),
            op=ALU.subtract))
        r.op("vector", lambda eng: eng.tensor_tensor(
            out=seg(st_a[:]), in0=seg(st_a[:]), in1=bcast16(gstat[:, 96:112]),
            op=ALU.mult))
        r.op("vector", lambda eng: eng.tensor_tensor(
            out=st_a[:], in0=st_a[:], in1=sgnw[:], op=ALU.mult))
        r.op("vector", lambda eng: eng.tensor_tensor(
            out=st_a[:], in0=st_a[:], in1=sgnb[:], op=ALU.add), inc="ve")
        r.op("vector", lambda eng: eng.tensor_tensor(
            out=st_b[:], in0=sr_t[:], in1=sk2_t[:], op=ALU.mult))
        r.op("vector", lambda eng: eng.tensor_tensor(
            out=st_b[:], in0=st_b[:], in1=srk[:], op=ALU.mult))
        r.op("vector", lambda eng: eng.tensor_reduce(
            out=gstat[:, 0:16], in_=seg(st_b[:]), axis=AX.X, op=ALU.add))
        r.op("vector", lambda eng: eng.tensor_tensor(
            out=seg(st_b[:]), in0=bcast16(gstat[:, 0:16]),
            in1=seg(sv_t[:]), op=ALU.mult))
        r.op("vector", lambda eng: eng.tensor_tensor(
            out=st_a[:], in0=st_a[:], in1=st_b[:], op=ALU.add))
        r.op("vector", lambda eng: eng.tensor_tensor(
            out=s_in[:], in0=st_a[:], in1=sg_t[:], op=ALU.mult), inc="ve")

        for c in range(8):
            r.wait("tensor", "ve")
            r.op("tensor", lambda eng, c=c: eng.transpose(
                p_tr[:], s_in[:, c * 128:(c + 1) * 128], sident[:]), inc="pe")
            r.wait("vector", "pe")
            r.op("vector", lambda eng, c=c: eng.tensor_copy(
                out=sfm_s[:, c * 128:(c + 1) * 128], in_=p_tr[:]), inc="ve",
                nodrain=True)
        r.wait("tensor", "ve")
        r.wait("tensor", "ac")
        for d in range(8):
            for hf in range(2):
                r.op("tensor", lambda eng, d=d, hf=hf: eng.matmul(
                    p_big[:, hf * 512:(hf + 1) * 512],
                    sfm_s[:, d * 128:(d + 1) * 128],
                    sWo[:, d * D + hf * 512: d * D + (hf + 1) * 512],
                    start=(d == 0), stop=(d == 7)),
                    inc=("pe" if (d == 7 and hf == 1) else None))
        r.wait("scalar", "pe")
        r.op("scalar", lambda eng: eng.activation(
            out=sstage[:], in_=p_big[:], func=AF.Identity), inc="ac")
        r.wait("sync", "ac")
        r.dma("sync", s0_scr.ap()[rows, :], sstage[:], inc="st_sync")

        r.wait("tensor", "ve")
        for d in range(8):
            r.op("tensor", lambda eng, d=d: eng.matmul(
                p_sm[:, 0:8], sxn_t[:, d * 128:(d + 1) * 128],
                srW[:, d * 8:(d + 1) * 8], start=(d == 0), stop=False), inc=None)
        r.op("tensor", lambda eng: eng.matmul(
            p_sm[:, 0:8], sones[0:1, :], srb[:], start=False, stop=True), inc="pe")
        r.wait("vector", "pe")
        r.op("vector", lambda eng: eng.tensor_reduce(
            out=sprob[:, 8:9], in_=p_sm[:, 0:8], axis=AX.X, op=ALU.max))
        r.op("vector", lambda eng: eng.tensor_scalar(
            out=sprob[:, 9:10], in0=sprob[:, 8:9], scalar1=-1.0, scalar2=None,
            op0=ALU.mult))
        r.op("vector", lambda eng: eng.tensor_copy(
            out=sprob[:, 0:8].rearrange("p a -> p a"), in_=p_sm[:, 0:8]), inc="ve")
        r.wait("scalar", "ve")
        r.op("scalar", lambda eng: eng.activation(
            out=sgates[:], in_=sprob[:, 0:8], func=AF.Exp,
            bias=sprob[:, 9:10]), inc="ac")
        r.wait("vector", "ac")
        r.op("vector", lambda eng: eng.tensor_reduce(
            out=sprob[:, 10:11], in_=sgates[:], axis=AX.X, op=ALU.max))
        r.op("vector", lambda eng: eng.tensor_reduce(
            out=sprob[:, 11:12], in_=sgates[:], axis=AX.X, op=ALU.add))
        r.op("vector", lambda eng: eng.reciprocal(
            out=sprob[:, 11:12], in_=sprob[:, 11:12]))
        r.op("vector", lambda eng: eng.tensor_scalar(
            out=st_b[:, 0:8], in0=sgates[:], scalar1=sprob[:, 10:11], scalar2=None,
            op0=ALU.is_ge))
        r.op("vector", lambda eng: eng.tensor_tensor(
            out=sgates[:], in0=sgates[:], in1=st_b[:, 0:8], op=ALU.mult))
        r.op("vector", lambda eng: eng.tensor_scalar(
            out=sgates_bf[:], in0=sgates[:], scalar1=sprob[:, 11:12], scalar2=None,
            op0=ALU.mult), inc="ve")
        r.wait("vector", "gts", r.val("gts"))
        for e in range(8):
            r.wait("tensor", "ve")
            r.op("tensor", lambda eng, e=e: eng.transpose(
                p_tr[0:1, :], sgates_bf[:, e:e + 1], sident[:]), inc="pe")
            r.wait("vector", "pe")
            r.op("vector", lambda eng, e=e: eng.tensor_copy(
                out=sgT_strip[0:1, e * 128:(e + 1) * 128], in_=p_tr[0:1, 0:128]),
                inc="ve", nodrain=True)
        r.wait("sync", "ve")
        r.dma("sync", bass.AP(gatesT_scr, i * 128, [[NT, 8], [1, 128]]),
              sgT_strip[0:1, :].rearrange("p (e j) -> p e j", e=8), inc="gts")

    r.wait("tensor", "cast", CAST)
    r.wait("sync", "yz", YZ)
    PHB = {"pe": r.val("pe"), "ve": r.val("ve"), "ac": r.val("ac")}
    for e in range(8):
        r.wait("sync", "pe")
        r.dma("sync", sb1e[:], b1bf.ap()[e:e + 1, :], inc="w2l")
        r.dma("sync", sb2e[:], b2bf.ap()[e:e + 1, :], inc="w2l")
        r.wait("tensor", "w2l", r.val("w2l"))
        for t in range(NTTG):
            cols = slice(t * TTG, (t + 1) * TTG)
            r.wait("sync", "ve")
            src = bass.AP(dxn, t * TTG, [[NT, 128], [128 * NT, 8], [1, TTG]])
            r.dma("sync", sxn_g[:].rearrange("p (c j) -> p c j", c=8), src, inc="xnl")
            XNL = r.val("xnl")
            r.wait("sync", "gts", r.val("gts"))
            r.wait("sync", "pe")
            r.dma("sync", sgTrow[:], gatesT_scr.ap()[e:e + 1, t * TTG:(t + 1) * TTG],
                  inc="gtl")
            r.wait("tensor", "gtl", r.val("gtl"))
            r.wait("tensor", "ve")
            r.op("tensor", lambda eng: eng.matmul(
                p_sm[:], sones[0:1, :], sgTrow[0:1, :],
                start=True, stop=True), inc="pe")
            r.wait("vector", "pe")
            r.op("vector", lambda eng: eng.tensor_copy(
                out=sgbc[:], in_=p_sm[:]), inc="ve", nodrain=True)
            r.wait("vector", "xnl", XNL)
            for c in range(8):
                r.op("vector", lambda eng, c=c: eng.tensor_tensor(
                    out=sxg[:, c * TTG:(c + 1) * TTG],
                    in0=sxn_g[:, c * TTG:(c + 1) * TTG], in1=sgbc[:],
                    op=ALU.mult), inc="ve", nodrain=True)
            r.op("vector", lambda eng: eng.engine_nop() if False else
                 eng.tensor_copy(out=sgbc[0:1, 0:1], in_=sgbc[0:1, 0:1]), inc="ve")
            for f in range(32):
                par = f % 2
                if f >= 2:
                    r.wait("sync", "pe")
                r.dma("sync", sw1f[par][:].rearrange("p (d j) -> p d j", d=8),
                      bass.AP(w1bf, e * D * DFF + f * 128,
                              [[DFF, 128], [128 * DFF, 8], [1, 128]]),
                      inc="w1l")
                r.wait("tensor", "w1l", r.val("w1l"))
                if f >= 2:
                    r.wait("tensor", "ac")
                r.wait("tensor", "ve")
                for d in range(8):
                    r.op("tensor", lambda eng, f=f, d=d, par=par: eng.matmul(
                        p_h[par][:], sw1f[par][:, d * 128:(d + 1) * 128],
                        sxg[:, d * TTG:(d + 1) * TTG],
                        start=(d == 0), stop=False), inc=None)
                r.op("tensor", lambda eng, f=f, par=par: eng.matmul(
                    p_h[par][:], sb1e[0:1, f * 128:(f + 1) * 128],
                    sgTrow[0:1, :],
                    start=False, stop=True), inc="pe")
                r.wait("scalar", "pe")
                r.op("scalar", lambda eng, f=f, par=par: eng.activation(
                    out=sh[:, f * TTG:(f + 1) * TTG], in_=p_h[par][:],
                    func=AF.Relu), inc="ac", nodrain=True)
            for fh in range(2):
                r.wait("sync", "pe")
                r.dma("sync", sW2e[:].rearrange("p (f j) -> p f j", f=16),
                      bass.AP(w2bf, e * DFF * D + fh * 16 * 128 * D,
                              [[D, 128], [128 * D, 16], [1, D]]), inc="w2l")
                r.wait("tensor", "w2l", r.val("w2l"))
                for ti in range(TTG // 128):
                    r.wait("tensor", "ac")
                    for f in range(16):
                        fg = fh * 16 + f
                        for hf in range(2):
                            r.op("tensor", lambda eng, f=f, fg=fg, ti=ti, hf=hf:
                                 eng.matmul(
                                     p_big[:, hf * 512:(hf + 1) * 512],
                                     sh[:, fg * TTG + ti * 128: fg * TTG + (ti + 1) * 128],
                                     sW2e[:, f * D + hf * 512: f * D + (hf + 1) * 512],
                                     start=(f == 0), stop=False), inc=None)
                    for hf in range(2):
                        if fh == 1:
                            r.op("tensor", lambda eng, ti=ti, hf=hf:
                                 eng.matmul(
                                     p_big[:, hf * 512:(hf + 1) * 512],
                                     sgTrow[0:1, ti * 128:(ti + 1) * 128],
                                     sb2e[0:1, hf * 512:(hf + 1) * 512],
                                     start=False, stop=True),
                                 inc=("pe" if hf == 1 else None))
                        else:
                            r.op("tensor", lambda eng, hf=hf: eng.matmul(
                                p_big[:, hf * 512:(hf + 1) * 512],
                                sones[0:1, 0:128],
                                szero.bitcast(BF16)[0:1, 0:512] if False else
                                szrow[0:1, hf * 512:(hf + 1) * 512],
                                start=False, stop=True),
                                inc=("pe" if hf == 1 else None))
                    gr = slice(t * TTG + ti * 128, t * TTG + ti * 128 + 128)
                    r.wait("scalar", "pe")
                    r.wait("scalar", "rmw", r.val("rmw"))
                    r.op("scalar", lambda eng: eng.activation(
                        out=sstage[:], in_=p_big[:], func=AF.Identity), inc="ac")
                    r.wait("gpsimd", "ac")
                    r.dma("gpsimd", y_scr.ap()[gr, :], sstage[:],
                          accum_op=ALU.add, inc="rmw")
    r.wait("sync", "rmw", r.val("rmw"))

    for i in range(NTile):
        rows = slice(i * 128, (i + 1) * 128)
        r.wait("sync", "ve")
        r.wait("sync", "ac")
        r.wait("sync", "st_sync", r.val("st_sync"))
        if F is None:
            r.dma("sync", sx_t[:], dx.ap()[rows, :], inc="dl")
        else:
            r.dma("sync", sx_t[:],
                  F.ext["xh"].ap()[1 + i * 128: 1 + (i + 1) * 128, :], inc="dl")
        r.dma("sync", ss0_t[:], s0_scr.ap()[rows, :], inc="dl")
        r.dma("sync", sy_t[:], y_scr.ap()[rows, :], inc="dl")
        DL = r.val("dl")
        r.wait("scalar", "dl", DL)
        r.op("scalar", lambda eng: eng.activation(
            out=sg0_t[:], in_=sy_t[:], func=AF.Silu), inc="ac")
        r.wait("vector", "ac")
        r.wait("vector", "dl", DL)
        r.op("vector", lambda eng: eng.tensor_tensor(
            out=sg0_t[:], in0=sg0_t[:], in1=ss0_t[:], op=ALU.mult))
        r.op("vector", lambda eng: eng.tensor_tensor(
            out=sx_t[:], in0=sx_t[:], in1=sg0_t[:], op=ALU.add), inc="ve")
        r.wait("sync", "ve")
        r.dma("sync", dx1.ap()[rows, :], sx_t[:], inc="st_sync")
        r.op("scalar", lambda eng: eng.activation(
            out=ssq32[:], in_=sx_t[:], func=AF.Square,
            accum_out=lstat[:, 1:2]), inc="ac")
        r.op("scalar", lambda eng: eng.activation(
            out=ssq32[:], in_=sx_t[:], func=AF.Identity,
            accum_out=lstat[:, 0:1]), inc="ac")
        r.wait("vector", "ac")
        r.op("vector", lambda eng: eng.tensor_scalar(
            out=lstat[:, 2:3], in0=lstat[:, 0:1], scalar1=1.0 / D,
            scalar2=None, op0=ALU.mult))
        r.op("vector", lambda eng: eng.tensor_scalar(
            out=lstat[:, 3:4], in0=lstat[:, 1:2], scalar1=1.0 / D,
            scalar2=LN_EPS, op0=ALU.mult, op1=ALU.add))
        r.op("vector", lambda eng: eng.tensor_scalar(
            out=lstat[:, 5:6], in0=lstat[:, 2:3], scalar1=lstat[:, 2:3],
            scalar2=None, op0=ALU.mult))
        r.op("vector", lambda eng: eng.tensor_tensor(
            out=lstat[:, 4:5], in0=lstat[:, 3:4], in1=lstat[:, 5:6],
            op=ALU.subtract), inc="ve")
        r.wait("scalar", "ve")
        r.op("scalar", lambda eng: eng.activation(
            out=lstat[:, 6:7], in_=lstat[:, 4:5], func=AF.Sqrt), inc="ac")
        r.wait("vector", "ac")
        r.op("vector", lambda eng: eng.reciprocal(
            out=lstat[:, 7:8], in_=lstat[:, 6:7]))
        r.op("vector", lambda eng: eng.tensor_scalar(
            out=sxn1_t[:], in0=sx_t[:], scalar1=lstat[:, 2:3],
            scalar2=lstat[:, 7:8], op0=ALU.subtract, op1=ALU.mult), inc="ve")
        r.wait("sync", "ve")
        r.dma("sync", dxn1.ap()[xn1_row0 + i * 128: xn1_row0 + (i + 1) * 128, :],
              sxn1_t[:], inc="st_sync")

    r.wait("sync", "st_sync", r.val("st_sync"))
    if F is None:
        r.build_block()
        r.close()
        return nc
    F.end_phase()




def build_l1(debug=False, F=None):
    if F is None:
        nc = bass.Bass("TRN2", target_bir_lowering=False)
        r = Rec(nc)
        din = lambda n, s, d: nc.dram_tensor(n, list(s), d, kind="ExternalInput")
    else:
        nc, r = F.nc, F.r
        F.begin_phase("D_")
        din = F.ext_in
    NTile = NT // 128
    NCT = NT // 64
    E1, F1, FSH = 64, 512, 512
    KS, PAD = 31, 30
    tg_sizes = [384] * 10 + [256]

    if F is None:
        dx1 = din("x1", [NT, D], F32)
        dxn1h = din("xn1h", [NT + PAD, D], BF16)
    else:
        dx1 = F.scr("x1", [NT, D], F32)
        dxn1h = F.scr("xn1h_scr", [NT + PAD, D], BF16)
    dwlin = din("w_lin", [496, D], F32)
    dgw = din("gate_w", [E1, D], F32)
    dgbias = din("gbias_tile", [128, E1], F32)
    dWeg = din("We_gate", [E1, D, F1], F32)
    dWeu = din("We_up", [E1, D, F1], F32)
    dWed = din("We_down", [E1, F1, D], F32)
    dWsg = din("Ws_gate", [D, FSH], F32)
    dWsu = din("Ws_up", [D, FSH], F32)
    dWsd = din("Ws_down", [FSH, D], F32)
    dident = din("identD", [128, 128], BF16)
    dbmask = din("bmask", [64, 16 * 94], BF16)
    dout = nc.dram_tensor("out", [NT, D], F32, kind="ExternalOutput")
    if debug:
        d_s1 = nc.dram_tensor("d_s1", [NT, D], F32, kind="ExternalOutput")
        d_wd = nc.dram_tensor("d_wd", [E1, NT], F32, kind="ExternalOutput")
        d_wsm = nc.dram_tensor("d_wsm", [NT, 496], F32, kind="ExternalOutput")

    mk_int = (lambda n, s, d: F.scr(n, s, d)) if F is not None else \
        (lambda n, s, d: nc.dram_tensor(n, list(s), d, kind="Internal"))
    w_scr = mk_int("w_scr", [NT, 496], F32)
    s1_scr = mk_int("s1_scr", [NT, D], F32)
    wdT_scr = mk_int("wdT_scr", [E1, NT], BF16)
    xn1fm_scr = mk_int("xn1fm_scr", [D, NT], BF16)
    wgbf = mk_int("wgbf", [E1, D, F1], BF16)
    wubf = mk_int("wubf", [E1, D, F1], BF16)
    wdbf = mk_int("wdbf", [E1, F1, D], BF16)

    if F is None:
        pA = r.ps("pA", [128, D], F32).ap()
        pB = r.ps("pB", [128, D], F32).ap()
        pC = r.ps("pC", [128, D], F32).ap()
        pD = r.ps("pD", [128, D], F32).ap()
        p_tr = pC.bitcast(BF16)[:, 0:128]
    else:
        pA, pB, pC, pD = F.pA, F.pB, F.pC, F.pD
        p_tr = F.pC_bf[:, 0:128]
    p_conv = pB[0:64, 0:64]

    sident = r.sb("sident", [128, 128], BF16)
    sbmask = r.sb("sbmask", [64, 16 * 94], BF16)
    sones = r.sb("sones", [1, 128], BF16)
    sxn_strip = r.sb("sxn_strip", [128, 8 * 128], BF16)
    sxn_tg = r.sb("sxn_tg", [128, 8 * 384], BF16)
    swlinT = r.sb("swlinT", [128, 8 * 496], BF16)
    sgwT = r.sb("sgwT", [128, 8 * E1], BF16)
    sgbias = r.sb("sgbias", [128, E1], F32)
    swltmp = r.sb("swltmp", [128, D], BF16)
    sxt = r.sb("sxt", [128, D], BF16)
    swsm = r.sb("swsm", [128, 496], F32)
    sstat = r.sb("sstat", [128, 48], F32)
    sscores = r.sb("sscores", [128, E1], F32)
    ssc = r.sb("ssc", [128, E1], F32)
    sgm = r.sb("sgm", [128, 8], F32)
    smask64 = r.sb("smask64", [128, E1], F32)
    ssel = r.sb("ssel", [128, E1], F32)
    spick = r.sb("spick", [128, E1], F32)
    swd = r.sb("swd", [128, E1], BF16)
    sband = r.sb("sband", [64, 16 * 94], BF16)
    sbandT = r.sb("sbandT", [94, 64], BF16)
    sxh = r.sb("sxh", [94, D], BF16)
    ss1t = r.sb("ss1t", [64, D], F32)
    swg = [r.sb(f"swg{p}", [128, 8 * F1], BF16) for p in range(2)]
    swu = [r.sb(f"swu{p}", [128, 8 * F1], BF16) for p in range(2)]
    swdn = [r.sb(f"swdn{p}", [128, 4 * D], BF16) for p in range(2)]
    swsg = r.sb("swsg", [128, 8 * FSH], BF16)
    swsu = r.sb("swsu", [128, 8 * FSH], BF16)
    swsd = r.sb("swsd", [128, 4 * D], BF16)
    swdTrow = r.sb("swdTrow", [1, E1 * 384], BF16)
    swdT64 = r.sb("swdT64", [64, 384], BF16)
    sbc = r.sb("sbc", [128, 384], BF16)
    sgl = r.sb("sgl", [128, 4 * 384], BF16)
    sh1 = r.sb("sh1", [128, 4 * 384], BF16)
    sg1 = r.sb("sg1", [128, 3 * D], BF16)
    sx1t = r.sb("sx1t", [128, D], F32)
    ss1c = r.sb("ss1c", [128, D], F32)

    r.dma("sync", sident[:], dident.ap(), inc="wld")
    r.dma("sync", sbmask[:], dbmask.ap(), inc="wld")
    r.dma("sync", sgbias[:], dgbias.ap(), inc="wld")
    r.op("gpsimd", lambda eng: eng.memset(sones[:], 1.0), inc="gp")
    for rt in range(4):
        rows = 128 if rt < 3 else 112
        r.wait("gpsimd", "pe")
        r.dma("gpsimd", swltmp[0:rows, :], dwlin.ap()[rt * 128: rt * 128 + rows, :],
              inc="cwl")
        CW = r.val("cwl")
        r.wait("tensor", "cwl", CW)
        for c in range(8):
            r.wait("tensor", "ve")
            r.op("tensor", lambda eng, rt=rt, c=c, rows=rows: eng.transpose(
                p_tr[0:128, 0:rows], swltmp[0:rows, c * 128:(c + 1) * 128],
                sident[0:rows, 0:rows]), inc="pe")
            r.wait("vector", "pe")
            r.op("vector", lambda eng, rt=rt, c=c, rows=rows: eng.tensor_copy(
                out=swlinT[:, c * 496 + rt * 128: c * 496 + rt * 128 + rows],
                in_=p_tr[0:128, 0:rows]), inc="ve", nodrain=True)
    r.wait("gpsimd", "pe")
    r.dma("gpsimd", swltmp[0:64, :], dgw.ap(), inc="cwl")
    r.wait("tensor", "cwl", r.val("cwl"))
    for c in range(8):
        r.wait("tensor", "ve")
        r.op("tensor", lambda eng, c=c: eng.transpose(
            p_tr[0:128, 0:64], swltmp[0:64, c * 128:(c + 1) * 128],
            sident[0:64, 0:64]), inc="pe")
        r.wait("vector", "pe")
        r.op("vector", lambda eng, c=c: eng.tensor_copy(
            out=sgwT[:, c * E1:(c + 1) * E1], in_=p_tr[0:128, 0:64]), inc="ve",
            nodrain=True)
    for d in range(8):
        r.dma("gpsimd", swsg[:, d * FSH:(d + 1) * FSH],
              bass.AP(dWsg, d * 128 * FSH, [[FSH, 128], [1, FSH]]), inc="cwl")
        r.dma("gpsimd", swsu[:, d * FSH:(d + 1) * FSH],
              bass.AP(dWsu, d * 128 * FSH, [[FSH, 128], [1, FSH]]), inc="cwl")
    for f in range(4):
        r.dma("gpsimd", swsd[:, f * D:(f + 1) * D],
              bass.AP(dWsd, f * 128 * D, [[D, 128], [1, D]]), inc="cwl")
    CWL = r.val("cwl")
    if DBG_WLINT:
        d_wlt = nc.dram_tensor("d_wlt", [128, 8 * 496], BF16, kind="ExternalOutput")
        d_gwt = nc.dram_tensor("d_gwt", [128, 8 * E1], BF16, kind="ExternalOutput")
        r.wait("gpsimd", "ve")
        r.dma("gpsimd", d_wlt.ap(), swlinT[:], inc="wltd")
        r.dma("gpsimd", d_gwt.ap(), sgwT[:], inc="wltd")
        r.wait("sync", "wltd", r.val("wltd"))
    r.dma("gpsimd", wgbf.ap(), dWeg.ap(), inc="cast")
    r.dma("gpsimd", wubf.ap(), dWeu.ap(), inc="cast")
    r.dma("gpsimd", wdbf.ap(), dWed.ap(), inc="cast")
    CAST = r.val("cast")
    WLD = r.val("wld")
    for e in ("tensor", "vector", "scalar"):
        r.wait(e, "wld", WLD)
        r.wait(e, "cwl", CWL)
        r.wait(e, "gp", r.val("gp"))

    seg31 = lambda t: t.rearrange("p (h k) -> p h k", h=16)
    seg8 = lambda t: t.rearrange("p (g k) -> p g k", g=8)

    for i in range(NTile):
        r.wait("sync", "ve")
        r.wait("sync", "xfs", r.val("xfs"))
        r.dma("sync", sxt[:], dxn1h.ap()[PAD + i * 128: PAD + (i + 1) * 128, :],
              inc="x1l")
        r.wait("tensor", "x1l", r.val("x1l"))
        for c in range(8):
            r.wait("tensor", "ve")
            r.op("tensor", lambda eng, c=c: eng.transpose(
                p_tr[:], sxt[:, c * 128:(c + 1) * 128], sident[:]), inc="pe")
            r.wait("vector", "pe")
            r.op("vector", lambda eng, c=c: eng.tensor_copy(
                out=sxn_strip[:, c * 128:(c + 1) * 128],
                in_=p_tr[:]), inc="ve", nodrain=True)
        r.wait("sync", "ve")
        r.dma("sync", bass.AP(xn1fm_scr, i * 128, [[NT, 128], [128 * NT, 8], [1, 128]]),
              sxn_strip[:].rearrange("p (c j) -> p c j", c=8), inc="xfs")
    XFS = r.val("xfs")

    for i in range(NTile):
        cols = slice(i * 128, (i + 1) * 128)
        r.wait("sync", "xfs", XFS)
        r.wait("sync", "pe")
        r.dma("sync", sxn_strip[:].rearrange("p (c j) -> p c j", c=8),
              bass.AP(xn1fm_scr, i * 128, [[NT, 128], [128 * NT, 8], [1, 128]]),
              inc="xtl")
        r.wait("tensor", "xtl", r.val("xtl"))
        r.wait("tensor", "ve")
        r.wait("tensor", "ac")
        for d in range(8):
            r.op("tensor", lambda eng, d=d: eng.matmul(
                pA[:, 0:496], sxn_strip[:, d * 128:(d + 1) * 128],
                swlinT[:, d * 496:(d + 1) * 496], start=(d == 0), stop=(d == 7)),
                inc=("pe" if d == 7 else None))
        r.wait("vector", "pe")
        r.wait("vector", "st_sync", r.val("st_sync"))
        r.op("vector", lambda eng: eng.tensor_reduce(
            out=sstat[:, 0:16], in_=seg31(pA[:, 0:496]), axis=AX.X, op=ALU.max))
        r.op("vector", lambda eng: eng.tensor_tensor(
            out=seg31(swsm[:]), in0=seg31(pA[:, 0:496]),
            in1=sstat[:, 0:16].unsqueeze(2).broadcast_to([128, 16, 31]),
            op=ALU.subtract), inc="ve")
        r.wait("scalar", "ve")
        r.op("scalar", lambda eng: eng.activation(
            out=swsm[:], in_=swsm[:], func=AF.Exp), inc="ac")
        r.wait("vector", "ac")
        r.op("vector", lambda eng: eng.tensor_reduce(
            out=sstat[:, 16:32], in_=seg31(swsm[:]), axis=AX.X, op=ALU.add))
        r.op("vector", lambda eng: eng.reciprocal(
            out=sstat[:, 32:48], in_=sstat[:, 16:32]))
        r.op("vector", lambda eng: eng.tensor_tensor(
            out=seg31(swsm[:]), in0=seg31(swsm[:]),
            in1=sstat[:, 32:48].unsqueeze(2).broadcast_to([128, 16, 31]),
            op=ALU.mult), inc="ve")
        r.wait("sync", "ve")
        r.dma("sync", w_scr.ap()[cols, :], swsm[:], inc="st_sync")

        r.wait("tensor", "ve")
        for d in range(8):
            r.op("tensor", lambda eng, d=d: eng.matmul(
                pA[:, 0:64], sxn_strip[:, d * 128:(d + 1) * 128],
                sgwT[:, d * E1:(d + 1) * E1], start=(d == 0), stop=(d == 7)),
                inc=("pe" if d == 7 else None))
        r.wait("scalar", "pe")
        r.op("scalar", lambda eng: eng.activation(
            out=sscores[:], in_=pA[:, 0:64], func=AF.Sigmoid), inc="ac")
        r.wait("vector", "ac")
        r.op("vector", lambda eng: eng.tensor_tensor(
            out=ssc[:], in0=sscores[:], in1=sgbias[:], op=ALU.add), inc="ve")
        r.op("vector", lambda eng: eng.tensor_reduce(
            out=sstat[:, 0:8], in_=seg8(ssc[:]), axis=AX.X, op=ALU.max))
        r.op("vector", lambda eng: eng.tensor_tensor(
            out=seg8(smask64[:]), in0=seg8(ssc[:]),
            in1=sstat[:, 0:8].unsqueeze(2).broadcast_to([128, 8, 8]),
            op=ALU.is_ge))
        r.op("vector", lambda eng: eng.scalar_tensor_tensor(
            out=smask64[:], in0=smask64[:], scalar=-1e30, in1=ssc[:],
            op0=ALU.mult, op1=ALU.add))
        r.op("vector", lambda eng: eng.tensor_reduce(
            out=sstat[:, 8:16], in_=seg8(smask64[:]), axis=AX.X, op=ALU.max))
        r.op("vector", lambda eng: eng.tensor_tensor(
            out=sgm[:], in0=sstat[:, 0:8], in1=sstat[:, 8:16], op=ALU.add),
            inc="ve")
        r.op("vector", lambda eng: eng.memset(sstat[:, 16:24], 0.0))
        for it in range(4):
            r.op("vector", lambda eng: eng.tensor_reduce(
                out=sstat[:, 24:25], in_=sgm[:], axis=AX.X, op=ALU.max))
            r.op("vector", lambda eng: eng.tensor_scalar(
                out=sstat[:, 32:40], in0=sgm[:], scalar1=sstat[:, 24:25],
                scalar2=None, op0=ALU.is_ge))
            r.op("vector", lambda eng: eng.tensor_tensor(
                out=sstat[:, 16:24], in0=sstat[:, 16:24], in1=sstat[:, 32:40],
                op=ALU.max))
            r.op("vector", lambda eng: eng.scalar_tensor_tensor(
                out=sgm[:], in0=sstat[:, 32:40], scalar=-1e30, in1=sgm[:],
                op0=ALU.mult, op1=ALU.add))
        r.op("vector", lambda eng: eng.tensor_scalar(
            out=sstat[:, 32:40], in0=sstat[:, 16:24], scalar1=1e30,
            scalar2=-1e30, op0=ALU.mult, op1=ALU.add))
        r.op("vector", lambda eng: eng.tensor_tensor(
            out=seg8(smask64[:]), in0=seg8(ssc[:]),
            in1=sstat[:, 32:40].unsqueeze(2).broadcast_to([128, 8, 8]),
            op=ALU.add))
        r.op("vector", lambda eng: eng.memset(ssel[:], 0.0))
        for it in range(8):
            r.op("vector", lambda eng: eng.tensor_reduce(
                out=sstat[:, 24:25], in_=smask64[:], axis=AX.X, op=ALU.max))
            r.op("vector", lambda eng: eng.tensor_scalar(
                out=spick[:], in0=smask64[:], scalar1=sstat[:, 24:25],
                scalar2=None, op0=ALU.is_ge))
            r.op("vector", lambda eng: eng.tensor_tensor(
                out=ssel[:], in0=ssel[:], in1=spick[:], op=ALU.max))
            r.op("vector", lambda eng: eng.scalar_tensor_tensor(
                out=smask64[:], in0=spick[:], scalar=-1e30, in1=smask64[:],
                op0=ALU.mult, op1=ALU.add))
        r.op("vector", lambda eng: eng.tensor_tensor(
            out=ssel[:], in0=ssel[:], in1=sscores[:], op=ALU.mult))
        r.op("vector", lambda eng: eng.tensor_reduce(
            out=sstat[:, 24:25], in_=ssel[:], axis=AX.X, op=ALU.add))
        r.op("vector", lambda eng: eng.tensor_scalar(
            out=sstat[:, 25:26], in0=sstat[:, 24:25], scalar1=1e-20, scalar2=None,
            op0=ALU.add))
        r.op("vector", lambda eng: eng.reciprocal(
            out=sstat[:, 26:27], in_=sstat[:, 25:26]))
        r.op("vector", lambda eng: eng.tensor_scalar(
            out=swd[:], in0=ssel[:], scalar1=sstat[:, 26:27], scalar2=2.5,
            op0=ALU.mult, op1=ALU.mult), inc="ve")
        r.wait("tensor", "ve")
        r.op("tensor", lambda eng: eng.transpose(
            p_tr[0:64, :], swd[:], sident[:]), inc="pe")
        r.wait("vector", "pe")
        r.op("vector", lambda eng: eng.tensor_copy(
            out=swdT64[0:64, 0:128], in_=p_tr[0:64, 0:128]), inc="wdv",
            nodrain=True)
        r.wait("sync", "wdv", r.val("wdv"))
        r.dma("sync", bass.AP(wdT_scr, i * 128, [[NT, 64], [1, 128]]),
              swdT64[0:64, 0:128], inc="gts")

    r.wait("gpsimd", "gts", r.val("gts"))
    r.wait("gpsimd", "st_sync", r.val("st_sync"))
    for i in range(NCT):
        r.wait("gpsimd", "ve")
        r.wait("gpsimd", "pe")
        src = bass.AP(w_scr, i * 64 * 496, [[495, 64], [31, 16], [1, 94]])
        r.dma("gpsimd", sband[:].rearrange("p (h s) -> p h s", h=16), src, inc="bl")
        r.wait("sync", "ve")
        r.wait("sync", "pe")
        r.dma("sync", sxh[:], dxn1h.ap()[i * 64: i * 64 + 94, :], inc="bl")
        BL = r.val("bl")
        r.wait("vector", "bl", BL)
        r.op("vector", lambda eng: eng.tensor_tensor(
            out=sband[:], in0=sband[:], in1=sbmask[:], op=ALU.mult), inc="ve")
        for h in range(16):
            r.wait("tensor", "ve")
            r.wait("tensor", "ac")
            r.op("tensor", lambda eng, h=h: eng.transpose(
                p_tr[0:94, 0:64], sband[:, h * 94:(h + 1) * 94],
                sident[0:64, 0:64]), inc="pe")
            r.wait("vector", "pe")
            r.op("vector", lambda eng: eng.tensor_copy(
                out=sbandT[:], in_=p_tr[0:94, 0:64]), inc="ve", nodrain=True)
            r.wait("tensor", "ve")
            r.op("tensor", lambda eng, h=h: eng.matmul(
                p_conv, sbandT[:], sxh[:, h * 64:(h + 1) * 64],
                start=True, stop=True), inc="pe")
            r.wait("scalar", "pe")
            r.wait("scalar", "st_sync", r.val("st_sync"))
            r.op("scalar", lambda eng, h=h: eng.activation(
                out=ss1t[0:64, h * 64:(h + 1) * 64], in_=p_conv, func=AF.Identity),
                inc="ac", nodrain=True)
        r.wait("sync", "ac")
        r.dma("sync", s1_scr.ap()[i * 64:(i + 1) * 64, :], ss1t[0:64, :],
              inc="st_sync")

    r.wait("tensor", "cast", CAST)
    t0c = 0
    for tgi, TG in enumerate(tg_sizes):
        ntile = TG // 128
        r.wait("sync", "gts", r.val("gts"))
        r.wait("sync", "pe")
        r.dma("sync", swdT64[0:64, 0:TG], wdT_scr.ap()[:, t0c:t0c + TG], inc="wl")
        r.wait("gpsimd", "wl", r.val("wl"))
        r.dma("gpsimd", bass.AP(swdTrow, 0, [[TG * 64, 1], [TG, 64], [1, TG]]),
              swdT64[0:64, 0:TG], inc="wl")
        r.wait("tensor", "wl", r.val("wl"))
        r.wait("sync", "pe")
        r.dma("sync", sxn_tg[:].rearrange("p (c j) -> p c j", c=8)[:, :, 0:TG],
              bass.AP(xn1fm_scr, t0c, [[NT, 128], [128 * NT, 8], [1, TG]]),
              inc="xtl")
        r.wait("tensor", "xtl", r.val("xtl"))
        first_y = True
        for e in range(E1 + 1):
            par = e % 2
            shared = (e == E1)
            if not shared:
                r.wait("sync", "pe")
                r.dma("sync", swg[par][:].rearrange("p (d j) -> p d j", d=8),
                      bass.AP(wgbf, e * D * F1, [[F1, 128], [128 * F1, 8], [1, F1]]),
                      inc="wl")
                r.dma("sync", swu[par][:].rearrange("p (d j) -> p d j", d=8),
                      bass.AP(wubf, e * D * F1, [[F1, 128], [128 * F1, 8], [1, F1]]),
                      inc="wl")
                r.dma("sync", swdn[par][:].rearrange("p (f j) -> p f j", f=4),
                      bass.AP(wdbf, e * F1 * D, [[D, 128], [128 * D, 4], [1, D]]),
                      inc="wl")
                wgt, wut, wdt = swg[par], swu[par], swdn[par]
                r.wait("tensor", "wl", r.val("wl"))
                r.wait("tensor", "ve")
                r.op("tensor", lambda eng, e=e, TG=TG: eng.matmul(
                    pD[:, 0:TG], sones[0:1, :], swdTrow[0:1, e * TG:(e + 1) * TG],
                    start=True, stop=True), inc="pe")
                r.wait("vector", "pe")
                r.wait("vector", "gpv")
                r.op("vector", lambda eng, TG=TG: eng.tensor_copy(
                    out=sbc[:, 0:TG], in_=pD[:, 0:TG]), inc="ve", nodrain=True)
            else:
                wgt, wut, wdt = swsg, swsu, swsd
            for f in range(4):
                r.wait("tensor", "ve")
                r.wait("tensor", "gpv")
                for d in range(8):
                    r.op("tensor", lambda eng, f=f, d=d, wgt=wgt, t0c=t0c, TG=TG:
                         eng.matmul(
                             pD[:, 0:TG], wgt[:, d * F1 + f * 128: d * F1 + (f + 1) * 128],
                             sxn_tg[:, d * 384: d * 384 + TG],
                             start=(d == 0), stop=(d == 7)),
                         inc=("pe" if d == 7 else None))
                r.wait("scalar", "pe")
                r.op("scalar", lambda eng, f=f, TG=TG: eng.activation(
                    out=sgl[:, f * 384: f * 384 + TG], in_=pD[:, 0:TG],
                    func=AF.Silu), inc="ac", nodrain=True)
                r.wait("tensor", "ac")
                for d in range(8):
                    r.op("tensor", lambda eng, f=f, d=d, wut=wut, t0c=t0c, TG=TG:
                         eng.matmul(
                             pD[:, 0:TG], wut[:, d * F1 + f * 128: d * F1 + (f + 1) * 128],
                             sxn_tg[:, d * 384: d * 384 + TG],
                             start=(d == 0), stop=(d == 7)),
                         inc=("pe" if d == 7 else None))
                r.wait("vector", "pe")
                r.wait("vector", "ac")
                r.op("vector", lambda eng, f=f, TG=TG: eng.tensor_tensor(
                    out=sh1[:, f * 384: f * 384 + TG], in0=pD[:, 0:TG],
                    in1=sgl[:, f * 384: f * 384 + TG], op=ALU.mult), inc="ve",
                    nodrain=True)
                if not shared:
                    r.wait("gpsimd", "ve")
                    r.op("gpsimd", lambda eng, f=f, TG=TG: eng.tensor_tensor(
                        out=sh1[:, f * 384: f * 384 + TG],
                        in0=sh1[:, f * 384: f * 384 + TG],
                        in1=sbc[:, 0:TG], op=ALU.mult), inc="gpv")
            if not shared:
                r.wait("tensor", "gpv")
            else:
                r.wait("tensor", "ve")
            r.wait("tensor", "ac")
            pys = [pA, pB, pC]
            for ti in range(ntile):
                for f in range(4):
                    for hf in range(2):
                        r.op("tensor", lambda eng, ti=ti, f=f, hf=hf, wdt=wdt,
                             fy=first_y: eng.matmul(
                                 pys[ti][:, hf * 512:(hf + 1) * 512],
                                 sh1[:, f * 384 + ti * 128: f * 384 + (ti + 1) * 128],
                                 wdt[:, f * D + hf * 512: f * D + (hf + 1) * 512],
                                 start=(fy and f == 0), stop=(e == E1 and f == 3)),
                             inc=("pe" if (e == E1 and f == 3 and hf == 1) else None))
            first_y = False
        r.wait("scalar", "pe")
        r.wait("scalar", "ve")
        for ti in range(ntile):
            r.op("scalar", lambda eng, ti=ti: eng.activation(
                out=sg1[:, ti * D:(ti + 1) * D], in_=pys[ti][:], func=AF.Silu),
                inc="ac", nodrain=True)
        for ti in range(ntile):
            rows = slice(t0c + ti * 128, t0c + (ti + 1) * 128)
            r.wait("sync", "ve")
            r.dma("sync", sx1t[:], dx1.ap()[rows, :], inc="cl")
            r.dma("sync", ss1c[:], s1_scr.ap()[rows, :], inc="cl")
            CL = r.val("cl")
            r.wait("vector", "cl", CL)
            r.wait("vector", "ac")
            r.op("vector", lambda eng, ti=ti: eng.tensor_tensor(
                out=ss1c[:], in0=ss1c[:], in1=sg1[:, ti * D:(ti + 1) * D],
                op=ALU.mult))
            r.op("vector", lambda eng: eng.tensor_tensor(
                out=sx1t[:], in0=sx1t[:], in1=ss1c[:], op=ALU.add), inc="ve")
            r.wait("sync", "ve")
            r.dma("sync", dout.ap()[rows, :], sx1t[:], inc="st_sync")
        t0c += TG

    if debug:
        r.wait("gpsimd", "st_sync", r.val("st_sync"))
        r.dma("gpsimd", d_s1.ap(), s1_scr.ap(), inc="dbg")
        r.dma("gpsimd", d_wd.ap(), wdT_scr.ap(), inc="dbg")
        r.dma("gpsimd", d_wsm.ap(), w_scr.ap(), inc="dbg")
        r.wait("sync", "dbg", r.val("dbg"))
    r.wait("sync", "st_sync", r.val("st_sync"))
    if F is None:
        r.build_block()
        r.close()
        return nc
    F.end_phase()





def emit_scan_fused(F):
    nc, r = F.nc, F.r
    F.begin_phase("B_")
    C = 128
    NCH = NT // C
    NS2 = 16
    PCH = 2
    NG = NCH // PCH

    d_wT = F.scratch["wsig_tm"]
    d_kkT = F.scratch["kk_tm"]
    d_aT = F.scratch["a_tm"]
    d_k2T = F.scratch["k2_tm"]
    d_vT = F.scratch["v_tm"]
    d_rF = F.scratch["r_fm"]
    d_kkF = F.scratch["kk_fm"]
    d_aF = F.scratch["a_fm"]
    d_k2F = F.scratch["k2_fm"]
    d_o = F.scr("o", [NT, D], F32)
    rhat_scr = F.scr("rhat_scr", [NCH, 64, 2048], BF16)
    op_scr = F.scr("op_scr", [NCH, 128, 1024], BF16)
    iph_scr = F.scr("iph_scr", [NCH, 64, 1024], BF16)
    dhat_scr = F.scr("dhat_scr", [NCH, 64, 1024], BF16)
    lamc_scr = F.scr("lamc_scr", [NCH, 128, 8], F32)
    cc1_in = F.scr("cc1_in", [1024, 64], F32)
    cc1_out = F.scr("cc1_out", [2048, 64], F32)
    dident = F.ext_in("identD", [128, 128], BF16)
    dut_i = F.ext_in("utinclD", [128, 128], BF16)
    dut_s = F.ext_in("utstrictD", [128, 128], BF16)
    dmsl = F.ext_in("mslD", [128, 128], F32)
    dmsu = F.ext_in("msuD", [128, 128], F32)
    dmtu = F.ext_in("mtuD", [128, 128], F32)
    disodd = F.ext_in("is_odd", [64, 1], F32)

    p_cumA = F.pA[:, 0:512]
    p_cumB = F.pA[:, 512:1024]
    p_cum_all = F.pA
    p_a = F.pB[:, 0:128]
    p_b = F.pB[:, 512:640]
    p_sm = F.pC[:, 0:64]
    p_rh = F.pC[0:64, 512:640]
    p_h = F.pD[0:64, 0:64]
    p_o = F.pD[:, 512:576]
    p_st = F.pD[0:64, 0:64]

    sident = r.sb("sident", [128, 128], BF16)
    sut_i = r.sb("sut_i", [128, 128], BF16)
    sut_s = r.sb("sut_s", [128, 128], BF16)
    smsl = r.sb("smsl", [128, 128], F32)
    smsu = r.sb("smsu", [128, 128], F32)
    smtu = r.sb("smtu", [128, 128], F32)
    sisodd = r.sb("sisodd", [64, 1], F32)
    for dst, srct in ((sident, dident), (sut_i, dut_i), (sut_s, dut_s),
                      (smsl, dmsl), (smsu, dmsu), (smtu, dmtu), (sisodd, disodd)):
        r.dma("sync", dst[:], srct.ap(), inc="cld")
    CLD = r.val("cld")
    for e in ("tensor", "vector", "scalar"):
        r.wait(e, "cld", CLD)

    in_tm = {n: [r.sb(f"i_{n}{p}", [128, PCH * 1024], BF16) for p in range(2)]
             for n in ("wT", "kkT", "aT", "k2T", "vT")}
    in_fm = {n: [r.sb(f"i_{n}{p}", [128, 8 * PCH * 128], BF16) for p in range(2)]
             for n in ("rF", "kkF", "aF", "k2F")}
    slogL = r.sb("slogL", [128, 1024], F32)
    slogLp = r.sb("slogLp", [128, 1024], F32)
    sLamI_tm = r.sb("sLamI_tm", [128, 1024], BF16)
    sLamP_tm = r.sb("sLamP_tm", [128, 1024], BF16)
    sAt_tm = r.sb("sAt_tm", [128, 1024], BF16)
    sBt_tm = r.sb("sBt_tm", [128, 1024], BF16)
    sKt_tm = r.sb("sKt_tm", [128, 1024], BF16)
    slogL_fm = [r.sb(f"slogL_fm{j}", [128, 128], F32) for j in range(8)]
    sLam_fm = [r.sb(f"sLam_fm{j}", [128, 129], BF16) for j in range(8)]
    sLamI_fm = [r.sb(f"sLamI_fm{j}", [128, 128], BF16) for j in range(8)]
    sAt_fm = [r.sb(f"sAt_fm{j}", [128, 128], BF16) for j in range(8)]
    sBt_fm = [r.sb(f"sBt_fm{j}", [128, 128], BF16) for j in range(8)]
    sKt_fm = [r.sb(f"sKt_fm{j}", [128, 128], BF16) for j in range(8)]
    sRt_fm = [r.sb(f"sRt_fm{j}", [128, 128], BF16) for j in range(8)]
    sM = r.sb("sM", [128, 128], BF16)
    sMT = r.sb("sMT", [128, 128], BF16)
    sNT = r.sb("sNT", [128, 128], BF16)
    sTLRBt = r.sb("sTLRBt", [128, 128], BF16)
    sTLRKt = r.sb("sTLRKt", [128, 128], BF16)
    sMk = [r.sb(f"sMk{i}", [128, 128], BF16) for i in range(2)]
    sMkT = [r.sb(f"sMkT{i}", [128, 128], BF16) for i in range(2)]
    sWT = [r.sb(f"sWT{i}", [128, 128], BF16) for i in range(2)]
    sNV = r.sb("sNV", [128, 64], BF16)
    sUp = r.sb("sUp", [128, 64], BF16)
    sWAs = r.sb("sWAs", [128, 64], BF16)
    srhat = r.sb("srhat", [64, 2048], BF16)
    sop = r.sb("sop", [128, 1024], BF16)
    siph = r.sb("siph", [64, 1024], BF16)
    sdhat = r.sb("sdhat", [64, 1024], BF16)
    slamc = r.sb("slamc", [128, 8], F32)
    sS = [r.sb(f"sS{s}", [64, 64], BF16) for s in range(NS2)]
    sdsf = r.sb("sdsf", [64, 1024], F32)
    piph = r.sb("piph", [64, 1024], BF16)
    pdhat = r.sb("pdhat", [64, 1024], BF16)
    plamc = r.sb("plamc", [128, 8], F32)
    prhat = r.sb("prhat", [64, 2048], BF16)
    pop = r.sb("pop", [128, 1024], BF16)
    so_t = r.sb("so_t", [128, 1024], F32)

    def load_group(g, buf):
        c0 = g * PCH
        for cc in range(PCH):
            ch = c0 + cc
            for n, dt_ in (("wT", d_wT), ("kkT", d_kkT), ("aT", d_aT),
                           ("k2T", d_k2T), ("vT", d_vT)):
                r.dma("sync", in_tm[n][buf][:, cc * 1024:(cc + 1) * 1024],
                      dt_.ap()[ch * 128:(ch + 1) * 128, :], inc="ld")
        for n, df in (("rF", d_rF), ("kkF", d_kkF), ("aF", d_aF), ("k2F", d_k2F)):
            for j in range(8):
                r.dma("sync", in_fm[n][buf][:, j * PCH * 128:(j + 1) * PCH * 128],
                      df.ap()[j * 128:(j + 1) * 128, c0 * 128:(c0 + PCH) * 128],
                      inc="ld")
    ld_per_group = (5 * PCH + 32) * 16

    grp_done = {}
    load_group(0, 0)

    for g in range(NG):
        buf = g % 2
        if g + 1 < NG:
            if g >= 1:
                for semn in ("pe", "ve"):
                    r.wait("sync", semn, grp_done[g - 1][semn])
            load_group(g + 1, 1 - buf)
        for e in ("tensor", "vector", "scalar"):
            r.wait(e, "ld", (g + 1) * ld_per_group)

        for cc in range(PCH):
            ch = g * PCH + cc
            wT_all = in_tm["wT"][buf][:, cc * 1024:(cc + 1) * 1024]
            kkT_all = in_tm["kkT"][buf][:, cc * 1024:(cc + 1) * 1024]
            aT_all = in_tm["aT"][buf][:, cc * 1024:(cc + 1) * 1024]
            k2T_all = in_tm["k2T"][buf][:, cc * 1024:(cc + 1) * 1024]
            vT_all = in_tm["vT"][buf][:, cc * 1024:(cc + 1) * 1024]

            r.wait("vector", "sws", r.val("sws"))
            r.wait("scalar", "sws", r.val("sws"))
            r.wait("tensor", "ve")
            for hf in range(2):
                r.op("tensor", lambda eng, hf=hf, w=wT_all: eng.matmul(
                    F.pA[:, hf * 512:(hf + 1) * 512], sut_i[:],
                    w[:, hf * 512:(hf + 1) * 512], start=True, stop=True),
                    inc=("pe" if hf == 1 else None))
            r.wait("vector", "pe")
            r.wait("vector", "ac")
            r.op("vector", lambda eng: eng.tensor_scalar(
                out=slogL[:], in0=p_cum_all[:], scalar1=DECAY, scalar2=None,
                op0=ALU.mult), inc="ve", nodrain=True)
            r.wait("tensor", "ve")
            for hf in range(2):
                r.op("tensor", lambda eng, hf=hf, w=wT_all: eng.matmul(
                    F.pA[:, hf * 512:(hf + 1) * 512], sut_s[:],
                    w[:, hf * 512:(hf + 1) * 512], start=True, stop=True),
                    inc=("pe" if hf == 1 else None))
            r.wait("vector", "pe")
            r.op("vector", lambda eng: eng.tensor_scalar(
                out=slogLp[:], in0=p_cum_all[:], scalar1=DECAY, scalar2=None,
                op0=ALU.mult), inc="ve", nodrain=True)
            r.wait("scalar", "ve")
            r.op("scalar", lambda eng: eng.activation(
                out=sLamI_tm[:], in_=slogL[:], func=AF.Exp, scale=-1.0), inc="ac",
                nodrain=True)
            r.op("scalar", lambda eng: eng.activation(
                out=sLamP_tm[:], in_=slogLp[:], func=AF.Exp), inc="ac", nodrain=True)
            r.wait("vector", "ac")
            r.op("vector", lambda eng, kk=kkT_all: eng.scalar_tensor_tensor(
                out=sAt_tm[:], in0=kk, scalar=-1.0, in1=sLamP_tm[:],
                op0=ALU.mult, op1=ALU.mult), inc="ve", nodrain=True)
            r.op("vector", lambda eng, kk=kkT_all, aa=aT_all: eng.tensor_tensor(
                out=sBt_tm[:], in0=kk, in1=aa, op=ALU.mult), inc="ve")
            r.op("vector", lambda eng: eng.tensor_tensor(
                out=sBt_tm[:], in0=sBt_tm[:], in1=sLamI_tm[:], op=ALU.mult),
                inc="ve")
            r.op("vector", lambda eng, k2=k2T_all: eng.tensor_tensor(
                out=sKt_tm[:], in0=k2, in1=sLamI_tm[:], op=ALU.mult), inc="ve",
                nodrain=True)

            for j in range(8):
                wT_j = wT_all[:, j * 128:(j + 1) * 128]
                kkF_j = in_fm["kkF"][buf][:, (j * PCH + cc) * 128:(j * PCH + cc + 1) * 128]
                aF_j = in_fm["aF"][buf][:, (j * PCH + cc) * 128:(j * PCH + cc + 1) * 128]
                k2F_j = in_fm["k2F"][buf][:, (j * PCH + cc) * 128:(j * PCH + cc + 1) * 128]
                rF_j = in_fm["rF"][buf][:, (j * PCH + cc) * 128:(j * PCH + cc + 1) * 128]
                r.wait("tensor", "ve")
                r.op("tensor", lambda eng, w=wT_j: eng.matmul(
                    p_a, w, sut_i[:], start=True, stop=True), inc="pe")
                r.wait("vector", "pe")
                r.wait("vector", "ac")
                r.op("vector", lambda eng, j=j: eng.tensor_scalar(
                    out=slogL_fm[j][:], in0=p_a, scalar1=DECAY, scalar2=None,
                    op0=ALU.mult), inc="ve")
                r.wait("scalar", "ve")
                r.op("scalar", lambda eng, j=j: eng.activation(
                    out=sLam_fm[j][:, 1:129], in_=slogL_fm[j][:], func=AF.Exp),
                    inc="ac", nodrain=True)
                r.op("scalar", lambda eng, j=j: eng.activation(
                    out=sLamI_fm[j][:], in_=slogL_fm[j][:], func=AF.Exp,
                    scale=-1.0), inc="ac", nodrain=True)
                r.op("scalar", lambda eng, j=j: eng.activation(
                    out=slamc[:, j:j + 1], in_=slogL_fm[j][:, 127:128],
                    func=AF.Exp), inc="ac", nodrain=True)
                r.wait("vector", "ac")
                r.op("vector", lambda eng, j=j, kk=kkF_j: eng.scalar_tensor_tensor(
                    out=sAt_fm[j][:], in0=kk, scalar=-1.0,
                    in1=sLam_fm[j][:, 0:128], op0=ALU.mult, op1=ALU.mult),
                    inc="ve", nodrain=True)
                r.op("vector", lambda eng, j=j, kk=kkF_j, aa=aF_j: eng.tensor_tensor(
                    out=sBt_fm[j][:], in0=kk, in1=aa, op=ALU.mult), inc="ve")
                r.op("vector", lambda eng, j=j: eng.tensor_tensor(
                    out=sBt_fm[j][:], in0=sBt_fm[j][:], in1=sLamI_fm[j][:],
                    op=ALU.mult), inc="ve")
                r.op("vector", lambda eng, j=j, k2=k2F_j: eng.tensor_tensor(
                    out=sKt_fm[j][:], in0=k2, in1=sLamI_fm[j][:], op=ALU.mult),
                    inc="ve", nodrain=True)
                r.op("vector", lambda eng, j=j, rr=rF_j: eng.tensor_tensor(
                    out=sRt_fm[j][:], in0=rr, in1=sLam_fm[j][:, 1:129],
                    op=ALU.mult), inc="ve", nodrain=True)


            for s in range(NS2):
                j = s // 2
                lo = (s % 2) * 64
                A_f = sAt_fm[j][lo:lo + 64, :]
                B_f = sBt_fm[j][lo:lo + 64, :]
                K_f = sKt_fm[j][lo:lo + 64, :]
                R_f = sRt_fm[j][lo:lo + 64, :]
                A_t = sAt_tm[:, s * 64:(s + 1) * 64]
                B_t = sBt_tm[:, s * 64:(s + 1) * 64]
                K_t = sKt_tm[:, s * 64:(s + 1) * 64]
                V_t = vT_all[:, s * 64:(s + 1) * 64]

                def mm(pp, lhsT, rhs, start=True, stop=True, inc="pe"):
                    r.op("tensor", lambda eng, pp=pp, l=lhsT, rr=rhs, st=start,
                         sp=stop: eng.matmul(pp, l, rr, start=st, stop=sp), inc=inc)

                def evac_v(dst, src_p, mask=None, add=None, inc="ve"):
                    r.wait("vector", "pe")
                    if mask is not None:
                        r.op("vector", lambda eng, d=dst, sp=src_p, m=mask:
                             eng.tensor_tensor(out=d, in0=sp, in1=m, op=ALU.mult),
                             inc=inc, nodrain=True)
                    elif add is not None:
                        r.op("vector", lambda eng, d=dst, sp=src_p, a=add:
                             eng.tensor_tensor(out=d, in0=sp, in1=a, op=ALU.add),
                             inc=inc, nodrain=True)
                    else:
                        r.op("vector", lambda eng, d=dst, sp=src_p:
                             eng.tensor_copy(out=d, in_=sp), inc=inc, nodrain=True)

                def evac_s(dst, src_p, inc="ac"):
                    r.wait("scalar", "pe")
                    r.op("scalar", lambda eng, d=dst, sp=src_p: eng.activation(
                        out=d, in_=sp, func=AF.Identity), inc=inc, nodrain=True)

                r.wait("tensor", "ve")
                r.wait("tensor", "ac")
                mm(p_a, A_f, B_f)
                evac_v(sM[:], p_a, mask=smsl[:])
                mm(p_b, B_f, A_f)
                evac_v(sMT[:], p_b, mask=smsu[:])
                r.wait("tensor", "ve")
                mm(p_a, K_f, A_f)
                evac_v(sNT[:], p_a, mask=smsu[:])
                mm(p_b, B_f, R_f)
                evac_v(sTLRBt[:], p_b, mask=smtu[:])
                r.wait("tensor", "ve")
                mm(p_a, K_f, R_f)
                evac_v(sTLRKt[:], p_a, mask=smtu[:])
                r.wait("vector", "ve")
                r.op("vector", lambda eng: eng.tensor_tensor(
                    out=sWT[0][:], in0=sMT[:], in1=sident[:], op=ALU.add), inc="ve")
                r.op("vector", lambda eng: eng.tensor_copy(out=sMk[0][:], in_=sM[:]),
                     inc="ve", nodrain=True)
                r.op("vector", lambda eng: eng.tensor_copy(out=sMkT[0][:], in_=sMT[:]),
                     inc="ve", nodrain=True)
                wt_cur = 0
                mk_cur = 0
                for lvl in range(6):
                    r.wait("tensor", "ve")
                    r.wait("tensor", "ac")
                    mm(p_a, sMkT[mk_cur][:], sMk[mk_cur][:])
                    evac_s(sMk[1 - mk_cur][:], p_a)
                    if lvl < 5:
                        mm(p_b, sMk[mk_cur][:], sMkT[mk_cur][:])
                        evac_s(sMkT[1 - mk_cur][:], p_b)
                    mk_cur = 1 - mk_cur
                    r.wait("tensor", "ac")
                    mm(p_a, sMk[mk_cur][:], sWT[wt_cur][:])
                    evac_v(sWT[1 - wt_cur][:], p_a, add=sWT[wt_cur][:])
                    wt_cur = 1 - wt_cur
                WT = sWT[wt_cur]
                r.wait("tensor", "ve")
                mm(p_sm, sNT[:], V_t)
                evac_v(sNV[:, 0:64], p_sm)
                r.wait("tensor", "ve")
                mm(p_sm, WT[:], sNV[:, 0:64])
                evac_v(sUp[:, 0:64], p_sm)
                mm(p_o, WT[:], A_t)
                evac_v(sWAs[:, 0:64], p_o)
                r.wait("tensor", "ve")
                r.wait("tensor", "sst")
                mm(p_rh, sWAs[:, 0:64], sTLRBt[:])
                evac_v(srhat[0:64, s * 128:(s + 1) * 128], p_rh, add=R_f,
                       inc="sst")
                mm(p_h, sWAs[:, 0:64], B_t)
                evac_v(siph[0:64, s * 64:(s + 1) * 64], p_h,
                       add=sident[0:64, 0:64], inc="sst")
                r.wait("tensor", "ve")
                r.wait("tensor", "sst")
                mm(p_h, B_t, sUp[:, 0:64], start=True, stop=False, inc=None)
                mm(p_h, K_t, V_t, start=False, stop=True)
                evac_v(sdhat[0:64, s * 64:(s + 1) * 64], p_h, inc="sst")
                mm(p_o, sTLRBt[:], sUp[:, 0:64], start=True, stop=False, inc=None)
                mm(p_o, sTLRKt[:], V_t, start=False, stop=True)
                evac_v(sop[:, s * 64:(s + 1) * 64], p_o, inc="sst")

            r.wait("sync", "sst", r.val("sst"))
            r.wait("sync", "ac")
            r.dma("sync", rhat_scr.ap()[ch], srhat[:], inc="sws")
            r.dma("sync", op_scr.ap()[ch], sop[:], inc="sws")
            r.dma("sync", iph_scr.ap()[ch], siph[:], inc="sws")
            r.dma("sync", dhat_scr.ap()[ch], sdhat[:], inc="sws")
            r.dma("sync", lamc_scr.ap()[ch], slamc[:], inc="sws")

        grp_done[g] = {"pe": r.val("pe"), "ve": r.val("ve")}

    for s in range(NS2):
        r.op("gpsimd", lambda eng, s=s: eng.memset(sS[s][:], 0.0), inc="gp")
    r.wait("vector", "gp", r.val("gp"))
    SWS = r.val("sws")
    r.wait("sync", "sws", SWS)
    r.wait("tensor", "sst", r.val("sst"))
    for ch in range(NCH):
        r.wait("sync", "ve")
        r.dma("sync", piph[:], iph_scr.ap()[ch], inc="p1l")
        r.dma("sync", pdhat[:], dhat_scr.ap()[ch], inc="p1l")
        r.dma("sync", plamc[:], lamc_scr.ap()[ch], inc="p1l")
        P1L = r.val("p1l")
        r.wait("tensor", "p1l", P1L)
        r.wait("vector", "p1l", P1L)
        for s in range(NS2):
            j = s // 2
            lo = (s % 2) * 64
            r.wait("tensor", "ve")
            r.op("tensor", lambda eng, s=s: eng.matmul(
                p_st, siph_sl(piph, s), sS[s][:], start=True, stop=True), inc="pe")
            r.wait("vector", "pe")
            r.op("vector", lambda eng, s=s: eng.tensor_tensor(
                out=sdsf[0:64, s * 64:(s + 1) * 64], in0=p_st,
                in1=pdhat[0:64, s * 64:(s + 1) * 64], op=ALU.add), inc="ve")
            r.op("vector", lambda eng, s=s, j=j, lo=lo: eng.tensor_scalar(
                out=sS[s][:], in0=sdsf[0:64, s * 64:(s + 1) * 64],
                scalar1=plamc[lo:lo + 64, j:j + 1], scalar2=None,
                op0=ALU.mult), inc="ve")

    for s in range(NS2):
        r.op("vector", lambda eng, s=s: eng.tensor_copy(
            out=sdsf[0:64, s * 64:(s + 1) * 64], in_=sS[s][:]), inc="ve",
            nodrain=True)
    r.wait("sync", "ve")
    r.dma("sync", bass.AP(cc1_in, 0, [[64, 64], [4096, 16], [1, 64]]),
          sdsf[0:64, :].rearrange("p (s v) -> p s v", s=16), inc="ccw")
    r.wait("gpsimd", "ccw", r.val("ccw"))
    r.op("gpsimd", lambda eng: eng.collective_compute(
        "AllGather", ALU.bypass,
        replica_groups=[[0, 1], [2, 3], [4, 5], [6, 7]],
        ins=[cc1_in.ap()], outs=[cc1_out.ap()]), inc="cc")
    r.op("gpsimd", lambda eng: eng.drain())
    r.wait("sync", "cc", r.val("cc"))
    r.dma("sync", sdsf[0:64, :].rearrange("p (s v) -> p s v", s=16),
          bass.AP(cc1_out, 0, [[64, 64], [4096, 16], [1, 64]]), inc="ccr")
    CCR = r.val("ccr")
    r.wait("vector", "ccr", CCR)
    r.op("vector", lambda eng: eng.tensor_scalar(
        out=sdsf[0:64, :], in0=sdsf[0:64, :], scalar1=sisodd[:],
        scalar2=None, op0=ALU.mult), inc="ve")
    for s in range(NS2):
        r.op("vector", lambda eng, s=s: eng.tensor_copy(
            out=sS[s][:], in_=sdsf[0:64, s * 64:(s + 1) * 64]), inc="ve",
            nodrain=True)
    r.op("vector", lambda eng: eng.drain())

    for ch in range(NCH):
        r.wait("sync", "ve")
        r.wait("sync", "oev", r.val("oev"))
        r.wait("sync", "so", r.val("so"))
        r.wait("vector", "so", r.val("so"))
        r.dma("sync", prhat[:], rhat_scr.ap()[ch], inc="p2l")
        r.dma("sync", pop[:], op_scr.ap()[ch], inc="p2l")
        r.dma("sync", piph[:], iph_scr.ap()[ch], inc="p2l")
        r.dma("sync", pdhat[:], dhat_scr.ap()[ch], inc="p2l")
        r.dma("sync", plamc[:], lamc_scr.ap()[ch], inc="p2l")
        P2L = r.val("p2l")
        r.wait("tensor", "p2l", P2L)
        r.wait("vector", "p2l", P2L)
        for s in range(NS2):
            j = s // 2
            lo = (s % 2) * 64
            r.wait("tensor", "ve")
            r.op("tensor", lambda eng, s=s: eng.matmul(
                p_o, prhat[0:64, s * 128:(s + 1) * 128], sS[s][:],
                start=True, stop=True), inc="pe")
            r.wait("vector", "pe")
            r.op("vector", lambda eng, s=s: eng.tensor_tensor(
                out=so_t[:, s * 64:(s + 1) * 64], in0=p_o,
                in1=pop[:, s * 64:(s + 1) * 64], op=ALU.add), inc="oev",
                nodrain=True)
            r.op("tensor", lambda eng, s=s: eng.matmul(
                p_st, siph_sl(piph, s), sS[s][:], start=True, stop=True), inc="pe")
            r.wait("vector", "pe")
            r.op("vector", lambda eng, s=s: eng.tensor_tensor(
                out=sdsf[0:64, s * 64:(s + 1) * 64], in0=p_st,
                in1=pdhat[0:64, s * 64:(s + 1) * 64], op=ALU.add), inc="ve")
            r.op("vector", lambda eng, s=s, j=j, lo=lo: eng.tensor_scalar(
                out=sS[s][:], in0=sdsf[0:64, s * 64:(s + 1) * 64],
                scalar1=plamc[lo:lo + 64, j:j + 1], scalar2=None,
                op0=ALU.mult), inc="ve")
        r.wait("sync", "oev", r.val("oev"))
        r.dma("sync", d_o.ap()[ch * 128:(ch + 1) * 128, :], so_t[:], inc="so")

    r.wait("sync", "so", r.val("so"))
    F.end_phase()


def siph_sl(piph, s):
    return piph[0:64, s * 64:(s + 1) * 64]


def build_fused(debug_cc=False):
    F = Fctx()
    build_l0pre(F=F)
    emit_scan_fused(F)
    build_l0post(F=F)
    r = F.r
    F.begin_phase("X_")
    cc2_in = F.scr("cc2_in", [30, D], BF16)
    cc2_out = F.scr("cc2_out", [60, D], BF16)
    xn1h = F.scratch["xn1h_scr"]
    disodd = F.ext_in("is_odd", [64, 1], F32)
    shalo = r.sb("shalo", [30, D], BF16)
    sisodd2 = r.sb("sisodd2", [64, 1], F32)
    r.dma("sync", sisodd2[:], disodd.ap(), inc="hl")
    r.dma("gpsimd", cc2_in.ap(), xn1h.ap()[NT:NT + 30, :], inc="hw")
    r.wait("gpsimd", "hw", r.val("hw"))
    r.op("gpsimd", lambda eng: eng.collective_compute(
        "AllGather", ALU.bypass,
        replica_groups=[[0, 1], [2, 3], [4, 5], [6, 7]],
        ins=[cc2_in.ap()], outs=[cc2_out.ap()]), inc="hcc")
    r.op("gpsimd", lambda eng: eng.drain())
    r.wait("sync", "hcc", r.val("hcc"))
    r.dma("sync", shalo[:], cc2_out.ap()[0:30, :], inc="hl")
    HL = r.val("hl")
    r.wait("vector", "hl", HL)
    r.op("vector", lambda eng: eng.tensor_scalar(
        out=shalo[:], in0=shalo[:], scalar1=sisodd2[0:30, 0:1], scalar2=None,
        op0=ALU.mult), inc="hv")
    r.wait("sync", "hv", r.val("hv"))
    r.dma("sync", xn1h.ap()[0:30, :], shalo[:], inc="hs")
    r.wait("sync", "hs", r.val("hs"))
    F.end_phase()
    build_l1(F=F)
    if debug_cc:
        r2 = F.r
        d_cc1in = F.nc.dram_tensor("d_cc1in", [1024, 64], F32, kind="ExternalOutput")
        d_cc1out = F.nc.dram_tensor("d_cc1out", [2048, 64], F32, kind="ExternalOutput")
        r2.barrier()
        d_oo = F.nc.dram_tensor("d_oo", [NT, D], F32, kind="ExternalOutput")
        d_x1 = F.nc.dram_tensor("d_x1", [NT, D], F32, kind="ExternalOutput")
        d_s1 = F.nc.dram_tensor("d_s1d", [NT, D], F32, kind="ExternalOutput")
        d_y = F.nc.dram_tensor("d_yd", [NT, D], F32, kind="ExternalOutput")
        r2.dma("gpsimd", d_oo.ap(), F.scratch["o"].ap(), inc="Zdbg")
        r2.dma("gpsimd", d_x1.ap(), F.scratch["x1"].ap(), inc="Zdbg")
        r2.dma("gpsimd", d_s1.ap(), F.scratch["s1_scr"].ap(), inc="Zdbg")
        r2.dma("gpsimd", d_y.ap(), F.scratch["y_scr"].ap(), inc="Zdbg")
        d_w = F.nc.dram_tensor("d_wscr", [NT, 496], F32, kind="ExternalOutput")
        r2.dma("gpsimd", d_w.ap(), F.scratch["w_scr"].ap(), inc="Zdbg")
        d_xn1 = F.nc.dram_tensor("d_xn1h", [NT + 30, D], BF16, kind="ExternalOutput")
        r2.dma("gpsimd", d_xn1.ap(), F.scratch["xn1h_scr"].ap(), inc="Zdbg")
        r2.dma("gpsimd", d_cc1in.ap(), F.scratch["cc1_in"].ap(), inc="Zdbg")
        r2.dma("gpsimd", d_cc1out.ap(), F.scratch["cc1_out"].ap(), inc="Zdbg")
        r2.wait("sync", "Zdbg", r2.val("Zdbg"))
    F.r.barrier()
    F.r.build_block()
    F.r.close()
    return F


def kernel(**inputs):
    C = _CONSTS
    cores = list(range(NCORE))
    x = inputs["L0_attn_hidden_states"]
    import os
    F = build_fused(debug_cc=bool(os.environ.get('DBGCC')))
    base = l0pre_inmaps(inputs)
    bmask = _l1_consts()
    gb_t = np.ascontiguousarray(np.broadcast_to(
        inputs["L1_ffn_gate_bias"].reshape(1, 64), (128, 64))).astype(np.float32)
    rk_t = np.ascontiguousarray(np.broadcast_to(
        inputs["L0_attn_r_k"].reshape(1, D), (128, D))).astype(np.float32)
    gnw_t = np.ascontiguousarray(np.broadcast_to(
        inputs["L0_attn_gn_weight"].reshape(1, D), (128, D))).astype(np.float32)
    gnb_t = np.ascontiguousarray(np.broadcast_to(
        inputs["L0_attn_gn_bias"].reshape(1, D), (128, D))).astype(np.float32)
    maps = []
    for c in cores:
        m = dict(base[c])
        m.update(
            utinclD=C["ut_incl"], utstrictD=C["ut_strict"],
            mslD=C["m_sl"], msuD=C["m_su"], mtuD=C["m_tu"],
            is_odd=np.full((64, 1), float(c % 2), np.float32),
            rk_tile=rk_t, gnw_tile=gnw_t, gnb_tile=gnb_t,
            Wo=inputs["L0_attn_Wo"], router_W=inputs["L0_ffn_router_W"],
            router_b=inputs["L0_ffn_router_b"].reshape(1, 8),
            W1=inputs["L0_ffn_W1"], b1=inputs["L0_ffn_b1"],
            W2=inputs["L0_ffn_W2"], b2=inputs["L0_ffn_b2"],
            w_lin=inputs["L1_attn_w_lin"], gate_w=inputs["L1_ffn_gate_w"],
            gbias_tile=gb_t,
            We_gate=inputs["L1_ffn_We_gate"], We_up=inputs["L1_ffn_We_up"],
            We_down=inputs["L1_ffn_We_down"],
            Ws_gate=inputs["L1_ffn_Ws_gate"], Ws_up=inputs["L1_ffn_Ws_up"],
            Ws_down=inputs["L1_ffn_Ws_down"],
            bmask=bmask,
        )
        maps.append(m)
    res = run_bass_kernel_spmd(F.nc, maps, core_ids=cores)
    T = 8192
    out = np.empty((4, T, D), np.float32)
    for c in cores:
        b, half = c // 2, c % 2
        out[b, half * NT:(half + 1) * NT] = res.results[c]["out"]
    return out



_CONSTS = host_consts()


def l0pre_inmaps(inputs):
    x = inputs["L0_attn_hidden_states"]
    B, T, _ = x.shape
    maps = []
    for c in range(NCORE):
        b, half = c // 2, c % 2
        t0 = half * NT
        if t0 == 0:
            xh = np.concatenate([np.zeros((1, D), np.float32),
                                 x[b, 0:NT].astype(np.float32)], axis=0)
        else:
            xh = np.ascontiguousarray(x[b, t0 - 1: t0 + NT])
        m = dict(
            xh=xh,
            Wr=inputs["L0_attn_Wr"], Wk=inputs["L0_attn_Wk"], Wv=inputs["L0_attn_Wv"],
            wA=inputs["L0_attn_wA"], wB=inputs["L0_attn_wB"],
            aA=inputs["L0_attn_aA"], aB=inputs["L0_attn_aB"],
            gA=inputs["L0_attn_gA"], gB=inputs["L0_attn_gB"],
            w_bias=inputs["L0_attn_w_bias"].reshape(1, D),
            a_bias=inputs["L0_attn_a_bias"].reshape(1, D),
            k_k=inputs["L0_attn_k_k"].reshape(1, D),
            k_a=inputs["L0_attn_k_a"].reshape(1, D),
            mixes=np.stack([inputs["L0_attn_mix_" + n].reshape(D) for n in
                            ("r", "w", "k", "v", "a", "g")]),
            identD=_CONSTS["ident"], blockonesD=_CONSTS["blockones"],
            blockpat2D=_CONSTS["blockpat2"],
        )
        maps.append(m)
    return maps


def _l1_consts():
    bmask = np.zeros((64, 16, 94), np.float32)
    for t in range(64):
        bmask[t, :, t:min(94, t + 31)] = 1.0
    return bmask.reshape(64, 16 * 94).astype(BF)


def kernel_multi(**inputs):
    C = _CONSTS
    cores = list(range(NCORE))
    x = inputs["L0_attn_hidden_states"]

    nc1 = build_l0pre()
    res1 = run_bass_kernel_spmd(nc1, l0pre_inmaps(inputs), core_ids=cores)
    r1 = res1.results

    T = 8192
    full_tm = {n: [np.concatenate([r1[2 * b][n], r1[2 * b + 1][n]], 0)
                   for b in range(4)]
               for n in ("wsig_tm", "kk_tm", "a_tm", "k2_tm", "v_tm")}
    full_fm = {n: [np.concatenate([r1[2 * b][n], r1[2 * b + 1][n]], 1)
                   for b in range(4)]
               for n in ("r_fm", "kk_fm", "a_fm", "k2_fm")}
    maps2 = []
    for c in cores:
        m = {}
        sts = [(g // 16, g % 16) for g in range(8 * c, 8 * c + 8)]
        for key, src_n in (("rF", "r_fm"), ("kkF", "kk_fm"), ("aF", "a_fm"),
                           ("k2F", "k2_fm")):
            m[key] = np.stack([full_fm[src_n][b][h * 64:(h + 1) * 64, :]
                               for (b, h) in sts])
        for key, src_n in (("wT", "wsig_tm"), ("kkT", "kk_tm"), ("aT", "a_tm"),
                           ("k2T", "k2_tm"), ("vT", "v_tm")):
            m[key] = np.stack([full_tm[src_n][b][:, h * 64:(h + 1) * 64]
                               for (b, h) in sts])
        m.update(identD=C["ident"], utinclD=C["ut_incl"], utstrictD=C["ut_strict"],
                 mslD=C["m_sl"], msuD=C["m_su"], mtuD=C["m_tu"])
        maps2.append(m)
    nc2 = build_scan()
    res2 = run_bass_kernel_spmd(nc2, maps2, core_ids=cores)
    r2 = res2.results

    rk_t = np.ascontiguousarray(np.broadcast_to(
        inputs["L0_attn_r_k"].reshape(1, D), (128, D))).astype(np.float32)
    gnw_t = np.ascontiguousarray(np.broadcast_to(
        inputs["L0_attn_gn_weight"].reshape(1, D), (128, D))).astype(np.float32)
    gnb_t = np.ascontiguousarray(np.broadcast_to(
        inputs["L0_attn_gn_bias"].reshape(1, D), (128, D))).astype(np.float32)
    maps3 = []
    for c in cores:
        b, half = c // 2, c % 2
        t0 = half * NT
        o_core = np.empty((NT, D), np.float32)
        for h in range(H):
            g = b * 16 + h
            o_core[:, h * 64:(h + 1) * 64] = \
                r2[g // 8]["o"][g % 8][t0:t0 + NT, :]
        m = dict(
            x=np.ascontiguousarray(x[b, t0:t0 + NT]).astype(np.float32),
            o=o_core,
            r_tm=r1[c]["r_tm"], k2_tm=r1[c]["k2_tm"], v_tm=r1[c]["v_tm"],
            g_tm=r1[c]["g_tm"], xn_fm=r1[c]["xn_fm"],
            rk_tile=rk_t, gnw_tile=gnw_t, gnb_tile=gnb_t,
            Wo=inputs["L0_attn_Wo"], router_W=inputs["L0_ffn_router_W"],
            router_b=inputs["L0_ffn_router_b"].reshape(1, 8),
            W1=inputs["L0_ffn_W1"], b1=inputs["L0_ffn_b1"],
            W2=inputs["L0_ffn_W2"], b2=inputs["L0_ffn_b2"],
            identD=C["ident"],
        )
        maps3.append(m)
    nc3 = build_l0post()
    res3 = run_bass_kernel_spmd(nc3, maps3, core_ids=cores)
    r3 = res3.results

    bmask = _l1_consts()
    gb_t = np.ascontiguousarray(np.broadcast_to(
        inputs["L1_ffn_gate_bias"].reshape(1, 64), (128, 64))).astype(np.float32)
    maps4 = []
    for c in cores:
        b, half = c // 2, c % 2
        if half == 0:
            xn1h = np.concatenate([np.zeros((30, D), BF), r3[c]["xn1_tm"]], 0)
        else:
            xn1h = np.concatenate([r3[c - 1]["xn1_tm"][-30:, :], r3[c]["xn1_tm"]], 0)
        m = dict(
            x1=r3[c]["x1"], xn1h=np.ascontiguousarray(xn1h),
            w_lin=inputs["L1_attn_w_lin"], gate_w=inputs["L1_ffn_gate_w"],
            gbias_tile=gb_t,
            We_gate=inputs["L1_ffn_We_gate"], We_up=inputs["L1_ffn_We_up"],
            We_down=inputs["L1_ffn_We_down"],
            Ws_gate=inputs["L1_ffn_Ws_gate"], Ws_up=inputs["L1_ffn_Ws_up"],
            Ws_down=inputs["L1_ffn_Ws_down"],
            identD=C["ident"], bmask=bmask,
        )
        maps4.append(m)
    nc4 = build_l1()
    res4 = run_bass_kernel_spmd(nc4, maps4, core_ids=cores)

    out = np.empty((4, T, D), np.float32)
    for c in cores:
        b, half = c // 2, c % 2
        out[b, half * NT:(half + 1) * NT] = res4.results[c]["out"]
    return out


# version 08 (final)
# speedup vs baseline: 1.2319x; 1.2319x over previous
import numpy as np
import ml_dtypes
from contextlib import ExitStack

DBG_WLINT = False

import concourse.bass as bass
import concourse.mybir as mybir
from concourse.bass_utils import run_bass_kernel_spmd

F32 = mybir.dt.float32
BF16 = mybir.dt.bfloat16
I32 = mybir.dt.int32
AF = mybir.ActivationFunctionType
ALU = mybir.AluOpType
AX = mybir.AxisListType
BF = ml_dtypes.bfloat16

NCORE = 8
NT = 4096
D = 1024
H = 16
HD = 64
DECAY = -0.6065306597126334
GN_EPS = HD * 1e-05
LN_EPS = 1e-5

ENGINES = ("sync", "tensor", "vector", "scalar", "gpsimd")


class Rec:

    def __init__(self, nc):
        self.nc = nc
        self.ops = {e: [] for e in ENGINES}
        self.stack = ExitStack()
        self.scopes = [self.stack]
        self.sems = {}
        self.cnt = {}
        self.prefix = ""

    def sem(self, name):
        name = self.prefix + name
        if name not in self.sems:
            self.sems[name] = self.stack.enter_context(self.nc.semaphore(name))
            self.cnt[name] = 0
        return self.sems[name]

    def push_scope(self):
        s = ExitStack()
        self.scopes.append(s)
        return s

    def pop_scope(self):
        self.scopes.pop().close()

    def sb(self, name, shape, dtype):
        return self.scopes[-1].enter_context(
            self.nc.sbuf_tensor(self.prefix + name, list(shape), dtype))

    def ps(self, name, shape, dtype=F32):
        return self.scopes[-1].enter_context(
            self.nc.psum_tensor(self.prefix + name, list(shape), dtype))

    def barrier(self):
        waits = [(n, v) for n, v in self.cnt.items() if v > 0]
        for e in ENGINES:
            for n, v in waits:
                h = self.sems[n]
                self.ops[e].append(lambda eng, h=h, v=v: eng.wait_ge(h, v))
        save_prefix = self.prefix
        self.prefix = ""
        self.sem("barm")
        for e in ENGINES:
            self.ops[e].append(lambda eng: eng.drain())
            self.cnt["barm"] += 1
            h = self.sems["barm"]
            self.ops[e].append(lambda eng, h=h: eng.nop().then_inc(h, 1))
        v = self.cnt["barm"]
        h = self.sems["barm"]
        for e in ENGINES:
            self.ops[e].append(lambda eng, h=h, v=v: eng.wait_ge(h, v))
        self.prefix = save_prefix

    def op(self, eng_name, fn, inc=None, inc_amt=1, nodrain=False):
        if inc is not None:
            sem_h = self.sem(inc)
            self.cnt[self.prefix + inc] += inc_amt

            def play(eng, fn=fn, sem_h=sem_h, amt=inc_amt):
                fn(eng).then_inc(sem_h, amt)
            self.ops[eng_name].append(play)
        else:
            self.ops[eng_name].append(fn)
        if eng_name in ("vector", "scalar") and not nodrain:
            self.ops[eng_name].append(lambda eng: eng.drain())

    def wait(self, eng_name, sem_name, val=None):
        sem_h = self.sem(sem_name)
        v = self.cnt[self.prefix + sem_name] if val is None else val
        if v <= 0:
            return
        self.ops[eng_name].append(lambda eng, s=sem_h, v=v: eng.wait_ge(s, v))

    def dma(self, q, out, in_, inc=None, **kw):
        if inc is None:
            inc = "st_" + q
        self.op(q, lambda eng, o=out, i=in_, kw=kw: eng.dma_start(out=o, in_=i, **kw),
                inc=inc, inc_amt=16)

    def val(self, sem_name):
        self.sem(sem_name)
        return self.cnt[self.prefix + sem_name]

    def close(self):
        self.stack.close()

    def build_block(self):
        with self.nc.Block() as blk:
            for e in ENGINES:
                if not self.ops[e]:
                    continue

                def make(ename):
                    def body(eng):
                        for f in self.ops[ename]:
                            f(eng)
                    return body
                getattr(blk, e)(make(e))



def host_consts():
    c = {}
    c["ident"] = np.eye(128, dtype=BF)
    c["ut_incl"] = np.triu(np.ones((128, 128), np.float32)).astype(BF)
    c["ut_strict"] = np.triu(np.ones((128, 128), np.float32), 1).astype(BF)
    c["m_sl"] = np.tril(np.ones((128, 128), np.float32), -1)
    c["m_su"] = np.triu(np.ones((128, 128), np.float32), 1)
    c["m_tl"] = np.tril(np.ones((128, 128), np.float32))
    c["m_tu"] = np.triu(np.ones((128, 128), np.float32))
    bo = np.zeros((128, 2), np.float32)
    bo[0:64, 0] = 1.0
    bo[64:128, 1] = 1.0
    c["blockones"] = bo.astype(BF)
    c["blockpat2"] = bo.T.copy().astype(BF)
    return c




class Fctx:

    def __init__(self):
        self.nc = bass.Bass("TRN2", target_bir_lowering=False)
        self.r = Rec(self.nc)
        self.pA = self.r.ps("arena_pA", [128, D], F32).ap()
        self.pB = self.r.ps("arena_pB", [128, D], F32).ap()
        self.pC = self.r.ps("arena_pC", [128, D], F32).ap()
        self.pD = self.r.ps("arena_pD", [128, D], F32).ap()
        self.pC_bf = self.pC.bitcast(BF16)
        self.ext = {}
        self.scratch = {}

    def ext_in(self, name, shape, dt_):
        if name not in self.ext:
            self.ext[name] = self.nc.dram_tensor(name, list(shape), dt_,
                                                 kind="ExternalInput")
        return self.ext[name]

    def scr(self, name, shape, dt_):
        if name not in self.scratch:
            self.scratch[name] = self.nc.dram_tensor(name, list(shape), dt_,
                                                     kind="Internal")
        return self.scratch[name]

    def begin_phase(self, prefix):
        self.r.barrier()
        self.r.ops["gpsimd"].append(lambda eng: eng.drain())
        self.r.prefix = prefix
        self.r.push_scope()

    def end_phase(self):
        self.r.pop_scope()
        self.r.prefix = ""



def build_l0pre(F=None):
    if F is None:
        nc = bass.Bass("TRN2", target_bir_lowering=False)
        r = Rec(nc)
        din = lambda n, s, d: nc.dram_tensor(n, list(s), d, kind="ExternalInput")
        dout = lambda n, s, d: nc.dram_tensor(n, list(s), d, kind="ExternalOutput")
    else:
        nc, r = F.nc, F.r
        F.begin_phase("A_")
        din = F.ext_in
        dout = F.scr
    GS = 512
    NG = NT // GS

    xh = din("xh", [NT + 1, D], F32)
    dW = {n: din(n, [D, D], F32) for n in ("Wr", "Wk", "Wv")}
    dwA = din("wA", [D, 64], F32)
    dwB = din("wB", [64, D], F32)
    daA = din("aA", [D, 64], F32)
    daB = din("aB", [64, D], F32)
    dgA = din("gA", [D, 160], F32)
    dgB = din("gB", [160, D], F32)
    dwbias = din("w_bias", [1, D], F32)
    dabias = din("a_bias", [1, D], F32)
    dkk = din("k_k", [1, D], F32)
    dka = din("k_a", [1, D], F32)
    dmix = din("mixes", [6, D], F32)
    dident = din("identD", [128, 128], BF16)
    dblko = din("blockonesD", [128, 2], BF16)
    dblkp = din("blockpat2D", [2, 128], BF16)

    o_fm = {n: dout(n, [D, NT], BF16)
            for n in ("xn_fm", "r_fm", "kk_fm", "k2_fm", "a_fm")}
    o_tm = {n: dout(n, [NT, D], BF16)
            for n in ("r_tm", "kk_tm", "k2_tm", "v_tm", "a_tm", "wsig_tm", "g_tm")}

    sW = {n: r.sb("s" + n, [128, 8 * D], BF16) for n in ("Wr", "Wk", "Wv")}
    swA = r.sb("swA", [128, 8 * 64], BF16)
    saA = r.sb("saA", [128, 8 * 64], BF16)
    sgA = r.sb("sgA", [128, 8 * 160], BF16)
    swB = r.sb("swB", [64, D], BF16)
    saB = r.sb("saB", [64, D], BF16)
    sgB1 = r.sb("sgB1", [128, D], BF16)
    sgB2 = r.sb("sgB2", [32, D], BF16)
    swbias_r = r.sb("swbias_r", [1, D], BF16)
    sabias_r = r.sb("sabias_r", [1, D], BF16)
    sabias_fm = r.sb("sabias_fm", [128, 8], F32)
    skkv = r.sb("skkv", [128, 8], F32)
    skav = r.sb("skav", [128, 8], F32)
    sc0 = r.sb("sc0", [128, 8], F32)
    smix = r.sb("smix", [128, 48], F32)
    sident = r.sb("sident", [128, 128], BF16)
    sblko = r.sb("sblko", [128, 2], BF16)
    sblkp = r.sb("sblkp", [2, 128], BF16)
    sonesrow = r.sb("sonesrow", [1, 128], BF16)

    sx = [r.sb(f"sx{i}", [128, D], F32) for i in range(4)]
    sxhalo = r.sb("sxhalo", [1, D], F32)
    sxn_tm = [r.sb(f"sxn_tm{i}", [128, D], BF16) for i in range(4)]
    sxnh = r.sb("sxnh", [1, D], BF16)
    stat = r.sb("stat", [128, 8], F32)
    stat1 = r.sb("stat1", [1, 8], F32)
    ssq = r.sb("ssq", [128, D], BF16)
    sxn_fm = r.sb("sxn_fm", [128, 8 * (GS + 1)], BF16)
    sdelta = r.sb("sdelta", [128, 8 * GS], BF16)
    sxj = r.sb("sxj", [128, 8 * GS], BF16)
    sk32 = r.sb("sk32", [128, 8 * GS], F32)
    smid = r.sb("smid", [64, GS], BF16)
    smidg1 = r.sb("smidg1", [128, GS], BF16)
    smidg2 = r.sb("smidg2", [32, GS], BF16)
    sfm1 = r.sb("sfm1", [128, 8 * GS], BF16)
    sfm_a = r.sb("sfm_a", [128, 8 * GS], BF16)
    sfm_kkraw = r.sb("sfm_kkraw", [128, 8 * GS], BF16)
    sfm_kk = r.sb("sfm_kk", [128, 8 * GS], BF16)
    sfm_k2 = r.sb("sfm_k2", [128, 8 * GS], BF16)
    stm = r.sb("stm", [128, 4 * D], BF16)
    snorm = r.sb("snorm", [2, GS], F32)
    snorm2 = r.sb("snorm2", [2, GS], F32)
    snormbf = r.sb("snormbf", [2, GS], BF16)
    st4 = r.sb("st4", [128, GS], BF16)

    if F is None:
        p_proj = [r.ps(f"p_proj{i}", [128, GS], F32) for i in range(2)]
        p_tm = r.ps("p_tm", [128, D], F32)
        p_tr = r.ps("p_tr", [128, 128], BF16)
        p_small = r.ps("p_small", [64, GS], F32)
    else:
        p_proj = [F.pB[:, 0:512], F.pB[:, 512:1024]]
        p_tm = F.pA
        p_tr = F.pC_bf[:, 0:128]
        p_small = F.pD[0:64, 0:512]

    def wload(dst, src_t, rows, cols, row0=0, col0=0):
        src_ap = bass.AP(src_t, row0 * src_t.shape[1] + col0,
                         [[src_t.shape[1], rows], [1, cols]])
        r.dma("gpsimd", dst, src_ap, inc="wld")

    for d in range(8):
        for n in ("Wr", "Wk", "Wv"):
            wload(sW[n][:, d * D:(d + 1) * D], dW[n], 128, D, row0=d * 128)
        wload(swA[:, d * 64:(d + 1) * 64], dwA, 128, 64, row0=d * 128)
        wload(saA[:, d * 64:(d + 1) * 64], daA, 128, 64, row0=d * 128)
        wload(sgA[:, d * 160:(d + 1) * 160], dgA, 128, 160, row0=d * 128)
    wload(swB[:], dwB, 64, D)
    wload(saB[:], daB, 64, D)
    wload(sgB1[:], dgB, 128, D)
    wload(sgB2[:], dgB, 32, D, row0=128)
    r.dma("gpsimd", swbias_r[:], dwbias.ap(), inc="wld")
    r.dma("gpsimd", sabias_r[:], dabias.ap(), inc="wld")
    for dst, src in ((sabias_fm, dabias), (skkv, dkk), (skav, dka)):
        r.dma("gpsimd", dst[:], bass.AP(src, 0, [[1, 128], [128, 8]]), inc="wld",
              allow_slow_non_contiguous=True)
    for j in range(6):
        r.dma("gpsimd", smix[:, j * 8:(j + 1) * 8],
              bass.AP(dmix, j * D, [[1, 128], [128, 8]]), inc="wld",
              allow_slow_non_contiguous=True)
    r.dma("sync", sident[:], dident.ap(), inc="wld")
    r.dma("sync", sblko[:], dblko.ap(), inc="wld")
    r.dma("sync", sblkp[:], dblkp.ap(), inc="wld")
    r.op("gpsimd", lambda eng: eng.memset(sonesrow[:], 1.0), inc="wldg")
    WLD = r.val("wld")
    r.wait("vector", "wld", WLD)
    r.wait("vector", "wldg", r.val("wldg"))
    r.op("vector", lambda eng: eng.tensor_scalar(
        out=sc0[:], in0=skav[:], scalar1=-1.0, scalar2=1.0,
        op0=ALU.mult, op1=ALU.add), inc="ve")
    r.wait("tensor", "wld", WLD)
    r.wait("scalar", "wld", WLD)

    def ln_rows(x_ap, xn_ap, nrow, st, cols=D):
        r.op("scalar", lambda eng: eng.activation(
            out=ssq[0:nrow, 0:cols], in_=x_ap, func=AF.Square,
            accum_out=st[0:nrow, 1:2]), inc="ac")
        r.op("scalar", lambda eng: eng.activation(
            out=ssq[0:nrow, 0:cols], in_=x_ap, func=AF.Identity,
            accum_out=st[0:nrow, 0:1]), inc="ac")
        r.wait("vector", "ac")
        r.op("vector", lambda eng: eng.tensor_scalar(
            out=st[0:nrow, 2:3], in0=st[0:nrow, 0:1], scalar1=1.0 / cols,
            scalar2=None, op0=ALU.mult))
        r.op("vector", lambda eng: eng.tensor_scalar(
            out=st[0:nrow, 3:4], in0=st[0:nrow, 1:2], scalar1=1.0 / cols,
            scalar2=LN_EPS, op0=ALU.mult, op1=ALU.add))
        r.op("vector", lambda eng: eng.tensor_scalar(
            out=st[0:nrow, 5:6], in0=st[0:nrow, 2:3],
            scalar1=st[0:nrow, 2:3], scalar2=None, op0=ALU.mult))
        r.op("vector", lambda eng: eng.tensor_tensor(
            out=st[0:nrow, 4:5], in0=st[0:nrow, 3:4], in1=st[0:nrow, 5:6],
            op=ALU.subtract), inc="ve")
        r.wait("scalar", "ve")
        r.op("scalar", lambda eng: eng.activation(
            out=st[0:nrow, 6:7], in_=st[0:nrow, 4:5], func=AF.Sqrt), inc="ac")
        r.wait("vector", "ac")
        r.op("vector", lambda eng: eng.reciprocal(
            out=st[0:nrow, 7:8], in_=st[0:nrow, 6:7]))
        r.op("vector", lambda eng: eng.tensor_scalar(
            out=xn_ap, in0=x_ap, scalar1=st[0:nrow, 2:3],
            scalar2=st[0:nrow, 7:8], op0=ALU.subtract, op1=ALU.mult), inc="ve")

    def transpose_pe(src_ap, dst_ap):
        p, f = src_ap.shape
        r.wait("tensor", "ve")
        r.op("tensor", lambda eng: eng.transpose(
            p_tr[0:f, 0:p], src_ap, sident[0:p, 0:p]), inc="pe")
        r.wait("vector", "pe")
        r.op("vector", lambda eng: eng.tensor_copy(
            out=dst_ap, in_=p_tr[0:f, 0:p]), inc="ve")

    def mk_xj(j):
        r.wait("vector", "ve")
        r.wait("vector", "pe")
        for c in range(8):
            r.op("vector", lambda eng, c=c, j=j: eng.scalar_tensor_tensor(
                out=sxj[:, c * GS:(c + 1) * GS],
                in0=sdelta[:, c * GS:(c + 1) * GS],
                scalar=smix[:, j * 8 + c: j * 8 + c + 1],
                in1=sxn_fm[:, c * (GS + 1) + 1: c * (GS + 1) + 1 + GS],
                op0=ALU.mult, op1=ALU.add), inc="ve")
        r.wait("tensor", "ve")

    def proj_fm(w_sb, evac):
        for dp in range(8):
            pp = p_proj[dp % 2]
            r.wait("tensor", "pev")
            for d in range(8):
                r.op("tensor", lambda eng, d=d, dp=dp, pp=pp: eng.matmul(
                    pp[:], w_sb[:, d * D + dp * 128: d * D + dp * 128 + 128],
                    sxj[:, d * GS:(d + 1) * GS],
                    start=(d == 0), stop=(d == 7)), inc=("pe" if d == 7 else None))
            evac(dp, pp)

    def evac_to(buf, engine="vector", func=None, bias=None):
        def evac(dp, pp):
            if engine == "vector":
                r.wait("vector", "pe")
                r.op("vector", lambda eng, dp=dp, pp=pp: eng.tensor_copy(
                    out=buf[:, dp * GS:(dp + 1) * GS], in_=pp[:]), inc="pev")
            else:
                r.wait("scalar", "pe")
                kw = {}
                if bias is not None:
                    kw["bias"] = bias[:, dp:dp + 1]
                r.op("scalar", lambda eng, dp=dp, pp=pp, kw=kw: eng.activation(
                    out=buf[:, dp * GS:(dp + 1) * GS], in_=pp[:], func=func, **kw),
                    inc="pev")
        return evac

    def tm_out_and_store(lhs_tiles, rhs_list, dram_t, t0, act=None, bias_row=None,
                         extra_lhs=None, extra_rhs=None):
        r.wait("sync", "st_sync", r.val("st_sync"))
        r.wait("scalar", "st_sync", r.val("st_sync"))
        for i in range(4):
            for hf in range(2):
                sl = slice(hf * 512, (hf + 1) * 512)
                r.op("tensor", lambda eng, i=i, sl=sl, lt=lhs_tiles: eng.matmul(
                    p_tm[:, sl], lt(i), rhs_list[0][:, sl], start=True,
                    stop=(extra_lhs is None and bias_row is None)),
                    inc=("pe" if (extra_lhs is None and bias_row is None) else None))
                if extra_lhs is not None:
                    r.op("tensor", lambda eng, i=i, sl=sl: eng.matmul(
                        p_tm[:, sl], extra_lhs(i), extra_rhs[:, sl], start=False,
                        stop=(bias_row is None)),
                        inc=("pe" if bias_row is None else None))
                if bias_row is not None:
                    r.op("tensor", lambda eng, sl=sl: eng.matmul(
                        p_tm[:, sl], sonesrow[0:1, 0:128], bias_row[0:1, sl],
                        start=False, stop=True), inc="pe")
            r.wait("scalar", "pe")
            fn = act if act is not None else AF.Identity
            r.op("scalar", lambda eng, i=i, fn=fn: eng.activation(
                out=stm[:, i * D:(i + 1) * D], in_=p_tm[:], func=fn), inc="ac")
            r.wait("tensor", "ac")
        r.wait("sync", "ac")
        dst = bass.AP(dram_t, t0 * D, [[D, 128], [128 * D, 4], [1, D]])
        src = stm[:].rearrange("p (i j) -> p i j", i=4)
        r.dma("sync", dst, src, inc="st_sync")

    def fm_store(buf, dram_t, t0):
        dst = bass.AP(dram_t, t0, [[NT, 128], [128 * NT, 8], [1, GS]])
        src = buf[:].rearrange("p (c j) -> p c j", c=8)
        r.dma("sync", dst, src, inc="st_sync")

    def transpose_chunks_to_tm(fm_buf, dram_t, t0):
        r.wait("sync", "st_sync", r.val("st_sync"))
        r.wait("vector", "st_sync", r.val("st_sync"))
        for i in range(4):
            for c in range(8):
                transpose_pe(fm_buf[:, c * GS + i * 128: c * GS + (i + 1) * 128],
                             stm[:, i * D + c * 128: i * D + (c + 1) * 128])
        r.wait("sync", "ve")
        dst = bass.AP(dram_t, t0 * D, [[D, 128], [128 * D, 4], [1, D]])
        src = stm[:].rearrange("p (i j) -> p i j", i=4)
        r.dma("sync", dst, src, inc="st_sync")

    for g in range(NG):
        t0 = g * GS
        if g > 0:
            r.wait("sync", "ve")
            r.wait("sync", "ac")
            r.wait("sync", "pe")
            r.wait("sync", "st_sync", r.val("st_sync"))
        for i in range(4):
            r.dma("sync", sx[i][:], xh.ap()[1 + t0 + i * 128: 1 + t0 + (i + 1) * 128, :],
                  inc="xl")
        r.dma("sync", sxhalo[:], xh.ap()[t0:t0 + 1, :], inc="xl")
        XL = r.val("xl")
        r.wait("scalar", "xl", XL)
        r.wait("vector", "xl", XL)

        for i in range(4):
            ln_rows(sx[i][:], sxn_tm[i][:], 128, stat)
        ln_rows(sxhalo[:], sxnh[:], 1, stat1)

        for i in range(4):
            for c in range(8):
                transpose_pe(sxn_tm[i][:, c * 128:(c + 1) * 128],
                             sxn_fm[:, c * (GS + 1) + 1 + i * 128: c * (GS + 1) + 1 + (i + 1) * 128])
        for c in range(8):
            transpose_pe(sxnh[0:1, c * 128:(c + 1) * 128],
                         sxn_fm[:, c * (GS + 1): c * (GS + 1) + 1])
        r.wait("sync", "ve")
        dst = bass.AP(o_fm["xn_fm"], t0, [[NT, 128], [128 * NT, 8], [1, GS]])
        src = sxn_fm[:].rearrange("p (c j) -> p c j", c=8)[:, :, 1:GS + 1]
        r.dma("sync", dst, src, inc="st_sync")

        r.wait("vector", "ve")
        for c in range(8):
            r.op("vector", lambda eng, c=c: eng.tensor_tensor(
                out=sdelta[:, c * GS:(c + 1) * GS],
                in0=sxn_fm[:, c * (GS + 1): c * (GS + 1) + GS],
                in1=sxn_fm[:, c * (GS + 1) + 1: c * (GS + 1) + 1 + GS],
                op=ALU.subtract), inc="ve")

        mk_xj(1)
        for d in range(8):
            r.op("tensor", lambda eng, d=d: eng.matmul(
                p_small[:], swA[:, d * 64:(d + 1) * 64], sxj[:, d * GS:(d + 1) * GS],
                start=(d == 0), stop=(d == 7)), inc=("pe" if d == 7 else None))
        r.wait("scalar", "pe")
        r.op("scalar", lambda eng: eng.activation(
            out=smid[:], in_=p_small[:], func=AF.Tanh), inc="ac")
        r.wait("tensor", "ac")
        tm_out_and_store(lambda i: smid[:, i * 128:(i + 1) * 128], [swB],
                         o_tm["wsig_tm"], t0, act=AF.Sigmoid, bias_row=swbias_r)

        mk_xj(4)
        for d in range(8):
            r.op("tensor", lambda eng, d=d: eng.matmul(
                p_small[:], saA[:, d * 64:(d + 1) * 64], sxj[:, d * GS:(d + 1) * GS],
                start=(d == 0), stop=(d == 7)), inc=("pe" if d == 7 else None))
        r.wait("vector", "pe")
        r.op("vector", lambda eng: eng.tensor_copy(out=smid[:], in_=p_small[:]), inc="ve")
        r.wait("tensor", "ve")
        tm_out_and_store(lambda i: smid[:, i * 128:(i + 1) * 128], [saB],
                         o_tm["a_tm"], t0, act=AF.Sigmoid, bias_row=sabias_r)
        for dp in range(8):
            pp = p_proj[dp % 2]
            r.wait("tensor", "pev")
            r.op("tensor", lambda eng, dp=dp, pp=pp: eng.matmul(
                pp[:], saB[:, dp * 128:(dp + 1) * 128], smid[:],
                start=True, stop=True), inc="pe")
            r.wait("scalar", "pe")
            r.op("scalar", lambda eng, dp=dp, pp=pp: eng.activation(
                out=sfm_a[:, dp * GS:(dp + 1) * GS], in_=pp[:], func=AF.Sigmoid,
                bias=sabias_fm[:, dp:dp + 1]), inc="pev")
        r.wait("sync", "pev")
        fm_store(sfm_a, o_fm["a_fm"], t0)

        mk_xj(2)
        proj_fm(sW["Wk"], evac_to(sk32))
        r.wait("vector", "pev")
        r.wait("scalar", "pev")
        for c in range(8):
            r.op("vector", lambda eng, c=c: eng.tensor_scalar(
                out=sfm_kkraw[:, c * GS:(c + 1) * GS], in0=sk32[:, c * GS:(c + 1) * GS],
                scalar1=skkv[:, c:c + 1], scalar2=None, op0=ALU.mult), inc="ve")
            r.wait("scalar", "ve")
            r.wait("scalar", "pe")
            r.op("scalar", lambda eng, c=c: eng.activation(
                out=ssq[:, 0:GS], in_=sfm_kkraw[:, c * GS:(c + 1) * GS],
                func=AF.Square), inc="ac")
            r.wait("tensor", "ac")
            r.op("tensor", lambda eng: eng.matmul(
                p_small[0:2, :], sblko[:], ssq[:, 0:GS], start=True, stop=True), inc="pe")
            r.wait("scalar", "pe")
            r.op("scalar", lambda eng: eng.activation(
                out=snorm[0:2, :], in_=p_small[0:2, :], func=AF.Sqrt), inc="ac")
            r.wait("vector", "ac")
            r.op("vector", lambda eng: eng.tensor_scalar(
                out=snorm[0:2, :], in0=snorm[0:2, :], scalar1=1e-12, scalar2=None,
                op0=ALU.max))
            r.op("vector", lambda eng: eng.reciprocal(
                out=snorm2[0:2, :], in_=snorm[0:2, :]))
            r.op("vector", lambda eng: eng.tensor_copy(
                out=snormbf[0:2, :], in_=snorm2[0:2, :]), inc="ve")
            r.wait("tensor", "ve")
            r.wait("tensor", "pev")
            r.op("tensor", lambda eng: eng.matmul(
                p_proj[0][:], sblkp[:], snormbf[0:2, :], start=True, stop=True), inc="pe")
            r.wait("vector", "pe")
            r.op("vector", lambda eng, c=c: eng.tensor_tensor(
                out=sfm_kk[:, c * GS:(c + 1) * GS],
                in0=sfm_kkraw[:, c * GS:(c + 1) * GS],
                in1=p_proj[0][:], op=ALU.mult), inc="pev")
            r.op("vector", lambda eng, c=c: eng.tensor_scalar(
                out=st4[:], in0=sfm_a[:, c * GS:(c + 1) * GS],
                scalar1=skav[:, c:c + 1], scalar2=sc0[:, c:c + 1],
                op0=ALU.mult, op1=ALU.add))
            r.op("vector", lambda eng, c=c: eng.tensor_tensor(
                out=sfm_k2[:, c * GS:(c + 1) * GS], in0=sk32[:, c * GS:(c + 1) * GS],
                in1=st4[:], op=ALU.mult), inc="ve")
        r.wait("sync", "ve")
        r.wait("sync", "pev")
        fm_store(sfm_kk, o_fm["kk_fm"], t0)
        fm_store(sfm_k2, o_fm["k2_fm"], t0)
        transpose_chunks_to_tm(sfm_kk, o_tm["kk_tm"], t0)
        transpose_chunks_to_tm(sfm_k2, o_tm["k2_tm"], t0)

        mk_xj(3)
        proj_fm(sW["Wv"], evac_to(sfm1))
        r.wait("vector", "pev")
        transpose_chunks_to_tm(sfm1, o_tm["v_tm"], t0)

        mk_xj(0)
        proj_fm(sW["Wr"], evac_to(sfm1))
        r.wait("sync", "pev")
        fm_store(sfm1, o_fm["r_fm"], t0)
        r.wait("vector", "pev")
        transpose_chunks_to_tm(sfm1, o_tm["r_tm"], t0)

        mk_xj(5)
        r.wait("tensor", "pev")
        for d in range(8):
            r.op("tensor", lambda eng, d=d: eng.matmul(
                p_proj[0][:], sgA[:, d * 160: d * 160 + 128],
                sxj[:, d * GS:(d + 1) * GS],
                start=(d == 0), stop=(d == 7)), inc=("pe" if d == 7 else None))
        for d in range(8):
            r.op("tensor", lambda eng, d=d: eng.matmul(
                p_small[0:32, :], sgA[:, d * 160 + 128: d * 160 + 160],
                sxj[:, d * GS:(d + 1) * GS],
                start=(d == 0), stop=(d == 7)), inc=("pe" if d == 7 else None))
        r.wait("scalar", "pe")
        r.op("scalar", lambda eng: eng.activation(
            out=smidg1[:], in_=p_proj[0][:], func=AF.Sigmoid), inc="ac")
        r.op("scalar", lambda eng: eng.activation(
            out=smidg2[:], in_=p_small[0:32, :], func=AF.Sigmoid), inc="ac")
        r.wait("tensor", "ac")
        tm_out_and_store(lambda i: smidg1[:, i * 128:(i + 1) * 128], [sgB1],
                         o_tm["g_tm"], t0, act=None,
                         extra_lhs=lambda i: smidg2[:, i * 128:(i + 1) * 128],
                         extra_rhs=sgB2)

    r.wait("sync", "st_sync", r.val("st_sync"))
    if F is None:
        r.build_block()
        r.close()
        return nc
    F.end_phase()



def build_scan(stage=6, debug=False, T=8192):
    nc = bass.Bass("TRN2", target_bir_lowering=False)
    r = Rec(nc)
    C = 128
    NCH = T // C
    NS = 8
    PCH = min(4, NCH)

    d_fm = {n: nc.dram_tensor(n, [NS, 64, T], BF16, kind="ExternalInput")
            for n in ("rF", "kkF", "aF", "k2F")}
    d_tm = {n: nc.dram_tensor(n, [NS, T, 64], BF16, kind="ExternalInput")
            for n in ("wT", "kkT", "aT", "k2T", "vT")}
    dident = nc.dram_tensor("identD", [128, 128], BF16, kind="ExternalInput")
    dut_i = nc.dram_tensor("utinclD", [128, 128], BF16, kind="ExternalInput")
    dut_s = nc.dram_tensor("utstrictD", [128, 128], BF16, kind="ExternalInput")
    dmsl = nc.dram_tensor("mslD", [128, 128], F32, kind="ExternalInput")
    dmsu = nc.dram_tensor("msuD", [128, 128], F32, kind="ExternalInput")
    dmtu = nc.dram_tensor("mtuD", [128, 128], F32, kind="ExternalInput")
    d_o = nc.dram_tensor("o", [NS, T, 64], F32, kind="ExternalOutput")
    if debug:
        dbg_names = ["dM", "dMT", "dNT", "dTLRBt", "dTLRKt", "dWT",
                     "dAtf", "dBtf", "dKtf", "dRtf", "dLam", "dLamI"]
        d_dbg = {n: nc.dram_tensor(n, [128, 129 if n == "dLam" else 128], F32,
                                   kind="ExternalOutput") for n in dbg_names}
        d_dbg2 = {n: nc.dram_tensor(n, [128, 64], F32, kind="ExternalOutput")
                  for n in ["dNV", "dUp", "dWA"]}
        d_dbg3 = {n: nc.dram_tensor(n, [64, 128], F32, kind="ExternalOutput")
                  for n in ["dRhat"]}
        d_dbg4 = {n: nc.dram_tensor(n, [64, 64], F32, kind="ExternalOutput")
                  for n in ["dIpH", "dS"]}
        d_dbg5 = nc.dram_tensor("dwin", [128, 512], BF16, kind="ExternalOutput")
        d_dbg8 = nc.dram_tensor("dut", [128, 128], BF16, kind="ExternalOutput")
        d_dbg9 = nc.dram_tensor("duts", [128, 128], BF16, kind="ExternalOutput")
        d_dbg6 = nc.dram_tensor("dlogL", [128, 128], F32, kind="ExternalOutput")
        d_dbg7 = nc.dram_tensor("dvin", [128, 512], BF16, kind="ExternalOutput")

    sident = r.sb("sident", [128, 128], BF16)
    sut_i = r.sb("sut_i", [128, 128], BF16)
    sut_s = r.sb("sut_s", [128, 128], BF16)
    smsl = r.sb("smsl", [128, 128], F32)
    smsu = r.sb("smsu", [128, 128], F32)
    smtu = r.sb("smtu", [128, 128], F32)

    in_tm = {n: [r.sb(f"i_{n}{p}", [128, PCH * 512], BF16) for p in range(2)]
             for n in ("wT", "kkT", "aT", "k2T", "vT")}
    in_fm = {n: [r.sb(f"i_{n}{p}", [128, PCH * 512], BF16) for p in range(2)]
             for n in ("rF", "kkF", "aF", "k2F")}

    slogL = r.sb("slogL", [128, 512], F32)
    slogLp = r.sb("slogLp", [128, 512], F32)
    sLamI_tm = r.sb("sLamI_tm", [128, 512], BF16)
    sLamP_tm = r.sb("sLamP_tm", [128, 512], BF16)
    sAt_tm = r.sb("sAt_tm", [128, 512], BF16)
    sBt_tm = r.sb("sBt_tm", [128, 512], BF16)
    sKt_tm = r.sb("sKt_tm", [128, 512], BF16)
    slogL_fm = [r.sb(f"slogL_fm{j}", [128, 128], F32) for j in range(4)]
    sLam_fm = [r.sb(f"sLam_fm{j}", [128, 129], BF16) for j in range(4)]
    sLamI_fm = [r.sb(f"sLamI_fm{j}", [128, 128], BF16) for j in range(4)]
    sLamC = [r.sb(f"sLamC{j}", [128, 1], F32) for j in range(4)]
    sAt_fm = [r.sb(f"sAt_fm{j}", [128, 128], BF16) for j in range(4)]
    sBt_fm = [r.sb(f"sBt_fm{j}", [128, 128], BF16) for j in range(4)]
    sKt_fm = [r.sb(f"sKt_fm{j}", [128, 128], BF16) for j in range(4)]
    sRt_fm = [r.sb(f"sRt_fm{j}", [128, 128], BF16) for j in range(4)]

    sM = r.sb("sM", [128, 128], BF16)
    sMT = r.sb("sMT", [128, 128], BF16)
    sNT = r.sb("sNT", [128, 128], BF16)
    sTLRBt = r.sb("sTLRBt", [128, 128], BF16)
    sTLRKt = r.sb("sTLRKt", [128, 128], BF16)
    sMk = [r.sb(f"sMk{i}", [128, 128], BF16) for i in range(2)]
    sMkT = [r.sb(f"sMkT{i}", [128, 128], BF16) for i in range(2)]
    sWT = [r.sb(f"sWT{i}", [128, 128], BF16) for i in range(2)]
    sNV = r.sb("sNV", [128, 64], BF16)
    sUp = r.sb("sUp", [128, 64], BF16)
    sWAs = r.sb("sWAs", [128, 64], BF16)
    sRhat = r.sb("sRhat", [64, 128], BF16)
    sIpH = r.sb("sIpH", [64, 64], BF16)
    sS = [r.sb(f"sS{s}", [64, 64], BF16) for s in range(NS)]
    so = [r.sb(f"so{p}", [128, 512], F32) for p in range(2)]

    p_cum = r.ps("p_cum", [128, 512], F32).ap()
    p_a = r.ps("p_a", [128, 512], F32).ap()[:, 0:128]
    p_b = r.ps("p_b", [128, 512], F32).ap()[:, 0:128]
    p_sm = r.ps("p_sm", [128, 512], F32).ap()[:, 0:64]
    p_rh = r.ps("p_rh", [64, 512], F32).ap()[0:64, 0:128]
    p_h = r.ps("p_h", [64, 512], F32).ap()[0:64, 0:64]
    p_o = r.ps("p_o", [128, 512], F32).ap()[:, 0:64]
    p_st = r.ps("p_st", [64, 512], F32).ap()[0:64, 0:64]

    for dst, src in ((sident, dident), (sut_i, dut_i), (sut_s, dut_s),
                     (smsl, dmsl), (smsu, dmsu), (smtu, dmtu)):
        r.dma("sync", dst[:], src.ap(), inc="cld")
    for s in range(NS):
        r.op("gpsimd", lambda eng, s=s: eng.memset(sS[s][:], 0.0), inc="gp")
    for j in range(4):
        r.op("gpsimd", lambda eng, j=j: eng.memset(sLam_fm[j][:, 0:1], 1.0), inc="gp")
    CLD = r.val("cld")
    GP0 = r.val("gp")
    for e in ("tensor", "vector", "scalar"):
        r.wait(e, "cld", CLD)
        r.wait(e, "gp", GP0)

    def load_group(g, buf):
        c0 = g * PCH
        for cc in range(PCH):
            for n, dt in (("wT", d_tm["wT"]), ("kkT", d_tm["kkT"]), ("aT", d_tm["aT"]),
                          ("k2T", d_tm["k2T"]), ("vT", d_tm["vT"])):
                src = bass.AP(dt, (c0 + cc) * C * 64,
                              [[64, 128], [T * 64, NS], [1, 64]])
                dst = in_tm[n][buf][:, cc * 512:(cc + 1) * 512].rearrange(
                    "p (s v) -> p s v", s=NS)
                r.dma("sync", dst, src, inc="ld")
            for n, df in (("rF", d_fm["rF"]), ("kkF", d_fm["kkF"]), ("aF", d_fm["aF"]),
                          ("k2F", d_fm["k2F"])):
                src = bass.AP(df, (c0 + cc) * C,
                              [[T, 128], [2 * 64 * T, 4], [1, C]])
                dst = in_fm[n][buf][:, cc * 512:(cc + 1) * 512].rearrange(
                    "p (j t) -> p j t", j=4)
                r.dma("sync", dst, src, inc="ld")

    NG = NCH // PCH
    grp_done = {}

    load_group(0, 0)
    ld_per_group = 9 * 16 * PCH

    for g in range(NG):
        buf = g % 2
        if g + 1 < NG:
            if g >= 1:
                for sem in ("pe", "ve"):
                    r.wait("sync", sem, grp_done[g - 1][sem])
            load_group(g + 1, 1 - buf)
        for e in ("tensor", "vector", "scalar"):
            r.wait(e, "ld", (g + 1) * ld_per_group)

        for cc in range(PCH):
            ch = g * PCH + cc
            ob = ch % 2
            tmsl = lambda n, lo, w: in_tm[n][buf][:, cc * 512 + lo: cc * 512 + lo + w]
            fmsl = lambda n, j: in_fm[n][buf][:, cc * 512 + j * 128: cc * 512 + (j + 1) * 128]

            if stage < 3:
                r.wait("vector", "st", max(0, r.val("st") - 16))
                for s in range(NS):
                    r.op("vector", lambda eng, s=s, ob=ob: eng.memset(
                        so[ob][:, s * 64:(s + 1) * 64], 0.0), inc="oev")
            if stage < 2:
                r.wait("sync", "oev", r.val("oev"))
                dst = bass.AP(d_o, ch * C * 64, [[64, 128], [T * 64, NS], [1, 64]])
                srcap = so[ob][:].rearrange("p (s v) -> p s v", s=NS)
                r.dma("sync", dst, srcap, inc="st")
                continue
            wT_all = tmsl("wT", 0, 512)
            kkT_all = tmsl("kkT", 0, 512)
            aT_all = tmsl("aT", 0, 512)
            k2T_all = tmsl("k2T", 0, 512)
            r.wait("tensor", "ve")
            r.op("tensor", lambda eng, w=wT_all: eng.matmul(
                p_cum[:], sut_i[:], w, start=True, stop=True), inc="pe")
            r.wait("vector", "pe")
            r.wait("vector", "ac")
            r.op("vector", lambda eng: eng.tensor_scalar(
                out=slogL[:], in0=p_cum[:], scalar1=DECAY, scalar2=None,
                op0=ALU.mult), inc="ve", nodrain=True)
            r.wait("tensor", "ve")
            r.op("tensor", lambda eng, w=wT_all: eng.matmul(
                p_cum[:], sut_s[:], w, start=True, stop=True), inc="pe")
            r.wait("vector", "pe")
            r.op("vector", lambda eng: eng.tensor_scalar(
                out=slogLp[:], in0=p_cum[:], scalar1=DECAY, scalar2=None,
                op0=ALU.mult), inc="ve", nodrain=True)
            r.wait("scalar", "ve")
            r.op("scalar", lambda eng: eng.activation(
                out=sLamI_tm[:], in_=slogL[:], func=AF.Exp, scale=-1.0), inc="ac",
                nodrain=True)
            r.op("scalar", lambda eng: eng.activation(
                out=sLamP_tm[:], in_=slogLp[:], func=AF.Exp), inc="ac", nodrain=True)
            r.wait("vector", "ac")
            r.op("vector", lambda eng, kk=kkT_all: eng.scalar_tensor_tensor(
                out=sAt_tm[:], in0=kk, scalar=-1.0,
                in1=sLamP_tm[:], op0=ALU.mult, op1=ALU.mult), inc="ve", nodrain=True)
            r.op("vector", lambda eng, kk=kkT_all, aa=aT_all: eng.tensor_tensor(
                out=sBt_tm[:], in0=kk, in1=aa, op=ALU.mult), inc="ve")
            r.op("vector", lambda eng: eng.tensor_tensor(
                out=sBt_tm[:], in0=sBt_tm[:], in1=sLamI_tm[:], op=ALU.mult),
                inc="ve")
            r.op("vector", lambda eng, k2=k2T_all: eng.tensor_tensor(
                out=sKt_tm[:], in0=k2, in1=sLamI_tm[:],
                op=ALU.mult), inc="ve", nodrain=True)

            for j in range(4):
                wT_j = tmsl("wT", j * 128, 128)
                kkF_j = fmsl("kkF", j)
                aF_j = fmsl("aF", j)
                k2F_j = fmsl("k2F", j)
                rF_j = fmsl("rF", j)
                r.wait("tensor", "ve")
                r.op("tensor", lambda eng, w=wT_j: eng.matmul(
                    p_a[:], w, sut_i[:], start=True, stop=True), inc="pe")
                r.wait("vector", "pe")
                r.wait("vector", "ac")
                r.op("vector", lambda eng, j=j: eng.tensor_scalar(
                    out=slogL_fm[j][:], in0=p_a[:], scalar1=DECAY, scalar2=None,
                    op0=ALU.mult), inc="ve")
                r.wait("scalar", "ve")
                r.op("scalar", lambda eng, j=j: eng.activation(
                    out=sLam_fm[j][:, 1:129], in_=slogL_fm[j][:], func=AF.Exp),
                    inc="ac", nodrain=True)
                r.op("scalar", lambda eng, j=j: eng.activation(
                    out=sLamI_fm[j][:], in_=slogL_fm[j][:], func=AF.Exp,
                    scale=-1.0), inc="ac", nodrain=True)
                r.op("scalar", lambda eng, j=j: eng.activation(
                    out=sLamC[j][:], in_=slogL_fm[j][:, 127:128], func=AF.Exp),
                    inc="ac", nodrain=True)
                r.wait("vector", "ac")
                r.op("vector", lambda eng, j=j, kk=kkF_j: eng.scalar_tensor_tensor(
                    out=sAt_fm[j][:], in0=kk, scalar=-1.0,
                    in1=sLam_fm[j][:, 0:128], op0=ALU.mult, op1=ALU.mult),
                    inc="ve", nodrain=True)
                r.op("vector", lambda eng, j=j, kk=kkF_j, aa=aF_j: eng.tensor_tensor(
                    out=sBt_fm[j][:], in0=kk, in1=aa, op=ALU.mult), inc="ve")
                r.op("vector", lambda eng, j=j: eng.tensor_tensor(
                    out=sBt_fm[j][:], in0=sBt_fm[j][:], in1=sLamI_fm[j][:],
                    op=ALU.mult), inc="ve")
                r.op("vector", lambda eng, j=j, k2=k2F_j: eng.tensor_tensor(
                    out=sKt_fm[j][:], in0=k2, in1=sLamI_fm[j][:],
                    op=ALU.mult), inc="ve", nodrain=True)
                r.op("vector", lambda eng, j=j, rr=rF_j: eng.tensor_tensor(
                    out=sRt_fm[j][:], in0=rr, in1=sLam_fm[j][:, 1:129],
                    op=ALU.mult), inc="ve", nodrain=True)

            r.wait("vector", "st", max(0, r.val("st") - 16 * 1))

            for s in range(NS if stage >= 3 else 0):
                j = s // 2
                lo = (s % 2) * 64
                A_f = sAt_fm[j][lo:lo + 64, :]
                B_f = sBt_fm[j][lo:lo + 64, :]
                K_f = sKt_fm[j][lo:lo + 64, :]
                R_f = sRt_fm[j][lo:lo + 64, :]
                A_t = sAt_tm[:, s * 64:(s + 1) * 64]
                B_t = sBt_tm[:, s * 64:(s + 1) * 64]
                K_t = sKt_tm[:, s * 64:(s + 1) * 64]
                V_t = tmsl("vT", s * 64, 64)
                lamc = sLamC[j][lo:lo + 64, :]

                def mm(pp, lhsT, rhs, start=True, stop=True, inc="pe"):
                    r.op("tensor", lambda eng, pp=pp, l=lhsT, rr=rhs, st=start,
                         sp=stop: eng.matmul(pp, l, rr, start=st, stop=sp), inc=inc)

                def evac_v(dst, src_p, mask=None, add=None, inc="ve", nodrain=True):
                    r.wait("vector", "pe")
                    if mask is not None:
                        r.op("vector", lambda eng, d=dst, sp=src_p, m=mask:
                             eng.tensor_tensor(out=d, in0=sp, in1=m, op=ALU.mult),
                             inc=inc, nodrain=nodrain)
                    elif add is not None:
                        r.op("vector", lambda eng, d=dst, sp=src_p, a=add:
                             eng.tensor_tensor(out=d, in0=sp, in1=a, op=ALU.add),
                             inc=inc, nodrain=nodrain)
                    else:
                        r.op("vector", lambda eng, d=dst, sp=src_p:
                             eng.tensor_copy(out=d, in_=sp), inc=inc, nodrain=nodrain)

                def evac_s(dst, src_p, inc="ac"):
                    r.wait("scalar", "pe")
                    r.op("scalar", lambda eng, d=dst, sp=src_p: eng.activation(
                        out=d, in_=sp, func=AF.Identity), inc=inc, nodrain=True)

                r.wait("tensor", "ve")
                r.wait("tensor", "ac")
                if stage < 4:
                    r.op("vector", lambda eng, s=s, ob=ob: eng.memset(
                        so[ob][:, s * 64:(s + 1) * 64], 0.0), inc="oev")
                    continue
                mm(p_a[:], A_f, B_f)
                evac_v(sM[:], p_a[:], mask=smsl[:])
                mm(p_b[:], B_f, A_f)
                evac_v(sMT[:], p_b[:], mask=smsu[:])
                r.wait("tensor", "ve")
                mm(p_a[:], K_f, A_f)
                evac_v(sNT[:], p_a[:], mask=smsu[:])
                mm(p_b[:], B_f, R_f)
                evac_v(sTLRBt[:], p_b[:], mask=smtu[:])
                r.wait("tensor", "ve")
                mm(p_a[:], K_f, R_f)
                evac_v(sTLRKt[:], p_a[:], mask=smtu[:])
                r.wait("vector", "ve")
                r.op("vector", lambda eng: eng.tensor_tensor(
                    out=sWT[0][:], in0=sMT[:], in1=sident[:], op=ALU.add), inc="ve")
                r.op("vector", lambda eng: eng.tensor_copy(out=sMk[0][:], in_=sM[:]),
                     inc="ve", nodrain=True)
                r.op("vector", lambda eng: eng.tensor_copy(out=sMkT[0][:], in_=sMT[:]),
                     inc="ve", nodrain=True)
                if stage < 5:
                    r.op("vector", lambda eng, s=s, ob=ob: eng.memset(
                        so[ob][:, s * 64:(s + 1) * 64], 0.0), inc="oev")
                    continue
                wt_cur = 0
                mk_cur = 0
                for lvl in range(6):
                    r.wait("tensor", "ve")
                    r.wait("tensor", "ac")
                    mm(p_a[:], sMkT[mk_cur][:], sMk[mk_cur][:])
                    evac_s(sMk[1 - mk_cur][:], p_a[:])
                    if lvl < 5:
                        mm(p_b[:], sMk[mk_cur][:], sMkT[mk_cur][:])
                        evac_s(sMkT[1 - mk_cur][:], p_b[:])
                    mk_cur = 1 - mk_cur
                    r.wait("tensor", "ac")
                    mm(p_a[:], sMk[mk_cur][:], sWT[wt_cur][:])
                    evac_v(sWT[1 - wt_cur][:], p_a[:], add=sWT[wt_cur][:])
                    wt_cur = 1 - wt_cur
                WT = sWT[wt_cur]
                r.wait("tensor", "ve")
                mm(p_sm[:], sNT[:], V_t)
                evac_v(sNV[:], p_sm[:])
                r.wait("tensor", "ve")
                mm(p_sm[:], WT[:], sNV[:])
                evac_v(sUp[:], p_sm[:])
                mm(p_o[:], WT[:], A_t)
                evac_v(sWAs[:], p_o[:])
                r.wait("tensor", "ve")
                mm(p_rh[:], sWAs[:], sTLRBt[:])
                evac_v(sRhat[:], p_rh[:], add=R_f)
                mm(p_h[:], sWAs[:], B_t)
                evac_v(sIpH[:], p_h[:], add=sident[0:64, 0:64])
                if stage < 6:
                    r.op("vector", lambda eng, s=s, ob=ob: eng.memset(
                        so[ob][:, s * 64:(s + 1) * 64], 0.0), inc="oev")
                    continue
                r.wait("tensor", "ve")
                mm(p_o[:], sRhat[:], sS[s][:], start=True, stop=False, inc=None)
                mm(p_o[:], sTLRBt[:], sUp[:], start=False, stop=False, inc=None)
                mm(p_o[:], sTLRKt[:], V_t, start=False, stop=True)
                evac_v(so[ob][:, s * 64:(s + 1) * 64], p_o[:], inc="oev",
                       nodrain=True)
                mm(p_st[:], sIpH[:], sS[s][:], start=True, stop=False, inc=None)
                mm(p_st[:], B_t, sUp[:], start=False, stop=False, inc=None)
                mm(p_st[:], K_t, V_t, start=False, stop=True)
                r.wait("vector", "pe")
                r.op("vector", lambda eng, s=s, lamc=lamc: eng.tensor_scalar(
                    out=sS[s][:], in0=p_st[:], scalar1=lamc, scalar2=None,
                    op0=ALU.mult), inc="ve", nodrain=True)
                if debug and ch == 0 and s == 0:
                    r.wait("gpsimd", "ve")
                    r.wait("gpsimd", "ac")
                    for nm, tile in (("dM", sM), ("dMT", sMT), ("dNT", sNT),
                                     ("dTLRBt", sTLRBt), ("dTLRKt", sTLRKt),
                                     ("dWT", sWT[wt_cur]), ("dAtf", sAt_fm[0]),
                                     ("dBtf", sBt_fm[0]), ("dKtf", sKt_fm[0]),
                                     ("dRtf", sRt_fm[0]), ("dLam", sLam_fm[0]),
                                     ("dLamI", sLamI_fm[0])):
                        r.dma("gpsimd", d_dbg[nm].ap(), tile[:], inc="dbg")
                    for nm, tile in (("dNV", sNV), ("dUp", sUp), ("dWA", sWAs)):
                        r.dma("gpsimd", d_dbg2[nm].ap(), tile[:], inc="dbg")
                    r.dma("gpsimd", d_dbg3["dRhat"].ap(), sRhat[:], inc="dbg")
                    r.dma("gpsimd", d_dbg4["dIpH"].ap(), sIpH[:], inc="dbg")
                    r.dma("gpsimd", d_dbg4["dS"].ap(), sS[0][:], inc="dbg")
                    r.dma("gpsimd", d_dbg5.ap(), in_tm["wT"][buf][:, 0:512], inc="dbg")
                    r.dma("gpsimd", d_dbg6.ap(), slogL_fm[0][:], inc="dbg")
                    r.dma("gpsimd", d_dbg7.ap(), in_tm["vT"][buf][:, 0:512], inc="dbg")
                    r.dma("gpsimd", d_dbg8.ap(), sut_i[:], inc="dbg")
                    r.dma("gpsimd", d_dbg9.ap(), sut_s[:], inc="dbg")
                    for e in ("tensor", "vector", "scalar"):
                        r.wait(e, "dbg", r.val("dbg"))

            r.wait("sync", "oev", r.val("oev"))
            dst = bass.AP(d_o, ch * C * 64, [[64, 128], [T * 64, NS], [1, 64]])
            src = so[ob][:].rearrange("p (s v) -> p s v", s=NS)
            r.dma("sync", dst, src, inc="st")

        grp_done[g] = {"pe": r.val("pe"), "ve": r.val("ve")}

    r.wait("sync", "st", r.val("st"))
    r.build_block()
    r.close()
    return nc



def build_l0post(F=None):
    if F is None:
        nc = bass.Bass("TRN2", target_bir_lowering=False)
        r = Rec(nc)
        din = lambda n, s, d: nc.dram_tensor(n, list(s), d, kind="ExternalInput")
    else:
        nc, r = F.nc, F.r
        F.begin_phase("C_")
        din = F.ext_in
    NTile = NT // 128
    TTG = 512
    NTTG = NT // TTG
    DFF = 4096

    if F is None:
        dx = din("x", [NT, D], F32)
        do = din("o", [NT, D], F32)
        dr = din("r_tm", [NT, D], BF16)
        dk2 = din("k2_tm", [NT, D], BF16)
        dv = din("v_tm", [NT, D], BF16)
        dg = din("g_tm", [NT, D], BF16)
        dxn = din("xn_fm", [D, NT], BF16)
    else:
        dx = None
        do = F.scr("o", [NT, D], F32)
        dr = F.scr("r_tm", [NT, D], BF16)
        dk2 = F.scr("k2_tm", [NT, D], BF16)
        dv = F.scr("v_tm", [NT, D], BF16)
        dg = F.scr("g_tm", [NT, D], BF16)
        dxn = F.scr("xn_fm", [D, NT], BF16)
    drk = din("rk_tile", [128, D], F32)
    dgnw = din("gnw_tile", [128, D], F32)
    dgnb = din("gnb_tile", [128, D], F32)
    dWo = din("Wo", [D, D], F32)
    drW = din("router_W", [D, 8], F32)
    drb = din("router_b", [1, 8], F32)
    dW1 = din("W1", [8, D, DFF], F32)
    db1 = din("b1", [8, DFF], F32)
    dW2 = din("W2", [8, DFF, D], F32)
    db2 = din("b2", [8, D], F32)
    dident = din("identD", [128, 128], BF16)

    if F is None:
        dx1 = nc.dram_tensor("x1", [NT, D], F32, kind="ExternalOutput")
        dxn1 = nc.dram_tensor("xn1_tm", [NT, D], BF16, kind="ExternalOutput")
        xn1_row0 = 0
    else:
        dx1 = F.scr("x1", [NT, D], F32)
        dxn1 = F.scr("xn1h_scr", [NT + 30, D], BF16)
        xn1_row0 = 30

    mk_int = (lambda n, s, d: F.scr(n, s, d)) if F is not None else \
        (lambda n, s, d: nc.dram_tensor(n, list(s), d, kind="Internal"))
    s0_scr = mk_int("s0_scr", [NT, D], F32)
    y_scr = mk_int("y_scr", [NT, D], F32)
    w1bf = mk_int("w1bf", [8, D, DFF], BF16)
    w2bf = mk_int("w2bf", [8, DFF, D], BF16)
    b1bf = mk_int("b1bf", [8, DFF], BF16)
    b2bf = mk_int("b2bf", [8, D], BF16)
    gatesT_scr = mk_int("gatesT_scr", [8, NT], BF16)

    sident = r.sb("sident", [128, 128], BF16)
    srk = r.sb("srk", [128, D], BF16)
    sgnw = r.sb("sgnw", [128, D], F32)
    sgnb = r.sb("sgnb", [128, D], F32)
    sWo = r.sb("sWo", [128, 8 * D], BF16)
    srW = r.sb("srW", [128, 8 * 8], BF16)
    srb = r.sb("srb", [1, 8], BF16)
    sones = r.sb("sones", [1, 128], BF16)
    sgT_strip = r.sb("sgT_strip", [1, 8 * 128], BF16)
    sgTrow = r.sb("sgTrow", [1, 512], BF16)

    so_t = r.sb("so_t", [128, D], F32)
    sr_t = r.sb("sr_t", [128, D], BF16)
    sk2_t = r.sb("sk2_t", [128, D], BF16)
    sv_t = r.sb("sv_t", [128, D], BF16)
    sg_t = r.sb("sg_t", [128, D], BF16)
    sxn_t = r.sb("sxn_t", [128, 8 * 128], BF16)
    ssq32 = r.sb("ssq32", [128, D], F32)
    gstat = r.sb("gstat", [128, 7 * 16], F32)
    st_a = r.sb("st_a", [128, D], F32)
    st_b = ssq32
    s_in = r.sb("s_in", [128, D], BF16)
    sfm_s = r.sb("sfm_s", [128, 8 * 128], BF16)
    sstage = r.sb("sstage", [128, D], F32)
    sstageB = st_a
    sgates = r.sb("sgates", [128, 8], F32)
    sgates_bf = r.sb("sgates_bf", [128, 8], BF16)
    sprob = r.sb("sprob", [128, 12], F32)

    sW2e2 = [r.sb(f"sW2e{p}", [128, 16 * D], BF16) for p in range(2)]
    sh = r.sb("sh", [128, 32 * TTG], BF16)
    sxn_g = r.sb("sxn_g", [128, 8 * TTG], BF16)
    sxg = r.sb("sxg", [128, 8 * TTG], BF16)
    sgbc = r.sb("sgbc", [128, TTG], BF16)
    sw1f = [r.sb(f"sw1f{p}", [128, 8 * 512], BF16) for p in range(2)]
    sb1e = r.sb("sb1e", [1, DFF], BF16)
    sb2e = r.sb("sb2e", [1, D], BF16)
    szero = r.sb("szero", [128, D], F32)
    szrow = r.sb("szrow", [1, D], BF16)

    sx_t = sstage
    ss0_t = st_b
    sy_t = st_a
    sg0_t = so_t
    sxn1_t = s_in
    lstat = r.sb("lstat", [128, 8], F32)

    if F is None:
        p_tr = r.ps("p_tr", [128, 256], BF16).ap()[:, 0:128]
        p_big = r.ps("p_big", [128, D], F32).ap()
        p_sm = r.ps("p_sm", [128, 512], F32).ap()
        p_h = [r.ps(f"p_h{i}", [128, 512], F32).ap() for i in range(2)]
    else:
        p_tr = F.pC_bf[:, 0:128]
        p_big = F.pA
        p_sm = F.pB[:, 0:512]
        p_h = [F.pD[:, 0:512], F.pD[:, 512:1024]]

    r.dma("sync", sident[:], dident.ap(), inc="wld")
    r.dma("gpsimd", srk[:], drk.ap(), inc="cast")
    r.dma("sync", sgnw[:], dgnw.ap(), inc="wld")
    r.dma("sync", sgnb[:], dgnb.ap(), inc="wld")
    for d in range(8):
        src = bass.AP(dWo, d * 128 * D, [[D, 128], [1, D]])
        r.dma("gpsimd", sWo[:, d * D:(d + 1) * D], src, inc="cast")
        src = bass.AP(drW, d * 128 * 8, [[8, 128], [1, 8]])
        r.dma("gpsimd", srW[:, d * 8:(d + 1) * 8], src, inc="cast")
    r.dma("gpsimd", srb[:], drb.ap(), inc="cast")
    r.op("gpsimd", lambda eng: eng.memset(sones[:], 1.0), inc="gp")
    r.op("gpsimd", lambda eng: eng.memset(szero[:], 0.0), inc="gp")
    r.op("gpsimd", lambda eng: eng.memset(szrow[:], 0.0), inc="gp")
    CAST2 = r.val("cast")
    r.dma("gpsimd", w1bf.ap(), dW1.ap(), inc="cast")
    r.dma("gpsimd", w2bf.ap(), dW2.ap(), inc="cast")
    r.dma("gpsimd", b1bf.ap(), db1.ap(), inc="cast")
    r.dma("gpsimd", b2bf.ap(), db2.ap(), inc="cast")
    CAST = r.val("cast")
    WLD = r.val("wld")
    GP = r.val("gp")
    for e in ("tensor", "vector", "scalar"):
        r.wait(e, "wld", WLD)
        r.wait(e, "cast", CAST2)
        r.wait(e, "gp", GP)

    r.wait("sync", "gp", GP)
    for i in range(NTile):
        r.dma("sync", y_scr.ap()[i * 128:(i + 1) * 128, :], szero[:], inc="yz")
    YZ = r.val("yz")

    seg = lambda t: t.rearrange("p (h j) -> p h j", h=16)

    def bcast16(col_ap):
        return col_ap.unsqueeze(2).broadcast_to([128, 16, 64])

    for i in range(NTile):
        rows = slice(i * 128, (i + 1) * 128)
        r.wait("sync", "ve")
        r.wait("sync", "ac")
        r.wait("sync", "pe")
        r.wait("sync", "st_sync", r.val("st_sync"))
        r.dma("sync", so_t[:], do.ap()[rows, :], inc="tl")
        r.dma("sync", sr_t[:], dr.ap()[rows, :], inc="tl")
        r.dma("sync", sk2_t[:], dk2.ap()[rows, :], inc="tl")
        r.dma("sync", sv_t[:], dv.ap()[rows, :], inc="tl")
        r.dma("sync", sg_t[:], dg.ap()[rows, :], inc="tl")
        src = bass.AP(dxn, i * 128, [[NT, 128], [128 * NT, 8], [1, 128]])
        r.dma("sync", sxn_t[:].rearrange("p (c j) -> p c j", c=8), src, inc="tl")
        TL = r.val("tl")
        r.wait("scalar", "tl", TL)
        r.wait("scalar", "ve")
        r.wait("vector", "tl", TL)

        r.op("scalar", lambda eng: eng.activation(
            out=ssq32[:], in_=so_t[:], func=AF.Square), inc="ac")
        r.wait("vector", "ac")
        r.op("vector", lambda eng: eng.tensor_reduce(
            out=gstat[:, 0:16], in_=seg(so_t[:]), axis=AX.X, op=ALU.add), inc="ve")
        r.op("vector", lambda eng: eng.tensor_reduce(
            out=gstat[:, 16:32], in_=seg(ssq32[:]), axis=AX.X, op=ALU.add), inc="ve")
        r.op("vector", lambda eng: eng.tensor_scalar(
            out=gstat[:, 32:48], in0=gstat[:, 0:16], scalar1=1.0 / 64,
            scalar2=None, op0=ALU.mult))
        r.op("vector", lambda eng: eng.tensor_scalar(
            out=gstat[:, 48:64], in0=gstat[:, 16:32], scalar1=1.0 / 64,
            scalar2=GN_EPS, op0=ALU.mult, op1=ALU.add))
        r.op("vector", lambda eng: eng.tensor_tensor(
            out=gstat[:, 64:80], in0=gstat[:, 32:48], in1=gstat[:, 32:48],
            op=ALU.mult))
        r.op("vector", lambda eng: eng.tensor_tensor(
            out=gstat[:, 80:96], in0=gstat[:, 48:64], in1=gstat[:, 64:80],
            op=ALU.subtract), inc="ve")
        r.wait("scalar", "ve")
        r.op("scalar", lambda eng: eng.activation(
            out=gstat[:, 80:96], in_=gstat[:, 80:96], func=AF.Sqrt), inc="ac")
        r.wait("vector", "ac")
        r.op("vector", lambda eng: eng.reciprocal(
            out=gstat[:, 96:112], in_=gstat[:, 80:96]))
        r.op("vector", lambda eng: eng.tensor_tensor(
            out=seg(st_a[:]), in0=seg(so_t[:]), in1=bcast16(gstat[:, 32:48]),
            op=ALU.subtract))
        r.op("vector", lambda eng: eng.tensor_tensor(
            out=seg(st_a[:]), in0=seg(st_a[:]), in1=bcast16(gstat[:, 96:112]),
            op=ALU.mult))
        r.op("vector", lambda eng: eng.tensor_tensor(
            out=st_a[:], in0=st_a[:], in1=sgnw[:], op=ALU.mult))
        r.op("vector", lambda eng: eng.tensor_tensor(
            out=st_a[:], in0=st_a[:], in1=sgnb[:], op=ALU.add), inc="ve")
        r.op("vector", lambda eng: eng.tensor_tensor(
            out=st_b[:], in0=sr_t[:], in1=sk2_t[:], op=ALU.mult))
        r.op("vector", lambda eng: eng.tensor_tensor(
            out=st_b[:], in0=st_b[:], in1=srk[:], op=ALU.mult))
        r.op("vector", lambda eng: eng.tensor_reduce(
            out=gstat[:, 0:16], in_=seg(st_b[:]), axis=AX.X, op=ALU.add))
        r.op("vector", lambda eng: eng.tensor_tensor(
            out=seg(st_b[:]), in0=bcast16(gstat[:, 0:16]),
            in1=seg(sv_t[:]), op=ALU.mult))
        r.op("vector", lambda eng: eng.tensor_tensor(
            out=st_a[:], in0=st_a[:], in1=st_b[:], op=ALU.add))
        r.op("vector", lambda eng: eng.tensor_tensor(
            out=s_in[:], in0=st_a[:], in1=sg_t[:], op=ALU.mult), inc="ve")

        for c in range(8):
            r.wait("tensor", "ve")
            r.op("tensor", lambda eng, c=c: eng.transpose(
                p_tr[:], s_in[:, c * 128:(c + 1) * 128], sident[:]), inc="pe")
            r.wait("vector", "pe")
            r.op("vector", lambda eng, c=c: eng.tensor_copy(
                out=sfm_s[:, c * 128:(c + 1) * 128], in_=p_tr[:]), inc="ve",
                nodrain=True)
        r.wait("tensor", "ve")
        r.wait("tensor", "ac")
        for d in range(8):
            for hf in range(2):
                r.op("tensor", lambda eng, d=d, hf=hf: eng.matmul(
                    p_big[:, hf * 512:(hf + 1) * 512],
                    sfm_s[:, d * 128:(d + 1) * 128],
                    sWo[:, d * D + hf * 512: d * D + (hf + 1) * 512],
                    start=(d == 0), stop=(d == 7)),
                    inc=("pe" if (d == 7 and hf == 1) else None))
        r.wait("scalar", "pe")
        r.op("scalar", lambda eng: eng.activation(
            out=sstage[:], in_=p_big[:], func=AF.Identity), inc="ac")
        r.wait("sync", "ac")
        r.dma("sync", s0_scr.ap()[rows, :], sstage[:], inc="st_sync")

        r.wait("tensor", "ve")
        for d in range(8):
            r.op("tensor", lambda eng, d=d: eng.matmul(
                p_sm[:, 0:8], sxn_t[:, d * 128:(d + 1) * 128],
                srW[:, d * 8:(d + 1) * 8], start=(d == 0), stop=False), inc=None)
        r.op("tensor", lambda eng: eng.matmul(
            p_sm[:, 0:8], sones[0:1, :], srb[:], start=False, stop=True), inc="pe")
        r.wait("vector", "pe")
        r.op("vector", lambda eng: eng.tensor_reduce(
            out=sprob[:, 8:9], in_=p_sm[:, 0:8], axis=AX.X, op=ALU.max))
        r.op("vector", lambda eng: eng.tensor_scalar(
            out=sprob[:, 9:10], in0=sprob[:, 8:9], scalar1=-1.0, scalar2=None,
            op0=ALU.mult))
        r.op("vector", lambda eng: eng.tensor_copy(
            out=sprob[:, 0:8].rearrange("p a -> p a"), in_=p_sm[:, 0:8]), inc="ve")
        r.wait("scalar", "ve")
        r.op("scalar", lambda eng: eng.activation(
            out=sgates[:], in_=sprob[:, 0:8], func=AF.Exp,
            bias=sprob[:, 9:10]), inc="ac")
        r.wait("vector", "ac")
        r.op("vector", lambda eng: eng.tensor_reduce(
            out=sprob[:, 10:11], in_=sgates[:], axis=AX.X, op=ALU.max))
        r.op("vector", lambda eng: eng.tensor_reduce(
            out=sprob[:, 11:12], in_=sgates[:], axis=AX.X, op=ALU.add))
        r.op("vector", lambda eng: eng.reciprocal(
            out=sprob[:, 11:12], in_=sprob[:, 11:12]))
        r.op("vector", lambda eng: eng.tensor_scalar(
            out=st_b[:, 0:8], in0=sgates[:], scalar1=sprob[:, 10:11], scalar2=None,
            op0=ALU.is_ge))
        r.op("vector", lambda eng: eng.tensor_tensor(
            out=sgates[:], in0=sgates[:], in1=st_b[:, 0:8], op=ALU.mult))
        r.op("vector", lambda eng: eng.tensor_scalar(
            out=sgates_bf[:], in0=sgates[:], scalar1=sprob[:, 11:12], scalar2=None,
            op0=ALU.mult), inc="ve")
        r.wait("vector", "gts", r.val("gts"))
        for e in range(8):
            r.wait("tensor", "ve")
            r.op("tensor", lambda eng, e=e: eng.transpose(
                p_tr[0:1, :], sgates_bf[:, e:e + 1], sident[:]), inc="pe")
            r.wait("vector", "pe")
            r.op("vector", lambda eng, e=e: eng.tensor_copy(
                out=sgT_strip[0:1, e * 128:(e + 1) * 128], in_=p_tr[0:1, 0:128]),
                inc="ve", nodrain=True)
        r.wait("sync", "ve")
        r.dma("sync", bass.AP(gatesT_scr, i * 128, [[NT, 8], [1, 128]]),
              sgT_strip[0:1, :].rearrange("p (e j) -> p e j", e=8), inc="gts")

    r.wait("tensor", "cast", CAST)
    r.wait("sync", "yz", YZ)
    w1cnt = [0]
    w2cnt = [0]
    PHB = {"pe": r.val("pe"), "ve": r.val("ve"), "ac": r.val("ac")}
    for e in range(8):
        r.wait("sync", "pe")
        r.dma("sync", sb1e[:], b1bf.ap()[e:e + 1, :], inc="w2l")
        r.dma("sync", sb2e[:], b2bf.ap()[e:e + 1, :], inc="w2l")
        r.wait("tensor", "w2l", r.val("w2l"))
        for t in range(NTTG):
            cols = slice(t * TTG, (t + 1) * TTG)
            r.wait("sync", "ve")
            src = bass.AP(dxn, t * TTG, [[NT, 128], [128 * NT, 8], [1, TTG]])
            r.dma("sync", sxn_g[:].rearrange("p (c j) -> p c j", c=8), src, inc="xnl")
            XNL = r.val("xnl")
            r.wait("sync", "gts", r.val("gts"))
            r.wait("sync", "pe")
            r.dma("sync", sgTrow[:], gatesT_scr.ap()[e:e + 1, t * TTG:(t + 1) * TTG],
                  inc="gtl")
            r.wait("tensor", "gtl", r.val("gtl"))
            r.wait("tensor", "ve")
            r.op("tensor", lambda eng: eng.matmul(
                p_sm[:], sones[0:1, :], sgTrow[0:1, :],
                start=True, stop=True), inc="pe")
            r.wait("vector", "pe")
            r.op("vector", lambda eng: eng.tensor_copy(
                out=sgbc[:], in_=p_sm[:]), inc="ve", nodrain=True)
            r.wait("vector", "xnl", XNL)
            for c in range(8):
                r.op("vector", lambda eng, c=c: eng.tensor_tensor(
                    out=sxg[:, c * TTG:(c + 1) * TTG],
                    in0=sxn_g[:, c * TTG:(c + 1) * TTG], in1=sgbc[:],
                    op=ALU.mult), inc="ve", nodrain=True)
            r.op("vector", lambda eng: eng.engine_nop() if False else
                 eng.tensor_copy(out=sgbc[0:1, 0:1], in_=sgbc[0:1, 0:1]), inc="ve")
            for fb in range(8):
                par = fb % 2
                if w1cnt[0] >= 2:
                    r.wait("sync", "pe")
                w1cnt[0] += 1
                r.dma("sync", sw1f[par][:].rearrange("p (d j) -> p d j", d=8),
                      bass.AP(w1bf, e * D * DFF + fb * 512,
                              [[DFF, 128], [128 * DFF, 8], [1, 512]]),
                      inc="w1l")
                r.wait("tensor", "w1l", r.val("w1l"))
                if fb == 0:
                    r.wait("tensor", "ve")
                for ff in range(4):
                    f = fb * 4 + ff
                    phh = p_h[f % 2]
                    if f >= 2:
                        r.wait("tensor", "ac")
                    for d in range(8):
                        r.op("tensor", lambda eng, ff=ff, d=d, par=par, phh=phh:
                             eng.matmul(
                                 phh[:], sw1f[par][:, d * 512 + ff * 128: d * 512 + (ff + 1) * 128],
                                 sxg[:, d * TTG:(d + 1) * TTG],
                                 start=(d == 0), stop=False), inc=None)
                    r.op("tensor", lambda eng, f=f, phh=phh: eng.matmul(
                        phh[:], sb1e[0:1, f * 128:(f + 1) * 128],
                        sgTrow[0:1, :],
                        start=False, stop=True), inc="pe")
                    r.wait("scalar", "pe")
                    r.op("scalar", lambda eng, f=f, phh=phh: eng.activation(
                        out=sh[:, f * TTG:(f + 1) * TTG], in_=phh[:],
                        func=AF.Relu), inc="ac", nodrain=True)
            for fh in range(2):
                par2 = (t * 2 + fh) % 2
                sW2e = sW2e2[par2]
                if w2cnt[0] >= 2:
                    r.wait("sync", "pe")
                w2cnt[0] += 1
                r.dma("sync", sW2e[:].rearrange("p (f j) -> p f j", f=16),
                      bass.AP(w2bf, e * DFF * D + fh * 16 * 128 * D,
                              [[D, 128], [128 * D, 16], [1, D]]), inc="w2l")
                r.wait("tensor", "w2l", r.val("w2l"))
                for ti in range(TTG // 128):
                    r.wait("tensor", "ac")
                    for f in range(16):
                        fg = fh * 16 + f
                        for hf in range(2):
                            r.op("tensor", lambda eng, f=f, fg=fg, ti=ti, hf=hf,
                                 sW2e=sW2e: eng.matmul(
                                     p_big[:, hf * 512:(hf + 1) * 512],
                                     sh[:, fg * TTG + ti * 128: fg * TTG + (ti + 1) * 128],
                                     sW2e[:, f * D + hf * 512: f * D + (hf + 1) * 512],
                                     start=(f == 0), stop=False), inc=None)
                    for hf in range(2):
                        if fh == 1:
                            r.op("tensor", lambda eng, ti=ti, hf=hf:
                                 eng.matmul(
                                     p_big[:, hf * 512:(hf + 1) * 512],
                                     sgTrow[0:1, ti * 128:(ti + 1) * 128],
                                     sb2e[0:1, hf * 512:(hf + 1) * 512],
                                     start=False, stop=True),
                                 inc=("pe" if hf == 1 else None))
                        else:
                            r.op("tensor", lambda eng, hf=hf: eng.matmul(
                                p_big[:, hf * 512:(hf + 1) * 512],
                                sones[0:1, 0:128],
                                szero.bitcast(BF16)[0:1, 0:512] if False else
                                szrow[0:1, hf * 512:(hf + 1) * 512],
                                start=False, stop=True),
                                inc=("pe" if hf == 1 else None))
                    gr = slice(t * TTG + ti * 128, t * TTG + ti * 128 + 128)
                    stg = sstageB if (w2cnt[0] * 4 + ti) % 2 else sstage
                    r.wait("scalar", "pe")
                    r.wait("scalar", "rmw", max(0, r.val("rmw") - 16))
                    r.op("scalar", lambda eng, stg=stg: eng.activation(
                        out=stg[:], in_=p_big[:], func=AF.Identity), inc="ac")
                    r.wait("gpsimd", "ac")
                    r.dma("gpsimd", y_scr.ap()[gr, :], stg[:],
                          accum_op=ALU.add, inc="rmw")
    r.wait("sync", "rmw", r.val("rmw"))

    for i in range(NTile):
        rows = slice(i * 128, (i + 1) * 128)
        r.wait("sync", "ve")
        r.wait("sync", "ac")
        r.wait("sync", "st_sync", r.val("st_sync"))
        if F is None:
            r.dma("sync", sx_t[:], dx.ap()[rows, :], inc="dl")
        else:
            r.dma("sync", sx_t[:],
                  F.ext["xh"].ap()[1 + i * 128: 1 + (i + 1) * 128, :], inc="dl")
        r.dma("sync", ss0_t[:], s0_scr.ap()[rows, :], inc="dl")
        r.dma("sync", sy_t[:], y_scr.ap()[rows, :], inc="dl")
        DL = r.val("dl")
        r.wait("scalar", "dl", DL)
        r.op("scalar", lambda eng: eng.activation(
            out=sg0_t[:], in_=sy_t[:], func=AF.Silu), inc="ac")
        r.wait("vector", "ac")
        r.wait("vector", "dl", DL)
        r.op("vector", lambda eng: eng.tensor_tensor(
            out=sg0_t[:], in0=sg0_t[:], in1=ss0_t[:], op=ALU.mult))
        r.op("vector", lambda eng: eng.tensor_tensor(
            out=sx_t[:], in0=sx_t[:], in1=sg0_t[:], op=ALU.add), inc="ve")
        r.wait("sync", "ve")
        r.dma("sync", dx1.ap()[rows, :], sx_t[:], inc="st_sync")
        r.op("scalar", lambda eng: eng.activation(
            out=ssq32[:], in_=sx_t[:], func=AF.Square,
            accum_out=lstat[:, 1:2]), inc="ac")
        r.op("scalar", lambda eng: eng.activation(
            out=ssq32[:], in_=sx_t[:], func=AF.Identity,
            accum_out=lstat[:, 0:1]), inc="ac")
        r.wait("vector", "ac")
        r.op("vector", lambda eng: eng.tensor_scalar(
            out=lstat[:, 2:3], in0=lstat[:, 0:1], scalar1=1.0 / D,
            scalar2=None, op0=ALU.mult))
        r.op("vector", lambda eng: eng.tensor_scalar(
            out=lstat[:, 3:4], in0=lstat[:, 1:2], scalar1=1.0 / D,
            scalar2=LN_EPS, op0=ALU.mult, op1=ALU.add))
        r.op("vector", lambda eng: eng.tensor_scalar(
            out=lstat[:, 5:6], in0=lstat[:, 2:3], scalar1=lstat[:, 2:3],
            scalar2=None, op0=ALU.mult))
        r.op("vector", lambda eng: eng.tensor_tensor(
            out=lstat[:, 4:5], in0=lstat[:, 3:4], in1=lstat[:, 5:6],
            op=ALU.subtract), inc="ve")
        r.wait("scalar", "ve")
        r.op("scalar", lambda eng: eng.activation(
            out=lstat[:, 6:7], in_=lstat[:, 4:5], func=AF.Sqrt), inc="ac")
        r.wait("vector", "ac")
        r.op("vector", lambda eng: eng.reciprocal(
            out=lstat[:, 7:8], in_=lstat[:, 6:7]))
        r.op("vector", lambda eng: eng.tensor_scalar(
            out=sxn1_t[:], in0=sx_t[:], scalar1=lstat[:, 2:3],
            scalar2=lstat[:, 7:8], op0=ALU.subtract, op1=ALU.mult), inc="ve")
        r.wait("sync", "ve")
        r.dma("sync", dxn1.ap()[xn1_row0 + i * 128: xn1_row0 + (i + 1) * 128, :],
              sxn1_t[:], inc="st_sync")

    r.wait("sync", "st_sync", r.val("st_sync"))
    if F is None:
        r.build_block()
        r.close()
        return nc
    F.end_phase()




def build_l1(debug=False, F=None):
    if F is None:
        nc = bass.Bass("TRN2", target_bir_lowering=False)
        r = Rec(nc)
        din = lambda n, s, d: nc.dram_tensor(n, list(s), d, kind="ExternalInput")
    else:
        nc, r = F.nc, F.r
        F.begin_phase("D_")
        din = F.ext_in
    NTile = NT // 128
    NCT = NT // 64
    E1, F1, FSH = 64, 512, 512
    KS, PAD = 31, 30
    tg_sizes = [384] * 10 + [256]

    if F is None:
        dx1 = din("x1", [NT, D], F32)
        dxn1h = din("xn1h", [NT + PAD, D], BF16)
    else:
        dx1 = F.scr("x1", [NT, D], F32)
        dxn1h = F.scr("xn1h_scr", [NT + PAD, D], BF16)
    dwlin = din("w_lin", [496, D], F32)
    dgw = din("gate_w", [E1, D], F32)
    dgbias = din("gbias_tile", [128, E1], F32)
    dWeg = din("We_gate", [E1, D, F1], F32)
    dWeu = din("We_up", [E1, D, F1], F32)
    dWed = din("We_down", [E1, F1, D], F32)
    dWsg = din("Ws_gate", [D, FSH], F32)
    dWsu = din("Ws_up", [D, FSH], F32)
    dWsd = din("Ws_down", [FSH, D], F32)
    dident = din("identD", [128, 128], BF16)
    dbmask = din("bmask", [64, 16 * 94], BF16)
    dout = nc.dram_tensor("out", [NT, D], F32, kind="ExternalOutput")
    if debug:
        d_s1 = nc.dram_tensor("d_s1", [NT, D], F32, kind="ExternalOutput")
        d_wd = nc.dram_tensor("d_wd", [E1, NT], F32, kind="ExternalOutput")
        d_wsm = nc.dram_tensor("d_wsm", [NT, 496], F32, kind="ExternalOutput")

    mk_int = (lambda n, s, d: F.scr(n, s, d)) if F is not None else \
        (lambda n, s, d: nc.dram_tensor(n, list(s), d, kind="Internal"))
    w_scr = mk_int("w_scr", [NT, 496], F32)
    s1_scr = mk_int("s1_scr", [NT, D], F32)
    wdT_scr = mk_int("wdT_scr", [E1, NT], BF16)
    xn1fm_scr = mk_int("xn1fm_scr", [D, NT], BF16)
    wgbf = mk_int("wgbf", [E1, D, F1], BF16)
    wubf = mk_int("wubf", [E1, D, F1], BF16)
    wdbf = mk_int("wdbf", [E1, F1, D], BF16)

    if F is None:
        pA = r.ps("pA", [128, D], F32).ap()
        pB = r.ps("pB", [128, D], F32).ap()
        pC = r.ps("pC", [128, D], F32).ap()
        pD = r.ps("pD", [128, D], F32).ap()
        p_tr = pC.bitcast(BF16)[:, 0:128]
    else:
        pA, pB, pC, pD = F.pA, F.pB, F.pC, F.pD
        p_tr = F.pC_bf[:, 0:128]
    p_conv = pB[0:64, 0:64]

    sident = r.sb("sident", [128, 128], BF16)
    sbmask = r.sb("sbmask", [64, 16 * 94], BF16)
    sones = r.sb("sones", [1, 128], BF16)
    sxn_strip = r.sb("sxn_strip", [128, 8 * 128], BF16)
    sxn_tg = r.sb("sxn_tg", [128, 8 * 384], BF16)
    swlinT = r.sb("swlinT", [128, 8 * 496], BF16)
    sgwT = r.sb("sgwT", [128, 8 * E1], BF16)
    sgbias = r.sb("sgbias", [128, E1], F32)
    swltmp = r.sb("swltmp", [128, D], BF16)
    sxt = r.sb("sxt", [128, D], BF16)
    swsm = r.sb("swsm", [128, 496], F32)
    sstat = r.sb("sstat", [128, 48], F32)
    sscores = r.sb("sscores", [128, E1], F32)
    ssc = r.sb("ssc", [128, E1], F32)
    sgm = r.sb("sgm", [128, 8], F32)
    smask64 = r.sb("smask64", [128, E1], F32)
    ssel = r.sb("ssel", [128, E1], F32)
    spick = r.sb("spick", [128, E1], F32)
    swd = r.sb("swd", [128, E1], BF16)
    sband = r.sb("sband", [64, 16 * 94], BF16)
    sbandT = r.sb("sbandT", [94, 64], BF16)
    sxh = r.sb("sxh", [94, D], BF16)
    ss1t = r.sb("ss1t", [64, D], F32)
    swg = [r.sb(f"swg{p}", [128, 8 * F1], BF16) for p in range(2)]
    swu = [r.sb(f"swu{p}", [128, 8 * F1], BF16) for p in range(2)]
    swdn = [r.sb(f"swdn{p}", [128, 4 * D], BF16) for p in range(2)]
    swsg = r.sb("swsg", [128, 8 * FSH], BF16)
    swsu = r.sb("swsu", [128, 8 * FSH], BF16)
    swsd = r.sb("swsd", [128, 4 * D], BF16)
    swdTrow = r.sb("swdTrow", [1, E1 * 384], BF16)
    swdT64 = r.sb("swdT64", [64, 384], BF16)
    sbc = r.sb("sbc", [128, 384], BF16)
    sgl = r.sb("sgl", [128, 4 * 384], BF16)
    sh1 = r.sb("sh1", [128, 4 * 384], BF16)
    sg1 = r.sb("sg1", [128, 3 * D], BF16)
    sx1t = r.sb("sx1t", [128, D], F32)
    ss1c = r.sb("ss1c", [128, D], F32)

    r.dma("sync", sident[:], dident.ap(), inc="wld")
    r.dma("sync", sbmask[:], dbmask.ap(), inc="wld")
    r.dma("sync", sgbias[:], dgbias.ap(), inc="wld")
    r.op("gpsimd", lambda eng: eng.memset(sones[:], 1.0), inc="gp")
    for rt in range(4):
        rows = 128 if rt < 3 else 112
        r.wait("gpsimd", "pe")
        r.dma("gpsimd", swltmp[0:rows, :], dwlin.ap()[rt * 128: rt * 128 + rows, :],
              inc="cwl")
        CW = r.val("cwl")
        r.wait("tensor", "cwl", CW)
        for c in range(8):
            r.wait("tensor", "ve")
            r.op("tensor", lambda eng, rt=rt, c=c, rows=rows: eng.transpose(
                p_tr[0:128, 0:rows], swltmp[0:rows, c * 128:(c + 1) * 128],
                sident[0:rows, 0:rows]), inc="pe")
            r.wait("vector", "pe")
            r.op("vector", lambda eng, rt=rt, c=c, rows=rows: eng.tensor_copy(
                out=swlinT[:, c * 496 + rt * 128: c * 496 + rt * 128 + rows],
                in_=p_tr[0:128, 0:rows]), inc="ve", nodrain=True)
    r.wait("gpsimd", "pe")
    r.dma("gpsimd", swltmp[0:64, :], dgw.ap(), inc="cwl")
    r.wait("tensor", "cwl", r.val("cwl"))
    for c in range(8):
        r.wait("tensor", "ve")
        r.op("tensor", lambda eng, c=c: eng.transpose(
            p_tr[0:128, 0:64], swltmp[0:64, c * 128:(c + 1) * 128],
            sident[0:64, 0:64]), inc="pe")
        r.wait("vector", "pe")
        r.op("vector", lambda eng, c=c: eng.tensor_copy(
            out=sgwT[:, c * E1:(c + 1) * E1], in_=p_tr[0:128, 0:64]), inc="ve",
            nodrain=True)
    for d in range(8):
        r.dma("gpsimd", swsg[:, d * FSH:(d + 1) * FSH],
              bass.AP(dWsg, d * 128 * FSH, [[FSH, 128], [1, FSH]]), inc="cwl")
        r.dma("gpsimd", swsu[:, d * FSH:(d + 1) * FSH],
              bass.AP(dWsu, d * 128 * FSH, [[FSH, 128], [1, FSH]]), inc="cwl")
    for f in range(4):
        r.dma("gpsimd", swsd[:, f * D:(f + 1) * D],
              bass.AP(dWsd, f * 128 * D, [[D, 128], [1, D]]), inc="cwl")
    CWL = r.val("cwl")
    if DBG_WLINT:
        d_wlt = nc.dram_tensor("d_wlt", [128, 8 * 496], BF16, kind="ExternalOutput")
        d_gwt = nc.dram_tensor("d_gwt", [128, 8 * E1], BF16, kind="ExternalOutput")
        r.wait("gpsimd", "ve")
        r.dma("gpsimd", d_wlt.ap(), swlinT[:], inc="wltd")
        r.dma("gpsimd", d_gwt.ap(), sgwT[:], inc="wltd")
        r.wait("sync", "wltd", r.val("wltd"))
    r.dma("gpsimd", wgbf.ap(), dWeg.ap(), inc="cast")
    r.dma("gpsimd", wubf.ap(), dWeu.ap(), inc="cast")
    r.dma("gpsimd", wdbf.ap(), dWed.ap(), inc="cast")
    CAST = r.val("cast")
    WLD = r.val("wld")
    for e in ("tensor", "vector", "scalar"):
        r.wait(e, "wld", WLD)
        r.wait(e, "cwl", CWL)
        r.wait(e, "gp", r.val("gp"))

    seg31 = lambda t: t.rearrange("p (h k) -> p h k", h=16)
    seg8 = lambda t: t.rearrange("p (g k) -> p g k", g=8)

    for i in range(NTile):
        r.wait("sync", "ve")
        r.wait("sync", "xfs", r.val("xfs"))
        r.dma("sync", sxt[:], dxn1h.ap()[PAD + i * 128: PAD + (i + 1) * 128, :],
              inc="x1l")
        r.wait("tensor", "x1l", r.val("x1l"))
        for c in range(8):
            r.wait("tensor", "ve")
            r.op("tensor", lambda eng, c=c: eng.transpose(
                p_tr[:], sxt[:, c * 128:(c + 1) * 128], sident[:]), inc="pe")
            r.wait("vector", "pe")
            r.op("vector", lambda eng, c=c: eng.tensor_copy(
                out=sxn_strip[:, c * 128:(c + 1) * 128],
                in_=p_tr[:]), inc="ve", nodrain=True)
        r.wait("sync", "ve")
        r.dma("sync", bass.AP(xn1fm_scr, i * 128, [[NT, 128], [128 * NT, 8], [1, 128]]),
              sxn_strip[:].rearrange("p (c j) -> p c j", c=8), inc="xfs")
    XFS = r.val("xfs")

    for i in range(NTile):
        cols = slice(i * 128, (i + 1) * 128)
        r.wait("sync", "xfs", XFS)
        r.wait("sync", "pe")
        r.dma("sync", sxn_strip[:].rearrange("p (c j) -> p c j", c=8),
              bass.AP(xn1fm_scr, i * 128, [[NT, 128], [128 * NT, 8], [1, 128]]),
              inc="xtl")
        r.wait("tensor", "xtl", r.val("xtl"))
        r.wait("tensor", "ve")
        r.wait("tensor", "ac")
        for d in range(8):
            r.op("tensor", lambda eng, d=d: eng.matmul(
                pA[:, 0:496], sxn_strip[:, d * 128:(d + 1) * 128],
                swlinT[:, d * 496:(d + 1) * 496], start=(d == 0), stop=(d == 7)),
                inc=("pe" if d == 7 else None))
        r.wait("vector", "pe")
        r.wait("vector", "st_sync", r.val("st_sync"))
        r.op("vector", lambda eng: eng.tensor_reduce(
            out=sstat[:, 0:16], in_=seg31(pA[:, 0:496]), axis=AX.X, op=ALU.max))
        r.op("vector", lambda eng: eng.tensor_tensor(
            out=seg31(swsm[:]), in0=seg31(pA[:, 0:496]),
            in1=sstat[:, 0:16].unsqueeze(2).broadcast_to([128, 16, 31]),
            op=ALU.subtract), inc="ve")
        r.wait("scalar", "ve")
        r.op("scalar", lambda eng: eng.activation(
            out=swsm[:], in_=swsm[:], func=AF.Exp), inc="ac")
        r.wait("vector", "ac")
        r.op("vector", lambda eng: eng.tensor_reduce(
            out=sstat[:, 16:32], in_=seg31(swsm[:]), axis=AX.X, op=ALU.add))
        r.op("vector", lambda eng: eng.reciprocal(
            out=sstat[:, 32:48], in_=sstat[:, 16:32]))
        r.op("vector", lambda eng: eng.tensor_tensor(
            out=seg31(swsm[:]), in0=seg31(swsm[:]),
            in1=sstat[:, 32:48].unsqueeze(2).broadcast_to([128, 16, 31]),
            op=ALU.mult), inc="ve")
        r.wait("sync", "ve")
        r.dma("sync", w_scr.ap()[cols, :], swsm[:], inc="st_sync")

        r.wait("tensor", "ve")
        for d in range(8):
            r.op("tensor", lambda eng, d=d: eng.matmul(
                pA[:, 0:64], sxn_strip[:, d * 128:(d + 1) * 128],
                sgwT[:, d * E1:(d + 1) * E1], start=(d == 0), stop=(d == 7)),
                inc=("pe" if d == 7 else None))
        r.wait("scalar", "pe")
        r.op("scalar", lambda eng: eng.activation(
            out=sscores[:], in_=pA[:, 0:64], func=AF.Sigmoid), inc="ac")
        r.wait("vector", "ac")
        r.op("vector", lambda eng: eng.tensor_tensor(
            out=ssc[:], in0=sscores[:], in1=sgbias[:], op=ALU.add), inc="ve")
        r.op("vector", lambda eng: eng.tensor_reduce(
            out=sstat[:, 0:8], in_=seg8(ssc[:]), axis=AX.X, op=ALU.max))
        r.op("vector", lambda eng: eng.tensor_tensor(
            out=seg8(smask64[:]), in0=seg8(ssc[:]),
            in1=sstat[:, 0:8].unsqueeze(2).broadcast_to([128, 8, 8]),
            op=ALU.is_ge))
        r.op("vector", lambda eng: eng.scalar_tensor_tensor(
            out=smask64[:], in0=smask64[:], scalar=-1e30, in1=ssc[:],
            op0=ALU.mult, op1=ALU.add))
        r.op("vector", lambda eng: eng.tensor_reduce(
            out=sstat[:, 8:16], in_=seg8(smask64[:]), axis=AX.X, op=ALU.max))
        r.op("vector", lambda eng: eng.tensor_tensor(
            out=sgm[:], in0=sstat[:, 0:8], in1=sstat[:, 8:16], op=ALU.add),
            inc="ve")
        r.op("vector", lambda eng: eng.memset(sstat[:, 16:24], 0.0))
        for it in range(4):
            r.op("vector", lambda eng: eng.tensor_reduce(
                out=sstat[:, 24:25], in_=sgm[:], axis=AX.X, op=ALU.max))
            r.op("vector", lambda eng: eng.tensor_scalar(
                out=sstat[:, 32:40], in0=sgm[:], scalar1=sstat[:, 24:25],
                scalar2=None, op0=ALU.is_ge))
            r.op("vector", lambda eng: eng.tensor_tensor(
                out=sstat[:, 16:24], in0=sstat[:, 16:24], in1=sstat[:, 32:40],
                op=ALU.max))
            r.op("vector", lambda eng: eng.scalar_tensor_tensor(
                out=sgm[:], in0=sstat[:, 32:40], scalar=-1e30, in1=sgm[:],
                op0=ALU.mult, op1=ALU.add))
        r.op("vector", lambda eng: eng.tensor_scalar(
            out=sstat[:, 32:40], in0=sstat[:, 16:24], scalar1=1e30,
            scalar2=-1e30, op0=ALU.mult, op1=ALU.add))
        r.op("vector", lambda eng: eng.tensor_tensor(
            out=seg8(smask64[:]), in0=seg8(ssc[:]),
            in1=sstat[:, 32:40].unsqueeze(2).broadcast_to([128, 8, 8]),
            op=ALU.add))
        r.op("vector", lambda eng: eng.memset(ssel[:], 0.0))
        for it in range(8):
            r.op("vector", lambda eng: eng.tensor_reduce(
                out=sstat[:, 24:25], in_=smask64[:], axis=AX.X, op=ALU.max))
            r.op("vector", lambda eng: eng.tensor_scalar(
                out=spick[:], in0=smask64[:], scalar1=sstat[:, 24:25],
                scalar2=None, op0=ALU.is_ge))
            r.op("vector", lambda eng: eng.tensor_tensor(
                out=ssel[:], in0=ssel[:], in1=spick[:], op=ALU.max))
            r.op("vector", lambda eng: eng.scalar_tensor_tensor(
                out=smask64[:], in0=spick[:], scalar=-1e30, in1=smask64[:],
                op0=ALU.mult, op1=ALU.add))
        r.op("vector", lambda eng: eng.tensor_tensor(
            out=ssel[:], in0=ssel[:], in1=sscores[:], op=ALU.mult))
        r.op("vector", lambda eng: eng.tensor_reduce(
            out=sstat[:, 24:25], in_=ssel[:], axis=AX.X, op=ALU.add))
        r.op("vector", lambda eng: eng.tensor_scalar(
            out=sstat[:, 25:26], in0=sstat[:, 24:25], scalar1=1e-20, scalar2=None,
            op0=ALU.add))
        r.op("vector", lambda eng: eng.reciprocal(
            out=sstat[:, 26:27], in_=sstat[:, 25:26]))
        r.op("vector", lambda eng: eng.tensor_scalar(
            out=swd[:], in0=ssel[:], scalar1=sstat[:, 26:27], scalar2=2.5,
            op0=ALU.mult, op1=ALU.mult), inc="ve")
        r.wait("tensor", "ve")
        r.op("tensor", lambda eng: eng.transpose(
            p_tr[0:64, :], swd[:], sident[:]), inc="pe")
        r.wait("vector", "pe")
        r.op("vector", lambda eng: eng.tensor_copy(
            out=swdT64[0:64, 0:128], in_=p_tr[0:64, 0:128]), inc="wdv",
            nodrain=True)
        r.wait("sync", "wdv", r.val("wdv"))
        r.dma("sync", bass.AP(wdT_scr, i * 128, [[NT, 64], [1, 128]]),
              swdT64[0:64, 0:128], inc="gts")

    r.wait("gpsimd", "gts", r.val("gts"))
    r.wait("gpsimd", "st_sync", r.val("st_sync"))
    for i in range(NCT):
        r.wait("gpsimd", "ve")
        r.wait("gpsimd", "pe")
        src = bass.AP(w_scr, i * 64 * 496, [[495, 64], [31, 16], [1, 94]])
        r.dma("gpsimd", sband[:].rearrange("p (h s) -> p h s", h=16), src, inc="bl")
        r.wait("sync", "ve")
        r.wait("sync", "pe")
        r.dma("sync", sxh[:], dxn1h.ap()[i * 64: i * 64 + 94, :], inc="bl")
        BL = r.val("bl")
        r.wait("vector", "bl", BL)
        r.op("vector", lambda eng: eng.tensor_tensor(
            out=sband[:], in0=sband[:], in1=sbmask[:], op=ALU.mult), inc="ve")
        for h in range(16):
            r.wait("tensor", "ve")
            r.wait("tensor", "ac")
            r.op("tensor", lambda eng, h=h: eng.transpose(
                p_tr[0:94, 0:64], sband[:, h * 94:(h + 1) * 94],
                sident[0:64, 0:64]), inc="pe")
            r.wait("vector", "pe")
            r.op("vector", lambda eng: eng.tensor_copy(
                out=sbandT[:], in_=p_tr[0:94, 0:64]), inc="ve", nodrain=True)
            r.wait("tensor", "ve")
            r.op("tensor", lambda eng, h=h: eng.matmul(
                p_conv, sbandT[:], sxh[:, h * 64:(h + 1) * 64],
                start=True, stop=True), inc="pe")
            r.wait("scalar", "pe")
            r.wait("scalar", "st_sync", r.val("st_sync"))
            r.op("scalar", lambda eng, h=h: eng.activation(
                out=ss1t[0:64, h * 64:(h + 1) * 64], in_=p_conv, func=AF.Identity),
                inc="ac", nodrain=True)
        r.wait("sync", "ac")
        r.dma("sync", s1_scr.ap()[i * 64:(i + 1) * 64, :], ss1t[0:64, :],
              inc="st_sync")

    r.wait("tensor", "cast", CAST)
    t0c = 0
    for tgi, TG in enumerate(tg_sizes):
        ntile = TG // 128
        r.wait("sync", "gts", r.val("gts"))
        r.wait("sync", "pe")
        r.dma("sync", swdT64[0:64, 0:TG], wdT_scr.ap()[:, t0c:t0c + TG], inc="wl")
        r.wait("gpsimd", "wl", r.val("wl"))
        r.dma("gpsimd", bass.AP(swdTrow, 0, [[TG * 64, 1], [TG, 64], [1, TG]]),
              swdT64[0:64, 0:TG], inc="wl")
        r.wait("tensor", "wl", r.val("wl"))
        r.wait("sync", "pe")
        r.dma("sync", sxn_tg[:].rearrange("p (c j) -> p c j", c=8)[:, :, 0:TG],
              bass.AP(xn1fm_scr, t0c, [[NT, 128], [128 * NT, 8], [1, TG]]),
              inc="xtl")
        r.wait("tensor", "xtl", r.val("xtl"))
        first_y = True
        for e in range(E1 + 1):
            par = e % 2
            shared = (e == E1)
            if not shared:
                r.wait("sync", "pe")
                r.dma("sync", swg[par][:].rearrange("p (d j) -> p d j", d=8),
                      bass.AP(wgbf, e * D * F1, [[F1, 128], [128 * F1, 8], [1, F1]]),
                      inc="wl")
                r.dma("sync", swu[par][:].rearrange("p (d j) -> p d j", d=8),
                      bass.AP(wubf, e * D * F1, [[F1, 128], [128 * F1, 8], [1, F1]]),
                      inc="wl")
                r.dma("sync", swdn[par][:].rearrange("p (f j) -> p f j", f=4),
                      bass.AP(wdbf, e * F1 * D, [[D, 128], [128 * D, 4], [1, D]]),
                      inc="wl")
                wgt, wut, wdt = swg[par], swu[par], swdn[par]
                r.wait("tensor", "wl", r.val("wl"))
                r.wait("tensor", "ve")
                r.wait("tensor", "ac")
                r.op("tensor", lambda eng, e=e, TG=TG: eng.matmul(
                    pD[:, 0:TG], sones[0:1, :], swdTrow[0:1, e * TG:(e + 1) * TG],
                    start=True, stop=True), inc="pe")
                r.wait("vector", "pe")
                r.wait("vector", "gpv")
                r.op("vector", lambda eng, TG=TG: eng.tensor_copy(
                    out=sbc[:, 0:TG], in_=pD[:, 0:TG]), inc="ve", nodrain=True)
                r.wait("tensor", "ve")
            else:
                wgt, wut, wdt = swsg, swsu, swsd
            r.wait("scalar", "ve")
            r.wait("tensor", "gpv")
            for f in range(4):
                r.wait("tensor", "ac")
                for d in range(8):
                    r.op("tensor", lambda eng, f=f, d=d, wgt=wgt, t0c=t0c, TG=TG:
                         eng.matmul(
                             pD[:, 0:TG], wgt[:, d * F1 + f * 128: d * F1 + (f + 1) * 128],
                             sxn_tg[:, d * 384: d * 384 + TG],
                             start=(d == 0), stop=(d == 7)),
                         inc=("pe" if d == 7 else None))
                r.wait("scalar", "pe")
                r.op("scalar", lambda eng, f=f, TG=TG: eng.activation(
                    out=sgl[:, f * 384: f * 384 + TG], in_=pD[:, 0:TG],
                    func=AF.Silu), inc="ac", nodrain=True)
                r.wait("tensor", "ve")
                for d in range(8):
                    r.op("tensor", lambda eng, f=f, d=d, wut=wut, t0c=t0c, TG=TG:
                         eng.matmul(
                             pD[:, 512:512 + TG], wut[:, d * F1 + f * 128: d * F1 + (f + 1) * 128],
                             sxn_tg[:, d * 384: d * 384 + TG],
                             start=(d == 0), stop=(d == 7)),
                         inc=("pe" if d == 7 else None))
                r.wait("vector", "pe")
                r.wait("vector", "ac")
                r.op("vector", lambda eng, f=f, TG=TG: eng.tensor_tensor(
                    out=sh1[:, f * 384: f * 384 + TG], in0=pD[:, 512:512 + TG],
                    in1=sgl[:, f * 384: f * 384 + TG], op=ALU.mult), inc="ve",
                    nodrain=True)
                if not shared:
                    r.wait("gpsimd", "ve")
                    r.op("gpsimd", lambda eng, f=f, TG=TG: eng.tensor_tensor(
                        out=sh1[:, f * 384: f * 384 + TG],
                        in0=sh1[:, f * 384: f * 384 + TG],
                        in1=sbc[:, 0:TG], op=ALU.mult), inc="gpv")
            if not shared:
                r.wait("tensor", "gpv")
            else:
                r.wait("tensor", "ve")
            r.wait("tensor", "ac")
            pys = [pA, pB, pC]
            for ti in range(ntile):
                for f in range(4):
                    for hf in range(2):
                        r.op("tensor", lambda eng, ti=ti, f=f, hf=hf, wdt=wdt,
                             fy=first_y: eng.matmul(
                                 pys[ti][:, hf * 512:(hf + 1) * 512],
                                 sh1[:, f * 384 + ti * 128: f * 384 + (ti + 1) * 128],
                                 wdt[:, f * D + hf * 512: f * D + (hf + 1) * 512],
                                 start=(fy and f == 0), stop=(e == E1 and f == 3)),
                             inc=("pe" if (e == E1 and f == 3 and hf == 1) else None))
            first_y = False
        r.wait("scalar", "pe")
        r.wait("scalar", "ve")
        for ti in range(ntile):
            r.op("scalar", lambda eng, ti=ti: eng.activation(
                out=sg1[:, ti * D:(ti + 1) * D], in_=pys[ti][:], func=AF.Silu),
                inc="ac", nodrain=True)
        for ti in range(ntile):
            rows = slice(t0c + ti * 128, t0c + (ti + 1) * 128)
            r.wait("sync", "ve")
            r.dma("sync", sx1t[:], dx1.ap()[rows, :], inc="cl")
            r.dma("sync", ss1c[:], s1_scr.ap()[rows, :], inc="cl")
            CL = r.val("cl")
            r.wait("vector", "cl", CL)
            r.wait("vector", "ac")
            r.op("vector", lambda eng, ti=ti: eng.tensor_tensor(
                out=ss1c[:], in0=ss1c[:], in1=sg1[:, ti * D:(ti + 1) * D],
                op=ALU.mult))
            r.op("vector", lambda eng: eng.tensor_tensor(
                out=sx1t[:], in0=sx1t[:], in1=ss1c[:], op=ALU.add), inc="ve")
            r.wait("sync", "ve")
            r.dma("sync", dout.ap()[rows, :], sx1t[:], inc="st_sync")
        t0c += TG

    if debug:
        r.wait("gpsimd", "st_sync", r.val("st_sync"))
        r.dma("gpsimd", d_s1.ap(), s1_scr.ap(), inc="dbg")
        r.dma("gpsimd", d_wd.ap(), wdT_scr.ap(), inc="dbg")
        r.dma("gpsimd", d_wsm.ap(), w_scr.ap(), inc="dbg")
        r.wait("sync", "dbg", r.val("dbg"))
    r.wait("sync", "st_sync", r.val("st_sync"))
    if F is None:
        r.build_block()
        r.close()
        return nc
    F.end_phase()





def emit_scan_fused(F):
    nc, r = F.nc, F.r
    F.begin_phase("B_")
    C = 128
    NCH = NT // C
    NS2 = 16
    PCH = 2
    NG = NCH // PCH

    d_wT = F.scratch["wsig_tm"]
    d_kkT = F.scratch["kk_tm"]
    d_aT = F.scratch["a_tm"]
    d_k2T = F.scratch["k2_tm"]
    d_vT = F.scratch["v_tm"]
    d_rF = F.scratch["r_fm"]
    d_kkF = F.scratch["kk_fm"]
    d_aF = F.scratch["a_fm"]
    d_k2F = F.scratch["k2_fm"]
    d_o = F.scr("o", [NT, D], F32)
    rhat_scr = F.scr("rhat_scr", [NCH, 64, 2048], BF16)
    op_scr = F.scr("op_scr", [NCH, 128, 1024], BF16)
    iph_scr = F.scr("iph_scr", [NCH, 64, 1024], BF16)
    dhat_scr = F.scr("dhat_scr", [NCH, 64, 1024], BF16)
    lamc_scr = F.scr("lamc_scr", [NCH, 128, 8], F32)
    cc1_in = F.scr("cc1_in", [1024, 64], F32)
    cc1_out = F.scr("cc1_out", [2048, 64], F32)
    dident = F.ext_in("identD", [128, 128], BF16)
    dut_i = F.ext_in("utinclD", [128, 128], BF16)
    dut_s = F.ext_in("utstrictD", [128, 128], BF16)
    dmsl = F.ext_in("mslD", [128, 128], F32)
    dmsu = F.ext_in("msuD", [128, 128], F32)
    dmtu = F.ext_in("mtuD", [128, 128], F32)
    disodd = F.ext_in("is_odd", [64, 1], F32)

    p_cumA = F.pA[:, 0:512]
    p_cumB = F.pA[:, 512:1024]
    p_cum_all = F.pA
    p_a = F.pB[:, 0:128]
    p_b = F.pB[:, 512:640]
    p_sm = F.pC[:, 0:64]
    p_rh = F.pC[0:64, 512:640]
    p_h = F.pD[0:64, 0:64]
    p_o = F.pD[:, 512:576]
    p_st = F.pD[0:64, 0:64]

    sident = r.sb("sident", [128, 128], BF16)
    sut_i = r.sb("sut_i", [128, 128], BF16)
    sut_s = r.sb("sut_s", [128, 128], BF16)
    smsl = r.sb("smsl", [128, 128], F32)
    smsu = r.sb("smsu", [128, 128], F32)
    smtu = r.sb("smtu", [128, 128], F32)
    sisodd = r.sb("sisodd", [64, 1], F32)
    for dst, srct in ((sident, dident), (sut_i, dut_i), (sut_s, dut_s),
                      (smsl, dmsl), (smsu, dmsu), (smtu, dmtu), (sisodd, disodd)):
        r.dma("sync", dst[:], srct.ap(), inc="cld")
    CLD = r.val("cld")
    for e in ("tensor", "vector", "scalar"):
        r.wait(e, "cld", CLD)

    in_tm = {n: [r.sb(f"i_{n}{p}", [128, PCH * 1024], BF16) for p in range(2)]
             for n in ("wT", "kkT", "aT", "k2T", "vT")}
    in_fm = {n: [r.sb(f"i_{n}{p}", [128, 8 * PCH * 128], BF16) for p in range(2)]
             for n in ("rF", "kkF", "aF", "k2F")}
    slogL = r.sb("slogL", [128, 1024], F32)
    slogLp = r.sb("slogLp", [128, 1024], F32)
    sLamI_tm = r.sb("sLamI_tm", [128, 1024], BF16)
    sLamP_tm = r.sb("sLamP_tm", [128, 1024], BF16)
    sAt_tm = r.sb("sAt_tm", [128, 1024], BF16)
    sBt_tm = r.sb("sBt_tm", [128, 1024], BF16)
    sKt_tm = r.sb("sKt_tm", [128, 1024], BF16)
    slogL_fm = [r.sb(f"slogL_fm{j}", [128, 128], F32) for j in range(8)]
    sLam_fm = [r.sb(f"sLam_fm{j}", [128, 129], BF16) for j in range(8)]
    sLamI_fm = [r.sb(f"sLamI_fm{j}", [128, 128], BF16) for j in range(8)]
    sAt_fm = [r.sb(f"sAt_fm{j}", [128, 128], BF16) for j in range(8)]
    sBt_fm = [r.sb(f"sBt_fm{j}", [128, 128], BF16) for j in range(8)]
    sKt_fm = [r.sb(f"sKt_fm{j}", [128, 128], BF16) for j in range(8)]
    sRt_fm = [r.sb(f"sRt_fm{j}", [128, 128], BF16) for j in range(8)]
    sM = r.sb("sM", [128, 128], BF16)
    sMT = r.sb("sMT", [128, 128], BF16)
    sNT = r.sb("sNT", [128, 128], BF16)
    sTLRBt = r.sb("sTLRBt", [128, 128], BF16)
    sTLRKt = r.sb("sTLRKt", [128, 128], BF16)
    sMk = [r.sb(f"sMk{i}", [128, 128], BF16) for i in range(2)]
    sMkT = [r.sb(f"sMkT{i}", [128, 128], BF16) for i in range(2)]
    sWT = [r.sb(f"sWT{i}", [128, 128], BF16) for i in range(2)]
    sNV = r.sb("sNV", [128, 64], BF16)
    sUp = r.sb("sUp", [128, 64], BF16)
    sWAs = r.sb("sWAs", [128, 64], BF16)
    srhat = r.sb("srhat", [64, 2048], BF16)
    sop = r.sb("sop", [128, 1024], BF16)
    siph = r.sb("siph", [64, 1024], BF16)
    sdhat = r.sb("sdhat", [64, 1024], BF16)
    slamc = r.sb("slamc", [128, 8], F32)
    sS = [r.sb(f"sS{s}", [64, 64], BF16) for s in range(NS2)]
    sdsf = r.sb("sdsf", [64, 1024], F32)
    piph = r.sb("piph", [64, 1024], BF16)
    pdhat = r.sb("pdhat", [64, 1024], BF16)
    plamc = r.sb("plamc", [128, 8], F32)
    prhat = r.sb("prhat", [64, 2048], BF16)
    pop = r.sb("pop", [128, 1024], BF16)
    so_t = r.sb("so_t", [128, 1024], F32)

    def load_group(g, buf):
        c0 = g * PCH
        for cc in range(PCH):
            ch = c0 + cc
            for n, dt_ in (("wT", d_wT), ("kkT", d_kkT), ("aT", d_aT),
                           ("k2T", d_k2T), ("vT", d_vT)):
                r.dma("sync", in_tm[n][buf][:, cc * 1024:(cc + 1) * 1024],
                      dt_.ap()[ch * 128:(ch + 1) * 128, :], inc="ld")
        for n, df in (("rF", d_rF), ("kkF", d_kkF), ("aF", d_aF), ("k2F", d_k2F)):
            for j in range(8):
                r.dma("sync", in_fm[n][buf][:, j * PCH * 128:(j + 1) * PCH * 128],
                      df.ap()[j * 128:(j + 1) * 128, c0 * 128:(c0 + PCH) * 128],
                      inc="ld")
    ld_per_group = (5 * PCH + 32) * 16

    grp_done = {}
    load_group(0, 0)

    for g in range(NG):
        buf = g % 2
        if g + 1 < NG:
            if g >= 1:
                for semn in ("pe", "ve"):
                    r.wait("sync", semn, grp_done[g - 1][semn])
            load_group(g + 1, 1 - buf)
        for e in ("tensor", "vector", "scalar"):
            r.wait(e, "ld", (g + 1) * ld_per_group)

        for cc in range(PCH):
            ch = g * PCH + cc
            wT_all = in_tm["wT"][buf][:, cc * 1024:(cc + 1) * 1024]
            kkT_all = in_tm["kkT"][buf][:, cc * 1024:(cc + 1) * 1024]
            aT_all = in_tm["aT"][buf][:, cc * 1024:(cc + 1) * 1024]
            k2T_all = in_tm["k2T"][buf][:, cc * 1024:(cc + 1) * 1024]
            vT_all = in_tm["vT"][buf][:, cc * 1024:(cc + 1) * 1024]

            r.wait("vector", "sws", r.val("sws"))
            r.wait("scalar", "sws", r.val("sws"))
            r.wait("tensor", "ve")
            for hf in range(2):
                r.op("tensor", lambda eng, hf=hf, w=wT_all: eng.matmul(
                    F.pA[:, hf * 512:(hf + 1) * 512], sut_i[:],
                    w[:, hf * 512:(hf + 1) * 512], start=True, stop=True),
                    inc=("pe" if hf == 1 else None))
            r.wait("vector", "pe")
            r.wait("vector", "ac")
            r.op("vector", lambda eng: eng.tensor_scalar(
                out=slogL[:], in0=p_cum_all[:], scalar1=DECAY, scalar2=None,
                op0=ALU.mult), inc="ve", nodrain=True)
            r.wait("tensor", "ve")
            for hf in range(2):
                r.op("tensor", lambda eng, hf=hf, w=wT_all: eng.matmul(
                    F.pA[:, hf * 512:(hf + 1) * 512], sut_s[:],
                    w[:, hf * 512:(hf + 1) * 512], start=True, stop=True),
                    inc=("pe" if hf == 1 else None))
            r.wait("vector", "pe")
            r.op("vector", lambda eng: eng.tensor_scalar(
                out=slogLp[:], in0=p_cum_all[:], scalar1=DECAY, scalar2=None,
                op0=ALU.mult), inc="ve", nodrain=True)
            r.wait("scalar", "ve")
            r.op("scalar", lambda eng: eng.activation(
                out=sLamI_tm[:], in_=slogL[:], func=AF.Exp, scale=-1.0), inc="ac",
                nodrain=True)
            r.op("scalar", lambda eng: eng.activation(
                out=sLamP_tm[:], in_=slogLp[:], func=AF.Exp), inc="ac", nodrain=True)
            r.wait("vector", "ac")
            r.op("vector", lambda eng, kk=kkT_all: eng.scalar_tensor_tensor(
                out=sAt_tm[:], in0=kk, scalar=-1.0, in1=sLamP_tm[:],
                op0=ALU.mult, op1=ALU.mult), inc="ve", nodrain=True)
            r.op("vector", lambda eng, kk=kkT_all, aa=aT_all: eng.tensor_tensor(
                out=sBt_tm[:], in0=kk, in1=aa, op=ALU.mult), inc="ve")
            r.op("vector", lambda eng: eng.tensor_tensor(
                out=sBt_tm[:], in0=sBt_tm[:], in1=sLamI_tm[:], op=ALU.mult),
                inc="ve")
            r.op("vector", lambda eng, k2=k2T_all: eng.tensor_tensor(
                out=sKt_tm[:], in0=k2, in1=sLamI_tm[:], op=ALU.mult), inc="ve",
                nodrain=True)

            for j in range(8):
                wT_j = wT_all[:, j * 128:(j + 1) * 128]
                kkF_j = in_fm["kkF"][buf][:, (j * PCH + cc) * 128:(j * PCH + cc + 1) * 128]
                aF_j = in_fm["aF"][buf][:, (j * PCH + cc) * 128:(j * PCH + cc + 1) * 128]
                k2F_j = in_fm["k2F"][buf][:, (j * PCH + cc) * 128:(j * PCH + cc + 1) * 128]
                rF_j = in_fm["rF"][buf][:, (j * PCH + cc) * 128:(j * PCH + cc + 1) * 128]
                r.wait("tensor", "ve")
                r.op("tensor", lambda eng, w=wT_j: eng.matmul(
                    p_a, w, sut_i[:], start=True, stop=True), inc="pe")
                r.wait("vector", "pe")
                r.wait("vector", "ac")
                r.op("vector", lambda eng, j=j: eng.tensor_scalar(
                    out=slogL_fm[j][:], in0=p_a, scalar1=DECAY, scalar2=None,
                    op0=ALU.mult), inc="ve")
                r.wait("scalar", "ve")
                r.op("scalar", lambda eng, j=j: eng.activation(
                    out=sLam_fm[j][:, 1:129], in_=slogL_fm[j][:], func=AF.Exp),
                    inc="ac", nodrain=True)
                r.op("scalar", lambda eng, j=j: eng.activation(
                    out=sLamI_fm[j][:], in_=slogL_fm[j][:], func=AF.Exp,
                    scale=-1.0), inc="ac", nodrain=True)
                r.op("scalar", lambda eng, j=j: eng.activation(
                    out=slamc[:, j:j + 1], in_=slogL_fm[j][:, 127:128],
                    func=AF.Exp), inc="ac", nodrain=True)
                r.wait("vector", "ac")
                r.op("vector", lambda eng, j=j, kk=kkF_j: eng.scalar_tensor_tensor(
                    out=sAt_fm[j][:], in0=kk, scalar=-1.0,
                    in1=sLam_fm[j][:, 0:128], op0=ALU.mult, op1=ALU.mult),
                    inc="ve", nodrain=True)
                r.op("vector", lambda eng, j=j, kk=kkF_j, aa=aF_j: eng.tensor_tensor(
                    out=sBt_fm[j][:], in0=kk, in1=aa, op=ALU.mult), inc="ve")
                r.op("vector", lambda eng, j=j: eng.tensor_tensor(
                    out=sBt_fm[j][:], in0=sBt_fm[j][:], in1=sLamI_fm[j][:],
                    op=ALU.mult), inc="ve")
                r.op("vector", lambda eng, j=j, k2=k2F_j: eng.tensor_tensor(
                    out=sKt_fm[j][:], in0=k2, in1=sLamI_fm[j][:], op=ALU.mult),
                    inc="ve", nodrain=True)
                r.op("vector", lambda eng, j=j, rr=rF_j: eng.tensor_tensor(
                    out=sRt_fm[j][:], in0=rr, in1=sLam_fm[j][:, 1:129],
                    op=ALU.mult), inc="ve", nodrain=True)


            for s in range(NS2):
                j = s // 2
                lo = (s % 2) * 64
                A_f = sAt_fm[j][lo:lo + 64, :]
                B_f = sBt_fm[j][lo:lo + 64, :]
                K_f = sKt_fm[j][lo:lo + 64, :]
                R_f = sRt_fm[j][lo:lo + 64, :]
                A_t = sAt_tm[:, s * 64:(s + 1) * 64]
                B_t = sBt_tm[:, s * 64:(s + 1) * 64]
                K_t = sKt_tm[:, s * 64:(s + 1) * 64]
                V_t = vT_all[:, s * 64:(s + 1) * 64]

                def mm(pp, lhsT, rhs, start=True, stop=True, inc="pe"):
                    r.op("tensor", lambda eng, pp=pp, l=lhsT, rr=rhs, st=start,
                         sp=stop: eng.matmul(pp, l, rr, start=st, stop=sp), inc=inc)

                def evac_v(dst, src_p, mask=None, add=None, inc="ve"):
                    r.wait("vector", "pe")
                    if mask is not None:
                        r.op("vector", lambda eng, d=dst, sp=src_p, m=mask:
                             eng.tensor_tensor(out=d, in0=sp, in1=m, op=ALU.mult),
                             inc=inc, nodrain=True)
                    elif add is not None:
                        r.op("vector", lambda eng, d=dst, sp=src_p, a=add:
                             eng.tensor_tensor(out=d, in0=sp, in1=a, op=ALU.add),
                             inc=inc, nodrain=True)
                    else:
                        r.op("vector", lambda eng, d=dst, sp=src_p:
                             eng.tensor_copy(out=d, in_=sp), inc=inc, nodrain=True)

                def evac_s(dst, src_p, inc="ac"):
                    r.wait("scalar", "pe")
                    r.op("scalar", lambda eng, d=dst, sp=src_p: eng.activation(
                        out=d, in_=sp, func=AF.Identity), inc=inc, nodrain=True)

                r.wait("tensor", "ve")
                r.wait("tensor", "ac")
                mm(p_a, A_f, B_f)
                evac_v(sM[:], p_a, mask=smsl[:])
                mm(p_b, B_f, A_f)
                evac_v(sMT[:], p_b, mask=smsu[:])
                r.wait("tensor", "ve")
                mm(p_a, K_f, A_f)
                evac_v(sNT[:], p_a, mask=smsu[:])
                mm(p_b, B_f, R_f)
                evac_v(sTLRBt[:], p_b, mask=smtu[:])
                r.wait("tensor", "ve")
                mm(p_a, K_f, R_f)
                evac_v(sTLRKt[:], p_a, mask=smtu[:])
                r.wait("vector", "ve")
                r.op("vector", lambda eng: eng.tensor_tensor(
                    out=sWT[0][:], in0=sMT[:], in1=sident[:], op=ALU.add), inc="ve")
                r.op("vector", lambda eng: eng.tensor_copy(out=sMk[0][:], in_=sM[:]),
                     inc="ve", nodrain=True)
                r.op("vector", lambda eng: eng.tensor_copy(out=sMkT[0][:], in_=sMT[:]),
                     inc="ve", nodrain=True)
                wt_cur = 0
                mk_cur = 0
                for lvl in range(6):
                    r.wait("tensor", "ve")
                    r.wait("tensor", "ac")
                    mm(p_a, sMkT[mk_cur][:], sMk[mk_cur][:])
                    evac_s(sMk[1 - mk_cur][:], p_a)
                    if lvl < 5:
                        mm(p_b, sMk[mk_cur][:], sMkT[mk_cur][:])
                        evac_s(sMkT[1 - mk_cur][:], p_b)
                    mk_cur = 1 - mk_cur
                    r.wait("tensor", "ac")
                    mm(p_a, sMk[mk_cur][:], sWT[wt_cur][:])
                    evac_v(sWT[1 - wt_cur][:], p_a, add=sWT[wt_cur][:])
                    wt_cur = 1 - wt_cur
                WT = sWT[wt_cur]
                r.wait("tensor", "ve")
                mm(p_sm, sNT[:], V_t)
                evac_v(sNV[:, 0:64], p_sm)
                r.wait("tensor", "ve")
                mm(p_sm, WT[:], sNV[:, 0:64])
                evac_v(sUp[:, 0:64], p_sm)
                mm(p_o, WT[:], A_t)
                evac_v(sWAs[:, 0:64], p_o)
                r.wait("tensor", "ve")
                r.wait("tensor", "sst")
                mm(p_rh, sWAs[:, 0:64], sTLRBt[:])
                evac_v(srhat[0:64, s * 128:(s + 1) * 128], p_rh, add=R_f,
                       inc="sst")
                mm(p_h, sWAs[:, 0:64], B_t)
                evac_v(siph[0:64, s * 64:(s + 1) * 64], p_h,
                       add=sident[0:64, 0:64], inc="sst")
                r.wait("tensor", "ve")
                r.wait("tensor", "sst")
                mm(p_h, B_t, sUp[:, 0:64], start=True, stop=False, inc=None)
                mm(p_h, K_t, V_t, start=False, stop=True)
                evac_v(sdhat[0:64, s * 64:(s + 1) * 64], p_h, inc="sst")
                mm(p_o, sTLRBt[:], sUp[:, 0:64], start=True, stop=False, inc=None)
                mm(p_o, sTLRKt[:], V_t, start=False, stop=True)
                evac_v(sop[:, s * 64:(s + 1) * 64], p_o, inc="sst")

            r.wait("sync", "sst", r.val("sst"))
            r.wait("sync", "ac")
            r.dma("sync", rhat_scr.ap()[ch], srhat[:], inc="sws")
            r.dma("sync", op_scr.ap()[ch], sop[:], inc="sws")
            r.dma("sync", iph_scr.ap()[ch], siph[:], inc="sws")
            r.dma("sync", dhat_scr.ap()[ch], sdhat[:], inc="sws")
            r.dma("sync", lamc_scr.ap()[ch], slamc[:], inc="sws")

        grp_done[g] = {"pe": r.val("pe"), "ve": r.val("ve")}

    for s in range(NS2):
        r.op("gpsimd", lambda eng, s=s: eng.memset(sS[s][:], 0.0), inc="gp")
    r.wait("vector", "gp", r.val("gp"))
    SWS = r.val("sws")
    r.wait("sync", "sws", SWS)
    r.wait("tensor", "sst", r.val("sst"))
    for ch in range(NCH):
        r.wait("sync", "ve")
        r.dma("sync", piph[:], iph_scr.ap()[ch], inc="p1l")
        r.dma("sync", pdhat[:], dhat_scr.ap()[ch], inc="p1l")
        r.dma("sync", plamc[:], lamc_scr.ap()[ch], inc="p1l")
        P1L = r.val("p1l")
        r.wait("tensor", "p1l", P1L)
        r.wait("vector", "p1l", P1L)
        for s in range(NS2):
            j = s // 2
            lo = (s % 2) * 64
            r.wait("tensor", "ve")
            r.op("tensor", lambda eng, s=s: eng.matmul(
                p_st, siph_sl(piph, s), sS[s][:], start=True, stop=True), inc="pe")
            r.wait("vector", "pe")
            r.op("vector", lambda eng, s=s: eng.tensor_tensor(
                out=sdsf[0:64, s * 64:(s + 1) * 64], in0=p_st,
                in1=pdhat[0:64, s * 64:(s + 1) * 64], op=ALU.add), inc="ve")
            r.op("vector", lambda eng, s=s, j=j, lo=lo: eng.tensor_scalar(
                out=sS[s][:], in0=sdsf[0:64, s * 64:(s + 1) * 64],
                scalar1=plamc[lo:lo + 64, j:j + 1], scalar2=None,
                op0=ALU.mult), inc="ve")

    for s in range(NS2):
        r.op("vector", lambda eng, s=s: eng.tensor_copy(
            out=sdsf[0:64, s * 64:(s + 1) * 64], in_=sS[s][:]), inc="ve",
            nodrain=True)
    r.wait("sync", "ve")
    r.dma("sync", bass.AP(cc1_in, 0, [[64, 64], [4096, 16], [1, 64]]),
          sdsf[0:64, :].rearrange("p (s v) -> p s v", s=16), inc="ccw")
    r.wait("gpsimd", "ccw", r.val("ccw"))
    r.op("gpsimd", lambda eng: eng.collective_compute(
        "AllGather", ALU.bypass,
        replica_groups=[[0, 1], [2, 3], [4, 5], [6, 7]],
        ins=[cc1_in.ap()], outs=[cc1_out.ap()]), inc="cc")
    r.op("gpsimd", lambda eng: eng.drain())
    r.wait("sync", "cc", r.val("cc"))
    r.dma("sync", sdsf[0:64, :].rearrange("p (s v) -> p s v", s=16),
          bass.AP(cc1_out, 0, [[64, 64], [4096, 16], [1, 64]]), inc="ccr")
    CCR = r.val("ccr")
    r.wait("vector", "ccr", CCR)
    r.op("vector", lambda eng: eng.tensor_scalar(
        out=sdsf[0:64, :], in0=sdsf[0:64, :], scalar1=sisodd[:],
        scalar2=None, op0=ALU.mult), inc="ve")
    for s in range(NS2):
        r.op("vector", lambda eng, s=s: eng.tensor_copy(
            out=sS[s][:], in_=sdsf[0:64, s * 64:(s + 1) * 64]), inc="ve",
            nodrain=True)
    r.op("vector", lambda eng: eng.drain())

    for ch in range(NCH):
        r.wait("sync", "ve")
        r.wait("sync", "oev", r.val("oev"))
        r.wait("sync", "so", r.val("so"))
        r.wait("vector", "so", r.val("so"))
        r.dma("sync", prhat[:], rhat_scr.ap()[ch], inc="p2l")
        r.dma("sync", pop[:], op_scr.ap()[ch], inc="p2l")
        r.dma("sync", piph[:], iph_scr.ap()[ch], inc="p2l")
        r.dma("sync", pdhat[:], dhat_scr.ap()[ch], inc="p2l")
        r.dma("sync", plamc[:], lamc_scr.ap()[ch], inc="p2l")
        P2L = r.val("p2l")
        r.wait("tensor", "p2l", P2L)
        r.wait("vector", "p2l", P2L)
        for s in range(NS2):
            j = s // 2
            lo = (s % 2) * 64
            r.wait("tensor", "ve")
            r.op("tensor", lambda eng, s=s: eng.matmul(
                p_o, prhat[0:64, s * 128:(s + 1) * 128], sS[s][:],
                start=True, stop=True), inc="pe")
            r.wait("vector", "pe")
            r.op("vector", lambda eng, s=s: eng.tensor_tensor(
                out=so_t[:, s * 64:(s + 1) * 64], in0=p_o,
                in1=pop[:, s * 64:(s + 1) * 64], op=ALU.add), inc="oev",
                nodrain=True)
            r.op("tensor", lambda eng, s=s: eng.matmul(
                p_st, siph_sl(piph, s), sS[s][:], start=True, stop=True), inc="pe")
            r.wait("vector", "pe")
            r.op("vector", lambda eng, s=s: eng.tensor_tensor(
                out=sdsf[0:64, s * 64:(s + 1) * 64], in0=p_st,
                in1=pdhat[0:64, s * 64:(s + 1) * 64], op=ALU.add), inc="ve")
            r.op("vector", lambda eng, s=s, j=j, lo=lo: eng.tensor_scalar(
                out=sS[s][:], in0=sdsf[0:64, s * 64:(s + 1) * 64],
                scalar1=plamc[lo:lo + 64, j:j + 1], scalar2=None,
                op0=ALU.mult), inc="ve")
        r.wait("sync", "oev", r.val("oev"))
        r.dma("sync", d_o.ap()[ch * 128:(ch + 1) * 128, :], so_t[:], inc="so")

    r.wait("sync", "so", r.val("so"))
    F.end_phase()


def siph_sl(piph, s):
    return piph[0:64, s * 64:(s + 1) * 64]


def build_fused(debug_cc=False):
    F = Fctx()
    build_l0pre(F=F)
    emit_scan_fused(F)
    build_l0post(F=F)
    r = F.r
    F.begin_phase("X_")
    cc2_in = F.scr("cc2_in", [30, D], BF16)
    cc2_out = F.scr("cc2_out", [60, D], BF16)
    xn1h = F.scratch["xn1h_scr"]
    disodd = F.ext_in("is_odd", [64, 1], F32)
    shalo = r.sb("shalo", [30, D], BF16)
    sisodd2 = r.sb("sisodd2", [64, 1], F32)
    r.dma("sync", sisodd2[:], disodd.ap(), inc="hl")
    r.dma("gpsimd", cc2_in.ap(), xn1h.ap()[NT:NT + 30, :], inc="hw")
    r.wait("gpsimd", "hw", r.val("hw"))
    r.op("gpsimd", lambda eng: eng.collective_compute(
        "AllGather", ALU.bypass,
        replica_groups=[[0, 1], [2, 3], [4, 5], [6, 7]],
        ins=[cc2_in.ap()], outs=[cc2_out.ap()]), inc="hcc")
    r.op("gpsimd", lambda eng: eng.drain())
    r.wait("sync", "hcc", r.val("hcc"))
    r.dma("sync", shalo[:], cc2_out.ap()[0:30, :], inc="hl")
    HL = r.val("hl")
    r.wait("vector", "hl", HL)
    r.op("vector", lambda eng: eng.tensor_scalar(
        out=shalo[:], in0=shalo[:], scalar1=sisodd2[0:30, 0:1], scalar2=None,
        op0=ALU.mult), inc="hv")
    r.wait("sync", "hv", r.val("hv"))
    r.dma("sync", xn1h.ap()[0:30, :], shalo[:], inc="hs")
    r.wait("sync", "hs", r.val("hs"))
    F.end_phase()
    build_l1(F=F)
    if debug_cc:
        r2 = F.r
        d_cc1in = F.nc.dram_tensor("d_cc1in", [1024, 64], F32, kind="ExternalOutput")
        d_cc1out = F.nc.dram_tensor("d_cc1out", [2048, 64], F32, kind="ExternalOutput")
        r2.barrier()
        d_oo = F.nc.dram_tensor("d_oo", [NT, D], F32, kind="ExternalOutput")
        d_x1 = F.nc.dram_tensor("d_x1", [NT, D], F32, kind="ExternalOutput")
        d_s1 = F.nc.dram_tensor("d_s1d", [NT, D], F32, kind="ExternalOutput")
        d_y = F.nc.dram_tensor("d_yd", [NT, D], F32, kind="ExternalOutput")
        r2.dma("gpsimd", d_oo.ap(), F.scratch["o"].ap(), inc="Zdbg")
        r2.dma("gpsimd", d_x1.ap(), F.scratch["x1"].ap(), inc="Zdbg")
        r2.dma("gpsimd", d_s1.ap(), F.scratch["s1_scr"].ap(), inc="Zdbg")
        r2.dma("gpsimd", d_y.ap(), F.scratch["y_scr"].ap(), inc="Zdbg")
        d_w = F.nc.dram_tensor("d_wscr", [NT, 496], F32, kind="ExternalOutput")
        r2.dma("gpsimd", d_w.ap(), F.scratch["w_scr"].ap(), inc="Zdbg")
        d_xn1 = F.nc.dram_tensor("d_xn1h", [NT + 30, D], BF16, kind="ExternalOutput")
        r2.dma("gpsimd", d_xn1.ap(), F.scratch["xn1h_scr"].ap(), inc="Zdbg")
        r2.dma("gpsimd", d_cc1in.ap(), F.scratch["cc1_in"].ap(), inc="Zdbg")
        r2.dma("gpsimd", d_cc1out.ap(), F.scratch["cc1_out"].ap(), inc="Zdbg")
        r2.wait("sync", "Zdbg", r2.val("Zdbg"))
    F.r.barrier()
    F.r.build_block()
    F.r.close()
    return F


def kernel(**inputs):
    C = _CONSTS
    cores = list(range(NCORE))
    x = inputs["L0_attn_hidden_states"]
    import os
    F = build_fused(debug_cc=bool(os.environ.get('DBGCC')))
    base = l0pre_inmaps(inputs)
    bmask = _l1_consts()
    gb_t = np.ascontiguousarray(np.broadcast_to(
        inputs["L1_ffn_gate_bias"].reshape(1, 64), (128, 64))).astype(np.float32)
    rk_t = np.ascontiguousarray(np.broadcast_to(
        inputs["L0_attn_r_k"].reshape(1, D), (128, D))).astype(np.float32)
    gnw_t = np.ascontiguousarray(np.broadcast_to(
        inputs["L0_attn_gn_weight"].reshape(1, D), (128, D))).astype(np.float32)
    gnb_t = np.ascontiguousarray(np.broadcast_to(
        inputs["L0_attn_gn_bias"].reshape(1, D), (128, D))).astype(np.float32)
    maps = []
    for c in cores:
        m = dict(base[c])
        m.update(
            utinclD=C["ut_incl"], utstrictD=C["ut_strict"],
            mslD=C["m_sl"], msuD=C["m_su"], mtuD=C["m_tu"],
            is_odd=np.full((64, 1), float(c % 2), np.float32),
            rk_tile=rk_t, gnw_tile=gnw_t, gnb_tile=gnb_t,
            Wo=inputs["L0_attn_Wo"], router_W=inputs["L0_ffn_router_W"],
            router_b=inputs["L0_ffn_router_b"].reshape(1, 8),
            W1=inputs["L0_ffn_W1"], b1=inputs["L0_ffn_b1"],
            W2=inputs["L0_ffn_W2"], b2=inputs["L0_ffn_b2"],
            w_lin=inputs["L1_attn_w_lin"], gate_w=inputs["L1_ffn_gate_w"],
            gbias_tile=gb_t,
            We_gate=inputs["L1_ffn_We_gate"], We_up=inputs["L1_ffn_We_up"],
            We_down=inputs["L1_ffn_We_down"],
            Ws_gate=inputs["L1_ffn_Ws_gate"], Ws_up=inputs["L1_ffn_Ws_up"],
            Ws_down=inputs["L1_ffn_Ws_down"],
            bmask=bmask,
        )
        maps.append(m)
    res = run_bass_kernel_spmd(F.nc, maps, core_ids=cores)
    T = 8192
    out = np.empty((4, T, D), np.float32)
    for c in cores:
        b, half = c // 2, c % 2
        out[b, half * NT:(half + 1) * NT] = res.results[c]["out"]
    return out



_CONSTS = host_consts()


def l0pre_inmaps(inputs):
    x = inputs["L0_attn_hidden_states"]
    B, T, _ = x.shape
    maps = []
    for c in range(NCORE):
        b, half = c // 2, c % 2
        t0 = half * NT
        if t0 == 0:
            xh = np.concatenate([np.zeros((1, D), np.float32),
                                 x[b, 0:NT].astype(np.float32)], axis=0)
        else:
            xh = np.ascontiguousarray(x[b, t0 - 1: t0 + NT])
        m = dict(
            xh=xh,
            Wr=inputs["L0_attn_Wr"], Wk=inputs["L0_attn_Wk"], Wv=inputs["L0_attn_Wv"],
            wA=inputs["L0_attn_wA"], wB=inputs["L0_attn_wB"],
            aA=inputs["L0_attn_aA"], aB=inputs["L0_attn_aB"],
            gA=inputs["L0_attn_gA"], gB=inputs["L0_attn_gB"],
            w_bias=inputs["L0_attn_w_bias"].reshape(1, D),
            a_bias=inputs["L0_attn_a_bias"].reshape(1, D),
            k_k=inputs["L0_attn_k_k"].reshape(1, D),
            k_a=inputs["L0_attn_k_a"].reshape(1, D),
            mixes=np.stack([inputs["L0_attn_mix_" + n].reshape(D) for n in
                            ("r", "w", "k", "v", "a", "g")]),
            identD=_CONSTS["ident"], blockonesD=_CONSTS["blockones"],
            blockpat2D=_CONSTS["blockpat2"],
        )
        maps.append(m)
    return maps


def _l1_consts():
    bmask = np.zeros((64, 16, 94), np.float32)
    for t in range(64):
        bmask[t, :, t:min(94, t + 31)] = 1.0
    return bmask.reshape(64, 16 * 94).astype(BF)


def kernel_multi(**inputs):
    C = _CONSTS
    cores = list(range(NCORE))
    x = inputs["L0_attn_hidden_states"]

    nc1 = build_l0pre()
    res1 = run_bass_kernel_spmd(nc1, l0pre_inmaps(inputs), core_ids=cores)
    r1 = res1.results

    T = 8192
    full_tm = {n: [np.concatenate([r1[2 * b][n], r1[2 * b + 1][n]], 0)
                   for b in range(4)]
               for n in ("wsig_tm", "kk_tm", "a_tm", "k2_tm", "v_tm")}
    full_fm = {n: [np.concatenate([r1[2 * b][n], r1[2 * b + 1][n]], 1)
                   for b in range(4)]
               for n in ("r_fm", "kk_fm", "a_fm", "k2_fm")}
    maps2 = []
    for c in cores:
        m = {}
        sts = [(g // 16, g % 16) for g in range(8 * c, 8 * c + 8)]
        for key, src_n in (("rF", "r_fm"), ("kkF", "kk_fm"), ("aF", "a_fm"),
                           ("k2F", "k2_fm")):
            m[key] = np.stack([full_fm[src_n][b][h * 64:(h + 1) * 64, :]
                               for (b, h) in sts])
        for key, src_n in (("wT", "wsig_tm"), ("kkT", "kk_tm"), ("aT", "a_tm"),
                           ("k2T", "k2_tm"), ("vT", "v_tm")):
            m[key] = np.stack([full_tm[src_n][b][:, h * 64:(h + 1) * 64]
                               for (b, h) in sts])
        m.update(identD=C["ident"], utinclD=C["ut_incl"], utstrictD=C["ut_strict"],
                 mslD=C["m_sl"], msuD=C["m_su"], mtuD=C["m_tu"])
        maps2.append(m)
    nc2 = build_scan()
    res2 = run_bass_kernel_spmd(nc2, maps2, core_ids=cores)
    r2 = res2.results

    rk_t = np.ascontiguousarray(np.broadcast_to(
        inputs["L0_attn_r_k"].reshape(1, D), (128, D))).astype(np.float32)
    gnw_t = np.ascontiguousarray(np.broadcast_to(
        inputs["L0_attn_gn_weight"].reshape(1, D), (128, D))).astype(np.float32)
    gnb_t = np.ascontiguousarray(np.broadcast_to(
        inputs["L0_attn_gn_bias"].reshape(1, D), (128, D))).astype(np.float32)
    maps3 = []
    for c in cores:
        b, half = c // 2, c % 2
        t0 = half * NT
        o_core = np.empty((NT, D), np.float32)
        for h in range(H):
            g = b * 16 + h
            o_core[:, h * 64:(h + 1) * 64] = \
                r2[g // 8]["o"][g % 8][t0:t0 + NT, :]
        m = dict(
            x=np.ascontiguousarray(x[b, t0:t0 + NT]).astype(np.float32),
            o=o_core,
            r_tm=r1[c]["r_tm"], k2_tm=r1[c]["k2_tm"], v_tm=r1[c]["v_tm"],
            g_tm=r1[c]["g_tm"], xn_fm=r1[c]["xn_fm"],
            rk_tile=rk_t, gnw_tile=gnw_t, gnb_tile=gnb_t,
            Wo=inputs["L0_attn_Wo"], router_W=inputs["L0_ffn_router_W"],
            router_b=inputs["L0_ffn_router_b"].reshape(1, 8),
            W1=inputs["L0_ffn_W1"], b1=inputs["L0_ffn_b1"],
            W2=inputs["L0_ffn_W2"], b2=inputs["L0_ffn_b2"],
            identD=C["ident"],
        )
        maps3.append(m)
    nc3 = build_l0post()
    res3 = run_bass_kernel_spmd(nc3, maps3, core_ids=cores)
    r3 = res3.results

    bmask = _l1_consts()
    gb_t = np.ascontiguousarray(np.broadcast_to(
        inputs["L1_ffn_gate_bias"].reshape(1, 64), (128, 64))).astype(np.float32)
    maps4 = []
    for c in cores:
        b, half = c // 2, c % 2
        if half == 0:
            xn1h = np.concatenate([np.zeros((30, D), BF), r3[c]["xn1_tm"]], 0)
        else:
            xn1h = np.concatenate([r3[c - 1]["xn1_tm"][-30:, :], r3[c]["xn1_tm"]], 0)
        m = dict(
            x1=r3[c]["x1"], xn1h=np.ascontiguousarray(xn1h),
            w_lin=inputs["L1_attn_w_lin"], gate_w=inputs["L1_ffn_gate_w"],
            gbias_tile=gb_t,
            We_gate=inputs["L1_ffn_We_gate"], We_up=inputs["L1_ffn_We_up"],
            We_down=inputs["L1_ffn_We_down"],
            Ws_gate=inputs["L1_ffn_Ws_gate"], Ws_up=inputs["L1_ffn_Ws_up"],
            Ws_down=inputs["L1_ffn_Ws_down"],
            identD=C["ident"], bmask=bmask,
        )
        maps4.append(m)
    nc4 = build_l1()
    res4 = run_bass_kernel_spmd(nc4, maps4, core_ids=cores)

    out = np.empty((4, T, D), np.float32)
    for c in cores:
        b, half = c // 2, c % 2
        out[b, half * NT:(half + 1) * NT] = res4.results[c]["out"]
    return out
